# version 21
# baseline (speedup 1.0000x reference)
import contextlib
import os
import numpy as np
import ml_dtypes
import concourse.bass as bass
import concourse.mybir as mybir
from concourse.bass_utils import run_bass_kernel_spmd

F32 = mybir.dt.float32
BF16 = mybir.dt.bfloat16
AF = mybir.ActivationFunctionType
ALU = mybir.AluOpType
AX = mybir.AxisListType
BF = ml_dtypes.bfloat16

T, D, NH, HD = 4096, 1024, 16, 64
NDC = D // 128
BLK = 512
NBLK = T // BLK
BT = BLK // 128

N_SUB = int(os.environ.get("MK_NSUB", "4"))



def _chtab(v):
    return np.ascontiguousarray(np.asarray(v, np.float32).reshape(NDC, 128).T)


def _wT(w):
    return np.ascontiguousarray(np.asarray(w).T)


def prep_host(inputs):
    g = {k: np.asarray(v) for k, v in inputs.items()}
    sh = {}
    sh["aWq"] = _wT(g["L0_attn_a_Wq"])
    sh["aWk"] = _wT(g["L0_attn_a_Wk"])
    sh["aWv"] = _wT(g["L0_attn_a_Wv"])
    sh["aWo"] = _wT(g["L0_attn_a_Wo"])
    w1 = g["L0_ffn_w1"]
    sh["fw1a"] = np.ascontiguousarray(w1[:D])
    sh["fw1b"] = np.ascontiguousarray(w1[D:])
    sh["fb1"] = np.ascontiguousarray(g["L0_ffn_b1"].reshape(4, 128).T)
    sh["fw2"] = g["L0_ffn_w2"]
    sh["fb2c"] = np.ascontiguousarray(np.tile(g["L0_ffn_b2"][None, :], (128, 1)))
    sh["ftau"] = g["L0_ffn_tau"].reshape(1, 1)
    sh["fWihT"] = _wT(g["L0_ffn_Wih"])
    sh["fWhhT"] = _wT(g["L0_ffn_Whh"])
    sh["fbihc"] = np.ascontiguousarray(np.tile(g["L0_ffn_bih"][None, :], (64, 1)))
    sh["fbhh"] = np.ascontiguousarray(g["L0_ffn_bhh"][None, :])
    sh["fwq"] = g["L0_ffn_wq"]
    sh["fbq"] = np.ascontiguousarray(g["L0_ffn_bq"].reshape(2, 128).T)
    sh["fwk"] = g["L0_ffn_wk"]
    sh["fbk"] = np.ascontiguousarray(g["L0_ffn_bk"].reshape(2, 128).T)
    sh["fwv"] = g["L0_ffn_wv"]
    sh["fbvc"] = np.ascontiguousarray(np.tile(g["L0_ffn_bv"][None, :], (64, 1)))
    sh["flngc"] = np.ascontiguousarray(np.tile(g["L0_ffn_ln_g"][None, :], (64, 1)))
    sh["flnbc"] = np.ascontiguousarray(np.tile(g["L0_ffn_ln_b"][None, :], (64, 1)))
    sh["fwvp"] = g["L0_ffn_wvp"]
    sh["fbvp"] = _chtab(g["L0_ffn_bvp"])
    sh["fwo"] = g["L0_ffn_wo"]
    sh["fboc"] = np.ascontiguousarray(np.tile(g["L0_ffn_bo"][None, :], (64, 1)))
    sh["fslot"] = g["L0_ffn_slot_init"]
    for nm in ("maa_x", "maa_w", "maa_k", "maa_v", "maa_r", "maa_g"):
        sh["b" + nm] = _chtab(g["L0_attn_b_" + nm])
    _mw1 = np.asarray(g["L0_attn_b_maa_w1"], np.float32)
    _w1p = np.zeros((1024, 288), np.float32)
    for f in range(4):
        _w1p[:, (f % 2) * 64 + (f // 2) * 128:(f % 2) * 64 + (f // 2) * 128 + 32] \
            = _mw1[:, f * 32:(f + 1) * 32]
    _w1p[:, 256:288] = _mw1[:, 128:160]
    sh["bmw1"] = _w1p
    _mw2 = np.ascontiguousarray(g["L0_attn_b_maa_w2"].reshape(160, 1024))
    _nm5 = ("maa_w", "maa_k", "maa_v", "maa_r", "maa_g")

    def _pack_pair(i, j):
        a = np.zeros((97, 1024), np.float32)
        a[0:32] = _mw2[i * 32:(i + 1) * 32]
        a[32] = np.asarray(g["L0_attn_b_" + _nm5[i]], np.float32)
        a[64:96] = _mw2[j * 32:(j + 1) * 32]
        a[96] = np.asarray(g["L0_attn_b_" + _nm5[j]], np.float32)
        return np.ascontiguousarray(a)
    sh["bmw2a"] = _pack_pair(0, 1)
    sh["bmw2b"] = _pack_pair(2, 3)
    _c = np.zeros((33, 1024), np.float32)
    _c[0:32] = _mw2[128:160]
    _c[32] = np.asarray(g["L0_attn_b_maa_g"], np.float32)
    sh["bmw2c"] = _c
    sh["bWr"] = g["L0_attn_b_Wr"]
    sh["bWk"] = g["L0_attn_b_Wk"]
    sh["bWv"] = g["L0_attn_b_Wv"]
    sh["bWo"] = g["L0_attn_b_Wo"]
    sh["bgw1"] = g["L0_attn_b_gate_w1"]
    sh["bgw2"] = g["L0_attn_b_gate_w2"]
    sh["btd"] = _chtab(g["L0_attn_b_time_decay"])
    sh["bdw1"] = g["L0_attn_b_decay_w1"]
    sh["bdw2"] = g["L0_attn_b_decay_w2"]
    u = np.asarray(g["L0_attn_b_u"], np.float32)
    u16 = np.zeros((D, NH), np.float32)
    for h in range(NH):
        u16[h * HD:(h + 1) * HD, h] = u[h]
    sh["bU16"] = np.ascontiguousarray(u16.reshape(NDC, 128, NH))
    sh["blngc"] = np.ascontiguousarray(
        np.tile(g["L0_attn_b_ln_g"][None, :], (128, 1))).astype(BF)
    sh["blnbc"] = np.ascontiguousarray(
        np.tile(g["L0_attn_b_ln_b"][None, :], (128, 1))).astype(BF)
    sh["identb"] = np.eye(128, dtype=BF)
    sh["identf"] = np.eye(128, dtype=np.float32)
    ka = np.arange(128)
    sh["maskA"] = (ka[:, None] <= ka[None, :]).astype(BF)
    ks = np.arange(64)
    sh["maskW"] = np.ascontiguousarray(
        np.tile((ks[:, None] < ks[None, :]).astype(np.float32), (1, NH)).reshape(
            64, NH, 64).transpose(1, 0, 2).reshape(NH * 64, 64).T)
    sh["maskW"] = (ks[:, None] < ks[None, :]).astype(np.float32)
    kb128 = np.arange(128)
    sh["maskB"] = (kb128[:, None] < kb128[None, :]).astype(BF)
    sh["onesb"] = np.ones((128, 72), BF)
    sh["onesf"] = np.ones((128, 144), np.float32)
    cv = np.array([1e-5, -8.0, 6.4e-4, 1e-8, 0.1, 75.0, 1.0, 0.0], np.float32)
    sh["cvals"] = np.ascontiguousarray(np.tile(cv[None, :], (128, 1)))
    xs = [np.ascontiguousarray(g["L0_attn_a_hidden_states"][i], np.float32)
          for i in range(8)]
    return sh, xs



class Ctx:
    def __init__(self, nc, blk, es):
        self.nc = nc
        self.blk = blk
        self.es = es
        self.sem = {}
        self.c = {}
        self.sbuf_bytes = 0

    def mk_sems(self, *names):
        for n in names:
            self.sem[n] = self.es.enter_context(self.nc.semaphore(n))
            self.c[n] = 0

    def inc(self, instr, name, k=1):
        instr.then_inc(self.sem[name], k)
        self.c[name] += k
        return self.c[name]

    def wait(self, eng, name, v=None):
        eng.wait_ge(self.sem[name], self.c[name] if v is None else v)

    def sb(self, name, shape, dt, stack=None):
        t = (stack or self.es).enter_context(
            self.nc.sbuf_tensor(name, list(shape), dt))
        n = 1
        for s in shape[1:]:
            n *= s
        self.sbuf_bytes += n * (2 if dt == BF16 else 4)
        return t

    def on(self, engine_name, fn):
        getattr(self.blk, engine_name)(fn)

    def dvsync(self):
        def f(v):
            self.inc(v.engine_nop(), "DV")
        self.on("vector", f)
        return self.c["DV"]

    def gpsync(self):
        def f(gp):
            self.inc(gp.engine_nop(), "GP")
        self.on("gpsimd", f)
        return self.c["GP"]

    def barrier(self):
        sem = self.sem["BAR"]
        dma_sems = ["LDW", "LDX", "LDH", "LDM", "LDG", "STH", "STX"]
        base = self.c["BAR"]
        tgt = base + 5
        self.c["BAR"] = tgt
        snap = {n: self.c[n] for n in dma_sems}

        dummy = self.dummy

        def body(eng, en):
            for n in dma_sems:
                if snap[n]:
                    eng.wait_ge(self.sem[n], snap[n])
            if en == "sync":
                eng.sem_inc(sem, 1)
            else:
                eng.drain().then_inc(sem, 1)
            eng.wait_ge(sem, tgt)
        for en in ("tensor", "vector", "scalar", "gpsimd", "sync"):
            self.on(en, lambda eng, en=en: body(eng, en))


class Ping:

    def __init__(self, n):
        self.free = [None] * n
        self.n = n

    def acquire(self, eng_wait, i):
        f = self.free[i % self.n]
        if f is not None:
            eng_wait(*f)

    def release(self, i, semname, val):
        self.free[i % self.n] = (semname, val)



def build(nc, sh):
    es = contextlib.ExitStack()
    blk = None

    dram = {}
    x0 = nc.dram_tensor("x0", [T, D], F32, kind="ExternalInput").ap()
    for nm, arr in sh.items():
        dt = BF16 if arr.dtype == BF else F32
        dram[nm] = nc.dram_tensor(nm, list(arr.shape), dt, kind="ExternalInput").ap()
    out = nc.dram_tensor("out", [T, D], F32, kind="ExternalOutput").ap()
    x1d = nc.dram_tensor("x1d", [T, D], F32, kind="Internal").ap()
    x2d = nc.dram_tensor("x2d", [T, D], F32, kind="Internal").ap()
    hd = nc.dram_tensor("hd", [T, D], BF16, kind="Internal").ap()

    blk = es.enter_context(nc.Block())
    ctx = Ctx(nc, blk, es)
    ctx.dummy = None
    ctx.mk_sems("BAR", "LDW", "LDX", "LDH", "LDM", "LDG", "STH", "STX",
                "PE", "DV", "AC", "GP")
    sb = ctx.sb

    W = [sb(f"Wslot{i}", (128, NDC, 1024), BF16) for i in range(4)]
    xin = sb("xin", (128, BT, 1024), F32)
    xout = sb("xout", (128, BT, 1024), F32)
    htok = sb("htok", (128, BT, 1024), BF16)
    hfeat = sb("hfeat", (128, NDC, BLK), BF16)
    stats = sb("stats", (128, 32), F32)
    ctx.dummy = stats

    identb = sb("identb_s", (128, 128), BF16)

    maskA = sb("maskA_s", (128, 128), BF16)
    maskB = sb("maskB_s", (128, 128), BF16)
    onesb = sb("onesb_s", (128, 72), BF16)
    onesf = sb("onesf_s", (128, 144), F32)
    cvals = sb("cvals_s", (128, 8), F32)


    psA = es.enter_context(nc.psum_tensor("psA", [128, 512], F32))
    psB = es.enter_context(nc.psum_tensor("psB", [128, 512], F32))
    psC = es.enter_context(nc.psum_tensor("psC", [128, 1024], F32))
    psD = es.enter_context(nc.psum_tensor("psD", [128, 512], F32))
    psE = es.enter_context(nc.psum_tensor("psE", [128, 512], F32))
    psT = es.enter_context(nc.psum_tensor("psT", [128, 2048], BF16))

    DUMP = os.environ.get("MK_DUMP", "")
    STOP = os.environ.get("MK_STOP", "")
    dump_done = [False]

    def dump_point(tag, mkap, nfree):
        if DUMP != tag or dump_done[0]:
            return
        dump_done[0] = True
        ctx.barrier()

        def f(gp):
            outv = out.rearrange("(a p) d -> p a d", p=128)
            src = mkap()
            np_ = src.shape[0]
            nfull, r = nfree // 1024, nfree % 1024
            if nfull:
                ctx.inc(gp.dma_start(
                    out=outv[0:np_, 0:nfull, :],
                    in_=src[:, 0:nfull * 1024].rearrange(
                        "p (a d) -> p a d", d=1024)), "STX", 16)
            if r:
                ctx.inc(gp.dma_start(
                    out=outv[0:np_, nfull, 0:r],
                    in_=src[:, nfull * 1024:nfree]), "STX", 16)
        ctx.on("gpsimd", f)

    def load_consts():
        def f(sync):
            for dst, src in ((identb, "identb"),
                             (maskA, "maskA"),
                             (maskB, "maskB"),
                             (onesb, "onesb"), (onesf, "onesf"),
                             (cvals, "cvals")):
                ctx.inc(sync.dma_start(out=dst[:], in_=dram[src][:, :]), "LDM", 16)
        ctx.on("sync", f)

        def g(gp):
            for i, src in enumerate(("aWq", "aWk", "aWv", "aWo")):
                ctx.inc(gp.dma_start(
                    out=W[i][:],
                    in_=dram[src].rearrange("(dc p) o -> p dc o", p=128)), "LDW", 16)
        ctx.on("gpsimd", g)



    def ln_block(xsrc, t0, nt=BT, g=None, htokb=None):
        g = g or {}
        hb = htok if htokb is None else htokb
        nw = nt * 128

        def f_load(sync):
            if g.get("xin_dv") is not None:
                sync.wait_ge(ctx.sem["DV"], g["xin_dv"])
            ctx.inc(sync.dma_start(
                out=xin[:, 0:nt, :],
                in_=xsrc[t0:t0 + nw, :].rearrange("(tt p) d -> p tt d", p=128)),
                "LDX", 16)
        ctx.on("sync", f_load)

        def f_stat_sc(sc):
            ctx.wait(sc, "LDX")
            if g.get("hfeat_pe") is not None:
                sc.wait_ge(ctx.sem["PE"], g["hfeat_pe"])
            if g.get("hfeat_dv") is not None:
                sc.wait_ge(ctx.sem["DV"], g["hfeat_dv"])
            if g.get("hfeat_gp") is not None:
                sc.wait_ge(ctx.sem["GP"], g["hfeat_gp"])
            scr = hfeat[:].rearrange("p a b -> p (a b)")
            for tt in range(nt):
                ctx.inc(sc.activation(scr[:, tt * 1024:(tt + 1) * 1024],
                                      xin[:, tt, :], AF.Square,
                                      accum_out=stats[:, 8 + tt:9 + tt]), "AC")
            for tt in range(nt):
                ctx.wait(sc, "AC")
                ctx.inc(sc.activation(scr[:, tt * 1024:(tt + 1) * 1024],
                                      xin[:, tt, :], AF.Identity,
                                      accum_out=stats[:, tt:tt + 1]), "AC")
        ctx.on("scalar", f_stat_sc)

        def f_stat(v):
            ctx.wait(v, "AC")
            ctx.inc(v.tensor_scalar(stats[:, 16:16 + nt], stats[:, 0:nt],
                                    -1.0 / D, None, ALU.mult), "DV")
            ctx.wait(v, "DV")
            ctx.inc(v.tensor_scalar(stats[:, 8:8 + nt], stats[:, 8:8 + nt],
                                    1.0 / D, None, ALU.mult), "DV")
            ctx.inc(v.tensor_tensor(stats[:, 24:24 + nt], stats[:, 16:16 + nt],
                                    stats[:, 16:16 + nt], ALU.mult), "DV")
            ctx.wait(v, "DV")
            ctx.inc(v.tensor_tensor(stats[:, 8:8 + nt], stats[:, 8:8 + nt],
                                    stats[:, 24:24 + nt], ALU.subtract), "DV")
        ctx.on("vector", f_stat)

        def f_sqrt(sc):
            ctx.wait(sc, "DV")
            ctx.inc(sc.activation(stats[:, 24:24 + nt], stats[:, 8:8 + nt],
                                  AF.Ln, bias=cvals[:, 0:1]), "AC")
            ctx.wait(sc, "AC")
            ctx.inc(sc.activation(stats[:, 8:8 + nt], stats[:, 24:24 + nt],
                                  AF.Exp, scale=-0.5), "AC")
        ctx.on("scalar", f_sqrt)

        def f_norm(v):
            ctx.wait(v, "AC")
            if g.get("htok_pe") is not None:
                v.wait_ge(ctx.sem["PE"], g["htok_pe"])
            for tt in range(nt):
                v.tensor_scalar(hb[:, tt, :], xin[:, tt, :],
                                stats[:, 16 + tt:17 + tt],
                                stats[:, 8 + tt:9 + tt], ALU.add, ALU.mult)
            ctx.inc(v.engine_nop(), "DV")
        ctx.on("vector", f_norm)
        dv_norm = ctx.c["DV"]
        ctx.norm_dv = dv_norm

        ac_prev = [None]
        for tp2 in range(0, nt, 2):
            def f_t(t, tp2=tp2, ap=ac_prev[0], dv_norm=dv_norm):
                if tp2 == 0:
                    t.wait_ge(ctx.sem["DV"], dv_norm)
                    if g.get("psT_ac") is not None:
                        t.wait_ge(ctx.sem["AC"], g["psT_ac"])
                    if g.get("psT_dv") is not None:
                        t.wait_ge(ctx.sem["DV"], g["psT_dv"])
                else:
                    t.wait_ge(ctx.sem["AC"], ap)
                for tt in (tp2, tp2 + 1):
                    bank = (tt % 2) * 1024
                    for dc in range(NDC):
                        ctx.inc(t.transpose(
                            psT[:, bank + dc * 128:bank + (dc + 1) * 128],
                            hb[:, tt, dc * 128:(dc + 1) * 128],
                            identb[:]), "PE")
            ctx.on("tensor", f_t)
            pe_t = ctx.c["PE"]

            def f_e(sc, tp2=tp2, pe_t=pe_t):
                ctx.wait(sc, "PE", pe_t)
                for tt in (tp2, tp2 + 1):
                    bank = (tt % 2) * 1024
                    ctx.inc(sc.activation(
                        hfeat[:, :, tt * 128:(tt + 1) * 128],
                        psT[:, bank:bank + 1024].rearrange(
                            "p (a b) -> p a b", b=128),
                        AF.Identity), "AC")
            ctx.on("scalar", f_e)
            ac_prev[0] = ctx.c["AC"]
        ctx.hfeat_ac = ctx.c["AC"]

    def proj_feat(wslot, dst, dst_off, nw, bias_tab=None):
        ping = Ping(2)
        ha = ctx.hfeat_ac
        for ot in range(NDC):
            ps = psA if ot % 2 == 0 else psB

            def f_mm(t, ot=ot, ps=ps, ha=ha):
                if ot < 2:
                    ctx.wait(t, "LDW")
                    t.wait_ge(ctx.sem["AC"], ha)
                ping.acquire(lambda s, v: t.wait_ge(ctx.sem[s], v), ot)
                for dc in range(NDC):
                    mm = t.matmul(ps[:, 0:nw], wslot[:, dc, ot * 128:(ot + 1) * 128],
                                  hfeat[:, dc, 0:nw],
                                  start=(dc == 0), stop=(dc == NDC - 1))
                ctx.inc(mm, "PE")
            ctx.on("tensor", f_mm)
            pe_v = ctx.c["PE"]

            def f_ev(v, ot=ot, ps=ps, pe_v=pe_v):
                ctx.wait(v, "PE", pe_v)
                if bias_tab is None:
                    ins = v.tensor_copy(dst[:, ot, dst_off:dst_off + nw],
                                        ps[:, 0:nw])
                else:
                    ins = v.tensor_scalar(dst[:, ot, dst_off:dst_off + nw],
                                          ps[:, 0:nw],
                                          bias_tab[:, ot:ot + 1], None, ALU.add)
                ctx.inc(ins, "DV")
            ctx.on("vector", f_ev)
            ping.release(ot, "DV", ctx.c["DV"])

    def emit_attn_a(xtgt):
        ascope = contextlib.ExitStack()
        Kf = sb("Kf", (128, NDC, 1024), BF16, stack=ascope)
        Vaug = sb("Vaug", (128, 8, NH, 65), BF16, stack=ascope)
        Qf = sb("Qf", (128, NDC, BLK), BF16, stack=ascope)
        Otok = sb("Otok", (128, BT, 1024), BF16, stack=ascope)
        Ofeat = sb("Ofeat", (128, NDC, BLK), BF16, stack=ascope)
        nrm4 = sb("nrm4", (128, 8), F32, stack=ascope)
        load_consts()

        def f_vone(v):
            v.memset(Vaug[:, :, :, 64:65], 1.0)
            ctx.inc(v.engine_nop(), "DV")
        ctx.on("vector", f_vone)
        ctx.barrier()

        kprev = {}
        pingC = Ping(1)
        for kb in range(2):
            ln_block(x0, kb * BLK, g={
                "xin_dv": kprev.get("norm_dv"),
                "hfeat_pe": kprev.get("pe"),
                "psT_ac": kprev.get("psT_ac"),
            })
            kprev["norm_dv"] = ctx.norm_dv
            kprev["psT_ac"] = ctx.hfeat_ac
            if kb == 0:
                if DUMP == "hd" and not dump_done[0]:
                    dump_done[0] = True
                    ctx.barrier()
                    def f_hd(gp):
                        ctx.inc(gp.dma_start(out=out[0:512, :], in_=hd[0:512, :]),
                                "STX", 16)
                    ctx.on("gpsimd", f_hd)
                dump_point("stats", lambda: stats[:], 32)
                dump_point("xin", lambda: xin[:].rearrange("p a b -> p (a b)"), 4096)
                dump_point("htok", lambda: htok[:].rearrange("p a b -> p (a b)"), 4096)
                dump_point("hfeat", lambda: hfeat[:].rearrange("p a b -> p (a b)"), 4096)
            proj_feat(W[1], Kf, kb * BLK, BLK)

            for tt in range(BT):
                def f_vm(t, tt=tt):
                    pingC.acquire(lambda s, v: t.wait_ge(ctx.sem[s], v), 0)
                    for half in range(2):
                        for dc in range(NDC):
                            mm = t.matmul(psC[:, half * 512:(half + 1) * 512],
                                          hfeat[:, dc, tt * 128:(tt + 1) * 128],
                                          W[2][:, dc, half * 512:(half + 1) * 512],
                                          start=(dc == 0), stop=(dc == NDC - 1))
                        ctx.inc(mm, "PE")
                ctx.on("tensor", f_vm)
                pe_v = ctx.c["PE"]

                def f_vev(v, tt=tt, kb=kb, pe_v=pe_v):
                    ctx.wait(v, "PE", pe_v)
                    ctx.inc(v.tensor_copy(
                        Vaug[:, kb * BT + tt, :, 0:64],
                        psC[:, 0:1024].rearrange("p (h j) -> p h j", h=NH)), "DV")
                ctx.on("vector", f_vev)
                pingC.release(0, "DV", ctx.c["DV"])
            kprev["pe"] = ctx.c["PE"]

        dump_point("Kf", lambda: Kf[:].rearrange("p a b -> p (a b)"), 8192)
        dump_point("Vaug", lambda: Vaug[:].rearrange("p a b c -> p (a b c)"), 8320)
        if STOP == "akey":
            ascope.close()
            return

        Ph2 = [W[1][:].rearrange("p a b -> p (a b)"),
               W[2][:].rearrange("p a b -> p (a b)")]
        aprev = {}
        ctx.barrier()
        for qb in range(NBLK):
            ln_block(x0, qb * BLK, g={
                "xin_dv": aprev.get("res_dv"),
                "hfeat_pe": aprev.get("qf_pe"),
                "psT_dv": aprev.get("te_dv"),
            })
            proj_feat(W[0], Qf, 0, BLK)
            aprev["qf_pe"] = ctx.c["PE"]
            r = qb % 2
            KTm = r * 4 + 4

            pingS = Ping(4)
            pingPV = Ping(2)
            pv_args = [None]

            def emit_pv(h, dv_mask):
                psPV = psD if h % 2 == 0 else psE
                Ph = Ph2[h % 2]

                def f_pv(t, h=h, psPV=psPV, Ph=Ph, dv_mask=dv_mask):
                    t.wait_ge(ctx.sem["DV"], dv_mask)
                    pingPV.acquire(lambda s, v: t.wait_ge(ctx.sem[s], v), h)
                    for qt in range(BT):
                        lq = r * 4 + qt
                        for kt in range(lq + 1):
                            mm = t.matmul(
                                psPV[:, qt * 65:qt * 65 + 65],
                                Ph[:, kt * 512 + qt * 128:kt * 512 + (qt + 1) * 128],
                                Vaug[:, kt, h, :],
                                start=(kt == 0), stop=(kt == lq))
                    ctx.inc(mm, "PE")
                ctx.on("tensor", f_pv)
                pe_pv = ctx.c["PE"]

                def f_oev(v, h=h, psPV=psPV, pe_pv=pe_pv):
                    ctx.wait(v, "PE", pe_pv)
                    pv = psPV[:, 0:260].rearrange("p (q z) -> p q z", z=65)
                    ctx.inc(v.tensor_copy(nrm4[:, 0:4].unsqueeze(2),
                                          pv[:, :, 64:65]), "DV")
                    ctx.wait(v, "DV")
                    ctx.inc(v.reciprocal(nrm4[:, 4:8], nrm4[:, 0:4]), "DV")
                    ctx.wait(v, "DV")
                    ctx.inc(v.tensor_tensor(
                        Otok[:, :, 64 * h:64 * h + 64], pv[:, :, 0:64],
                        nrm4[:, 4:8].unsqueeze(2).broadcast_to([128, 4, 64]),
                        ALU.mult), "DV")
                ctx.on("vector", f_oev)
                pingPV.release(h, "DV", ctx.c["DV"])

            sbank = [psA[:, 0:512], psB[:, 0:512], psC[:, 0:512],
                     psC[:, 512:1024]]
            for h in range(NH):
                hs, hf = 64 * (h % 2), h // 2
                Ph = Ph2[h % 2]
                for kp in range(0, KTm, 4):
                    grp = [kp, kp + 1, kp + 2, kp + 3]
                    for kt in grp:
                        qo = max(0, kt - r * 4) * 128
                        ps = sbank[kt % 4]

                        first = (h == 0 and kt == 0)

                        def f_sc(t, h=h, kt=kt, qo=qo, ps=ps, hs=hs, hf=hf,
                                 first=first):
                            if first:
                                ctx.wait(t, "DV")
                            pingS.acquire(
                                lambda s, v: t.wait_ge(ctx.sem[s], v), kt)
                            mm = t.matmul(ps[:, 0:512 - qo],
                                          Kf[hs:hs + 64, hf,
                                             kt * 128:(kt + 1) * 128],
                                          Qf[hs:hs + 64, hf, qo:512],
                                          start=True, stop=True)
                            ctx.inc(mm, "PE")
                        ctx.on("tensor", f_sc)
                    pe_grp = ctx.c["PE"]

                    def f_ex(sc, grp=grp, Ph=Ph, pe_grp=pe_grp):
                        ctx.wait(sc, "PE", pe_grp)
                        for kt in grp:
                            qo = max(0, kt - r * 4) * 128
                            ps = sbank[kt % 4]
                            ctx.inc(sc.activation(
                                Ph[:, kt * 512 + qo:(kt + 1) * 512],
                                ps[:, 0:512 - qo], AF.Exp, bias=cvals[:, 1:2],
                                scale=0.125), "AC")
                    ctx.on("scalar", f_ex)
                    for j, kt in enumerate(grp):
                        pingS.release(kt, "AC", ctx.c["AC"] - 3 + j)

                def f_mask(v, r=r, Ph=Ph):
                    ctx.wait(v, "AC")
                    base = r * 4 * 512
                    view = Ph[:, base:base + 2560].rearrange(
                        "p (j z) -> p j z", z=640)[:, :, 0:128]
                    mb = maskA[:, :].unsqueeze(1).broadcast_to([128, 4, 128])
                    ctx.inc(v.tensor_tensor(view, view, mb, ALU.mult), "DV")
                ctx.on("vector", f_mask)
                gp_mask = ctx.dvsync()

                if STOP != "asc" and pv_args[0] is not None:
                    emit_pv(*pv_args[0])
                pv_args[0] = (h, gp_mask)
            if STOP == "asc" and qb == 0:
                ascope.close()
                return
            emit_pv(*pv_args[0])
            if STOP == "aq1" and qb == 0:
                ascope.close()
                return

            if qb == 0:
                dump_point("Qf", lambda: Qf[:].rearrange("p a b -> p (a b)"), 4096)
                dump_point("Phead", lambda: Ph2[1][:, 0:4800], 4800)
                dump_point("Otok", lambda: Otok[:].rearrange("p a b -> p (a b)"), 4096)

            prev_te = [None, None]
            for tt in range(BT):
                bank = (tt % 2) * 1024

                def f_tp(t, tt=tt, bank=bank, pte=prev_te[tt % 2]):
                    if tt == 0:
                        ctx.wait(t, "DV")
                    if pte is not None:
                        t.wait_ge(ctx.sem["DV"], pte)
                    for dc in range(NDC):
                        ctx.inc(t.transpose(
                            psT[:, bank + dc * 128:bank + (dc + 1) * 128],
                            Otok[:, tt, dc * 128:(dc + 1) * 128],
                            identb[:]), "PE")
                ctx.on("tensor", f_tp)
                pe_v = ctx.c["PE"]

                def f_te(v, tt=tt, bank=bank, pe_v=pe_v):
                    ctx.wait(v, "PE", pe_v)
                    ctx.inc(v.tensor_copy(
                        Ofeat[:, :, tt * 128:(tt + 1) * 128],
                        psT[:, bank:bank + 1024].rearrange(
                            "p (a b) -> p a b", b=128)), "DV")
                ctx.on("vector", f_te)
                prev_te[tt % 2] = ctx.c["DV"]
            aprev["te_dv"] = ctx.c["DV"]
            if STOP == "atp" and qb == 0:
                ascope.close()
                return

            pingC = Ping(1)
            for tt in range(BT):
                def f_om(t, tt=tt):
                    if tt == 0:
                        ctx.wait(t, "DV")
                    pingC.acquire(lambda s, v: t.wait_ge(ctx.sem[s], v), 0)
                    for half in range(2):
                        for dc in range(NDC):
                            mm = t.matmul(psC[:, half * 512:(half + 1) * 512],
                                          Ofeat[:, dc, tt * 128:(tt + 1) * 128],
                                          W[3][:, dc, half * 512:(half + 1) * 512],
                                          start=(dc == 0), stop=(dc == NDC - 1))
                        ctx.inc(mm, "PE")
                ctx.on("tensor", f_om)
                pe_v = ctx.c["PE"]

                def f_res(v, tt=tt, pe_v=pe_v):
                    ctx.wait(v, "PE", pe_v)
                    if tt == 0:
                        ctx.wait(v, "STX")
                    ctx.inc(v.tensor_tensor(xout[:, tt, :], psC[:, 0:1024],
                                            xin[:, tt, :], ALU.add), "DV")
                ctx.on("vector", f_res)
                pingC.release(0, "DV", ctx.c["DV"])
            aprev["res_dv"] = ctx.c["DV"]

            def f_st(gp, qb=qb):
                ctx.wait(gp, "DV")
                ctx.inc(gp.dma_start(
                    out=xtgt[qb * BLK:(qb + 1) * BLK, :].rearrange(
                        "(tt p) d -> p tt d", p=128),
                    in_=xout[:, 0:BT, :]), "STX", 16)
            ctx.on("gpsimd", f_st)
            if STOP == f"aqb{qb}":
                ascope.close()
                return
        ascope.close()

    def emit_ffn(x_in, x_out, tag):
        fscope = contextlib.ExitStack()
        _sb0 = ctx.sb
        ctx_sb = lambda n, shp, dt: _sb0(n, shp, dt, stack=fscope)
        alpha = ctx_sb(tag + "alpha", (128, 32, 64), BF16)
        htok2 = ctx_sb(tag + "htok2", (128, BT, 1024), BF16)
        Gf = ctx_sb(tag + "Gf", (128, 4, 512), BF16)
        lgt = ctx_sb(tag + "lgt", (128, 4, 64), F32)
        LW = ctx_sb(tag + "LW", (128, 4, 64), F32)
        eqb = ctx_sb(tag + "eqb", (128, 4, 64), F32)
        mxs = ctx_sb(tag + "mxs", (128, 16), F32)
        S0f = ctx_sb(tag + "S0f", (64, 1024), F32)
        S0b = ctx_sb(tag + "S0b", (64, 1024), BF16)
        S0ft = ctx_sb(tag + "S0ft", (128, 8, 64), BF16)
        smf = ctx_sb(tag + "smf", (128, 8), F32)
        smfb = ctx_sb(tag + "smfb", (128, 8), BF16)
        b1p = ctx_sb(tag + "b1p", (128, 4), F32)
        w2sb = ctx_sb(tag + "w2sb", (128, 4, 64), BF16)
        fb1t = ctx_sb(tag + "fb1t", (128, 4), F32)
        staub = ctx_sb(tag + "staub", (128, 1), F32)
        b2s = ctx_sb(tag + "b2s", (128, 64), F32)
        tau1 = ctx_sb(tag + "tau1", (1, 4), F32)
        Cadd = ctx_sb(tag + "Cadd", (64, 3072), F32)
        Hn = ctx_sb(tag + "Hn", (64, 1024), F32)
        bhhb = ctx_sb(tag + "bhhb", (1, 3072), BF16)
        preN = ctx_sb(tag + "preN", (64, 1024), F32)
        rz = ctx_sb(tag + "rz", (64, 2048), F32)
        tmp1 = ctx_sb(tag + "tmp1", (64, 1024), F32)
        tmp2 = ctx_sb(tag + "tmp2", (64, 1024), F32)
        tmpA = ctx_sb(tag + "tmpA", (64, 1024), F32)
        tmpB = ctx_sb(tag + "tmpB", (64, 1024), F32)
        Snew = ctx_sb(tag + "Snew", (64, 1024), F32)
        Snf = ctx_sb(tag + "Snf", (128, 8, 64), BF16)
        qf2 = ctx_sb(tag + "qf2", (128, 2, 64), BF16)
        kf2 = ctx_sb(tag + "kf2", (128, 2, 64), BF16)
        fbqt = ctx_sb(tag + "fbqt", (128, 2), F32)
        fbkt = ctx_sb(tag + "fbkt", (128, 2), F32)
        sst = ctx_sb(tag + "sst", (64, 8), F32)
        atb = ctx_sb(tag + "atb", (64, 64), BF16)
        atT = ctx_sb(tag + "atT", (64, 64), BF16)
        vb2 = ctx_sb(tag + "vb2", (64, 1024), BF16)
        ynb = ctx_sb(tag + "ynb", (64, 1024), BF16)
        ynf = ctx_sb(tag + "ynf", (128, 8, 64), BF16)
        SVf = ctx_sb(tag + "SVf", (128, 8, 64), BF16)
        fbvpt = ctx_sb(tag + "fbvpt", (128, 8), F32)
        svwb = ctx_sb(tag + "svwb", (64, 1024), BF16)
        SVW2 = ctx_sb(tag + "SVW2", (128, 1024), BF16)
        aTb = ctx_sb(tag + "aTb", (64, 512), BF16)

        pingST = Ping(2)

        def slot_transpose(srcap, dst):
            for dc in range(NDC):
                half = dc % 2

                def f_t(t, dc=dc, half=half):
                    if dc == 0:
                        ctx.wait(t, "DV")
                    pingST.acquire(lambda s, v: t.wait_ge(ctx.sem[s], v), half)
                    ctx.inc(t.transpose(psT[:, half * 1024:half * 1024 + 64],
                                        srcap[0:64, dc * 128:(dc + 1) * 128],
                                        identb[0:64, 0:64]), "PE")
                ctx.on("tensor", f_t)
                pe_v = ctx.c["PE"]

                def f_e(v, dc=dc, half=half, pe_v=pe_v):
                    ctx.wait(v, "PE", pe_v)
                    ctx.inc(v.tensor_copy(dst[:, dc, 0:64],
                                          psT[:, half * 1024:half * 1024 + 64]),
                            "DV")
                ctx.on("vector", f_e)
                pingST.release(half, "DV", ctx.c["DV"])

        ctx.barrier()

        def f_ld(gp):
            for dst, src in (
                (W[0][:, :, 0:512], dram["fw1a"].rearrange("(dc p) o -> p dc o", p=128)),
                (W[1][:, :, 0:512], dram["fw1b"].rearrange("(dc p) o -> p dc o", p=128)),
                (w2sb[:], dram["fw2"].rearrange("(oc p) s -> p oc s", p=128)),
                (S0f[:], dram["fslot"][:, :]),
                (bhhb[:], dram["fbhh"][:, :]),
                (Cadd[:], dram["fbihc"][:, :]),
                (tau1[0:1, 0:1], dram["ftau"][:, :]),
                (fb1t[:], dram["fb1"][:, :]),
                (b2s[:], dram["fb2c"][:, :]),
                (fbqt[:], dram["fbq"][:, :]),
                (fbkt[:], dram["fbk"][:, :]),
                (fbvpt[:], dram["fbvp"][:, :]),
            ):
                ctx.inc(gp.dma_start(out=dst, in_=src), "LDW", 16)
        ctx.on("gpsimd", f_ld)

        def f_tauabs(sc):
            ctx.wait(sc, "LDW")
            ctx.inc(sc.activation(tau1[0:1, 1:2], tau1[0:1, 0:1], AF.Abs), "AC")
        ctx.on("scalar", f_tauabs)

        def f_pv(v):
            ctx.wait(v, "LDW")
            ctx.inc(v.tensor_copy(S0b[:], S0f[:]), "DV")
            ctx.wait(v, "AC")
            ctx.inc(v.tensor_scalar(tau1[0:1, 1:2], tau1[0:1, 1:2], 0.1, None,
                                    ALU.add), "DV")
            ctx.wait(v, "DV")
            ctx.inc(v.reciprocal(tau1[0:1, 2:3], tau1[0:1, 1:2]), "DV")
        ctx.on("vector", f_pv)

        slot_transpose(S0b[:], S0ft)

        def f_sm(t):
            ctx.wait(t, "DV")
            for dc in range(NDC):
                ctx.inc(t.matmul(psA[:, dc:dc + 1], S0b[:, dc * 128:(dc + 1) * 128],
                                 onesb[0:64, 0:1], start=True, stop=True), "PE")
            ctx.inc(t.matmul(psD[:, 0:1], onesf[0:1, 0:128], tau1[0:1, 2:3],
                             start=True, stop=True), "PE")
        ctx.on("tensor", f_sm)
        pe_sm = ctx.c["PE"]

        def f_sme(v, pe_sm=pe_sm):
            ctx.wait(v, "PE", pe_sm)
            ctx.inc(v.tensor_scalar(smf[:, 0:8], psA[:, 0:8], 1.0 / 64, None,
                                    ALU.mult), "DV")
        ctx.on("vector", f_sme)

        def f_smb(v):
            ctx.wait(v, "AC")
            ctx.wait(v, "DV")
            ctx.inc(v.tensor_copy(smfb[:], smf[:]), "DV")
            ctx.inc(v.tensor_copy(staub[:, 0:1], psD[:, 0:1]), "DV")
            ctx.wait(v, "DV")
            ctx.inc(v.tensor_tensor(b2s[:], b2s[:],
                                    staub[:, 0:1].broadcast_to([128, 64]),
                                    ALU.mult), "DV")
        ctx.on("vector", f_smb)
        dv_smb = ctx.c["DV"]

        def f_b1(t, dv_smb=dv_smb):
            t.wait_ge(ctx.sem["DV"], dv_smb)
            for oc in range(4):
                for dc in range(NDC):
                    mm = t.matmul(psB[:, oc:oc + 1],
                                  W[1][:, dc, oc * 128:(oc + 1) * 128],
                                  smfb[:, dc:dc + 1],
                                  start=(dc == 0), stop=(dc == NDC - 1))
                ctx.inc(mm, "PE")
        ctx.on("tensor", f_b1)
        pe_b1 = ctx.c["PE"]

        def f_b1e(v, pe_b1=pe_b1):
            ctx.wait(v, "PE", pe_b1)
            ctx.inc(v.tensor_tensor(b1p[:], psB[:, 0:4], fb1t[:], ALU.add), "DV")
        ctx.on("vector", f_b1e)

        ctx.barrier()

        def f_whh(gp):
            for sl, c0 in ((W[2], 0), (W[3], 1024)):
                ctx.inc(gp.dma_start(
                    out=sl[:],
                    in_=dram["fWhhT"][:, c0:c0 + 1024].rearrange(
                        "(dc p) o -> p dc o", p=128)), "LDW", 16)
        ctx.on("gpsimd", f_whh)

        def f_whh2(gp):
            ctx.wait(gp, "PE")
            ctx.inc(gp.dma_start(
                out=W[2][:],
                in_=dram["fWhhT"][:, 2048:3072].rearrange(
                    "(dc p) o -> p dc o", p=128)), "LDW", 16)

        for piece in range(6):
            wslot, coff = [(W[2], 0), (W[2], 512), (W[3], 0), (W[3], 512),
                           (W[2], 0), (W[2], 512)][piece]
            ps = psA if piece % 2 == 0 else psB
            if piece == 4:
                ctx.on("gpsimd", f_whh2)

            def f_hm(t, wslot=wslot, coff=coff, ps=ps, piece=piece):
                if piece in (0, 4):
                    ctx.wait(t, "LDW")
                if piece == 0:
                    ctx.wait(t, "DV")
                if piece >= 2:
                    ctx.wait(t, "DV")
                for dc in range(NDC):
                    t.matmul(ps[0:64, 0:512], S0ft[:, dc, :],
                             wslot[:, dc, coff:coff + 512],
                             start=(dc == 0), stop=False)
                mm = t.matmul(ps[0:64, 0:512], onesb[0:1, 0:64],
                              bhhb[0:1, piece * 512:(piece + 1) * 512],
                              start=False, stop=True)
                ctx.inc(mm, "PE")
            ctx.on("tensor", f_hm)
            pe_h = ctx.c["PE"]

            def f_he(v, piece=piece, ps=ps, pe_h=pe_h):
                ctx.wait(v, "PE", pe_h)
                if piece < 4:
                    ctx.inc(v.tensor_tensor(Cadd[:, piece * 512:(piece + 1) * 512],
                                            Cadd[:, piece * 512:(piece + 1) * 512],
                                            ps[0:64, 0:512], ALU.add), "DV")
                else:
                    off = (piece - 4) * 512
                    ctx.inc(v.tensor_copy(Hn[:, off:off + 512], ps[0:64, 0:512]),
                            "DV")
            ctx.on("vector", f_he)

        if STOP == "pre":
            fscope.close()
            return
        slotin_first = [True]
        fprev = {}
        si_args = [None]

        def emit_si(fb, hb, dv_alpha):
            def f_si(t, fb=fb, hb=hb, dv_alpha=dv_alpha):
                t.wait_ge(ctx.sem["DV"], dv_alpha)
                for tt in range(BT):
                    gtt = fb * BT + tt
                    st = slotin_first[0]
                    for half in range(2):
                        t.matmul(psC[0:64, half * 512:(half + 1) * 512],
                                 alpha[:, gtt, :],
                                 hb[:, tt, half * 512:(half + 1) * 512],
                                 start=st, stop=False, skip_group_check=True)
                    t.matmul(psE[0:64, 0:1], alpha[:, gtt, :], onesb[:, 0:1],
                             start=st, stop=False, skip_group_check=True)
                    slotin_first[0] = False
                ctx.inc(t.drain(), "PE")
            ctx.on("tensor", f_si)
            fprev["si_pe%d" % (fb % 2)] = ctx.c["PE"]

        ctx.barrier()
        for fb in range(NBLK):
            hb = htok if fb % 2 == 0 else htok2
            ln_block(x_in, fb * BLK, htokb=hb, g={
                "xin_dv": fprev.get("norm_dv"),
                "hfeat_pe": fprev.get("gm_pe"),
                "htok_pe": fprev.get("si_pe%d" % (fb % 2)),
                "psT_ac": fprev.get("psT_ac"),
            })
            fprev["norm_dv"] = ctx.norm_dv
            fprev["psT_ac"] = ctx.hfeat_ac

            pingG = Ping(2)
            for op in range(0, 4, 2):
                for oc in (op, op + 1):
                    ps = psA if oc % 2 == 0 else psB

                    def f_gm(t, oc=oc, ps=ps, ha=ctx.hfeat_ac):
                        if oc == 0:
                            t.wait_ge(ctx.sem["AC"], ha)
                        pingG.acquire(lambda s, v: t.wait_ge(ctx.sem[s], v), oc)
                        for dc in range(NDC):
                            mm = t.matmul(ps[:, 0:512],
                                          W[0][:, dc, oc * 128:(oc + 1) * 128],
                                          hfeat[:, dc, :],
                                          start=(dc == 0), stop=(dc == NDC - 1))
                        ctx.inc(mm, "PE")
                    ctx.on("tensor", f_gm)
                pe_g = ctx.c["PE"]

                def f_ge(sc, op=op, pe_g=pe_g):
                    ctx.wait(sc, "PE", pe_g)
                    for oc in (op, op + 1):
                        ps = psA if oc % 2 == 0 else psB
                        ctx.inc(sc.activation(Gf[:, oc, :], ps[:, 0:512], AF.Gelu,
                                              bias=b1p[:, oc:oc + 1]), "AC")
                ctx.on("scalar", f_ge)
                pingG.release(op, "AC", ctx.c["AC"] - 1)
                pingG.release(op + 1, "AC", ctx.c["AC"])
            fprev["gm_pe"] = ctx.c["PE"]

            def f_lm(t):
                ctx.wait(t, "AC")
                for tt in range(BT):
                    for oc in range(4):
                        mm = t.matmul(psD[:, tt * 64:(tt + 1) * 64],
                                      Gf[:, oc, tt * 128:(tt + 1) * 128],
                                      w2sb[:, oc, :],
                                      start=(oc == 0), stop=(oc == 3))
                    ctx.inc(mm, "PE")
            ctx.on("tensor", f_lm)
            pe_l = ctx.c["PE"]
            if si_args[0] is not None:
                emit_si(*si_args[0])

            def f_le(v, pe_l=pe_l):
                ctx.wait(v, "PE", pe_l)
                ctx.inc(v.scalar_tensor_tensor(
                    lgt[:], psD[:, 0:256].rearrange("p (a b) -> p a b", b=64),
                    staub[:, 0:1], b2s[:].unsqueeze(1).broadcast_to([128, 4, 64]),
                    ALU.mult, ALU.add), "DV")
                ctx.wait(v, "DV")
                ctx.inc(v.tensor_copy(LW[:], lgt[:]), "DV")
            ctx.on("vector", f_le)

            def f_topk(v):
                ctx.wait(v, "DV")
                for rnd in range(8):
                    ctx.inc(v.reduce_max(mxs[:, 0:4].unsqueeze(2), LW[:], AX.X),
                            "DV")
                    ctx.wait(v, "DV")
                    if rnd == 0:
                        ctx.inc(v.tensor_scalar(mxs[:, 4:8], mxs[:, 0:4], -1.0,
                                                None, ALU.mult), "DV")
                    if rnd == 7:
                        ctx.inc(v.tensor_copy(mxs[:, 8:12], mxs[:, 0:4]), "DV")
                        ctx.wait(v, "DV")
                        break
                    ctx.inc(v.tensor_tensor(
                        eqb[:], LW[:],
                        mxs[:, 0:4].unsqueeze(2).broadcast_to([128, 4, 64]),
                        ALU.is_equal), "DV")
                    ctx.wait(v, "DV")
                    ctx.inc(v.scalar_tensor_tensor(LW[:], eqb[:], -1e30, LW[:],
                                                   ALU.mult, ALU.add), "DV")
                    ctx.wait(v, "DV")
                ctx.inc(v.tensor_tensor(
                    eqb[:], lgt[:],
                    mxs[:, 8:12].unsqueeze(2).broadcast_to([128, 4, 64]),
                    ALU.is_ge), "DV")
            ctx.on("vector", f_topk)

            def f_aexp(sc):
                ctx.wait(sc, "DV")
                for tt in range(BT):
                    ctx.inc(sc.activation(LW[:, tt, :], lgt[:, tt, :], AF.Exp,
                                          bias=mxs[:, 4 + tt:5 + tt]), "AC")
            ctx.on("scalar", f_aexp)

            def f_anorm(v, fb=fb):
                ctx.wait(v, "AC")
                ctx.inc(v.tensor_tensor(LW[:], LW[:], eqb[:], ALU.mult), "DV")
                ctx.wait(v, "DV")
                ctx.inc(v.reduce_sum(mxs[:, 0:4].unsqueeze(2), LW[:], AX.X), "DV")
                ctx.wait(v, "DV")
                ctx.inc(v.reciprocal(mxs[:, 4:8], mxs[:, 0:4]), "DV")
                ctx.wait(v, "DV")
                ctx.inc(v.tensor_tensor(
                    alpha[:, fb * BT:(fb + 1) * BT, :], LW[:],
                    mxs[:, 4:8].unsqueeze(2).broadcast_to([128, 4, 64]),
                    ALU.mult), "DV")
            ctx.on("vector", f_anorm)
            dv_alpha = ctx.dvsync()

            si_args[0] = (fb, hb, dv_alpha)

        emit_si(*si_args[0])
        if STOP == "p1":
            fscope.close()
            return
        ctx.barrier()
        dump_point("falpha", lambda: alpha[:].rearrange("p a b -> p (a b)"), 2048)

        def f_mid_si(v):
            ctx.inc(v.tensor_scalar(sst[:, 0:1], psE[0:64, 0:1],
                                    cvals[0:64, 3:4], None, ALU.add), "DV")
            ctx.wait(v, "DV")
            ctx.inc(v.reciprocal(sst[:, 1:2], sst[:, 0:1]), "DV")
            ctx.wait(v, "DV")
            ctx.inc(v.tensor_scalar(S0b[:], psC[0:64, 0:1024], sst[:, 1:2], None,
                                    ALU.mult), "DV")
        ctx.on("vector", f_mid_si)
        ctx.dvsync()
        dump_point("fsi", lambda: S0b[:], 1024)
        if STOP == "mid0a":
            fscope.close()
            return

        slot_transpose(S0b[:], ynf)
        if STOP == "mid0b":
            fscope.close()
            return

        if STOP == "mid1":
            fscope.close()
            return
        ctx.barrier()

        def f_wih(gp):
            for sl, c0 in ((W[2], 0), (W[3], 1024)):
                ctx.inc(gp.dma_start(
                    out=sl[:],
                    in_=dram["fWihT"][:, c0:c0 + 1024].rearrange(
                        "(dc p) o -> p dc o", p=128)), "LDW", 16)
        ctx.on("gpsimd", f_wih)

        def f_wih2(gp):
            ctx.wait(gp, "PE")
            ctx.inc(gp.dma_start(
                out=W[2][:],
                in_=dram["fWihT"][:, 2048:3072].rearrange(
                    "(dc p) o -> p dc o", p=128)), "LDW", 16)

        for piece in range(6):
            wslot, coff = [(W[2], 0), (W[2], 512), (W[3], 0), (W[3], 512),
                           (W[2], 0), (W[2], 512)][piece]
            ps = psA if piece % 2 == 0 else psB
            if piece == 4:
                ctx.on("gpsimd", f_wih2)

            def f_gim(t, wslot=wslot, coff=coff, ps=ps, piece=piece):
                if piece in (0, 4):
                    ctx.wait(t, "LDW")
                if piece == 0:
                    ctx.wait(t, "DV")
                if piece >= 2:
                    ctx.wait(t, "DV")
                for dc in range(NDC):
                    mm = t.matmul(ps[0:64, 0:512], ynf[:, dc, :],
                                  wslot[:, dc, coff:coff + 512],
                                  start=(dc == 0), stop=(dc == NDC - 1))
                ctx.inc(mm, "PE")
            ctx.on("tensor", f_gim)
            pe_gi = ctx.c["PE"]

            def f_gie(v, piece=piece, ps=ps, pe_gi=pe_gi):
                ctx.wait(v, "PE", pe_gi)
                if piece < 4:
                    ctx.inc(v.tensor_tensor(rz[:, piece * 512:(piece + 1) * 512],
                                            ps[0:64, 0:512],
                                            Cadd[:, piece * 512:(piece + 1) * 512],
                                            ALU.add), "DV")
                else:
                    off = (piece - 4) * 512
                    ctx.inc(v.tensor_tensor(preN[:, off:off + 512],
                                            ps[0:64, 0:512],
                                            Cadd[:, 2048 + off:2048 + off + 512],
                                            ALU.add), "DV")
            ctx.on("vector", f_gie)

        def f_sig(sc):
            ctx.wait(sc, "DV")
            ctx.inc(sc.activation(rz[:, 0:1024], rz[:, 0:1024], AF.Sigmoid), "AC")
            ctx.inc(sc.activation(rz[:, 1024:2048], rz[:, 1024:2048], AF.Sigmoid),
                    "AC")
        ctx.on("scalar", f_sig)

        def f_gru1(v):
            ctx.wait(v, "AC")
            ctx.inc(v.tensor_tensor(tmp1[:], rz[:, 0:1024], Hn[:], ALU.mult), "DV")
            ctx.wait(v, "DV")
            ctx.inc(v.tensor_tensor(tmp2[:], preN[:], tmp1[:], ALU.add), "DV")
        ctx.on("vector", f_gru1)

        def f_tanh(sc):
            ctx.wait(sc, "DV")
            ctx.inc(sc.activation(tmp1[:], tmp2[:], AF.Tanh), "AC")
        ctx.on("scalar", f_tanh)

        def f_gru2(v):
            ctx.wait(v, "AC")
            ctx.inc(v.tensor_tensor(tmp2[:], S0f[:], tmp1[:], ALU.subtract), "DV")
            ctx.wait(v, "DV")
            ctx.inc(v.tensor_tensor(tmp2[:], rz[:, 1024:2048], tmp2[:], ALU.mult),
                    "DV")
            ctx.wait(v, "DV")
            ctx.inc(v.tensor_tensor(Snew[:], tmp1[:], tmp2[:], ALU.add), "DV")
            ctx.wait(v, "DV")
            ctx.inc(v.tensor_copy(S0b[:], Snew[:]), "DV")
        ctx.on("vector", f_gru2)
        ctx.dvsync()
        dump_point("fSnew", lambda: Snew[:], 1024)

        slot_transpose(S0b[:], Snf)

        if STOP == "gru":
            fscope.close()
            return
        ctx.barrier()

        def f_qkv_ld(gp):
            ctx.inc(gp.dma_start(out=W[2][:, :, 0:256], in_=dram["fwq"].rearrange(
                "(dc p) o -> p dc o", p=128)), "LDW", 16)
            ctx.inc(gp.dma_start(out=W[2][:, :, 256:512], in_=dram["fwk"].rearrange(
                "(dc p) o -> p dc o", p=128)), "LDW", 16)
            ctx.inc(gp.dma_start(out=W[3][:], in_=dram["fwv"].rearrange(
                "(dc p) o -> p dc o", p=128)), "LDW", 16)
        ctx.on("gpsimd", f_qkv_ld)

        pingQ = Ping(2)
        for which, (dst, bt) in enumerate(((qf2, "fbqt"), (kf2, "fbkt"))):
            btab = fbqt if which == 0 else fbkt
            for oc in range(2):
                ps = psA if oc == 0 else psB

                def f_qm(t, which=which, oc=oc, ps=ps):
                    if which == 0 and oc == 0:
                        ctx.wait(t, "LDW")
                        ctx.wait(t, "DV")
                    pingQ.acquire(lambda s, v: t.wait_ge(ctx.sem[s], v), oc)
                    for dc in range(NDC):
                        mm = t.matmul(ps[:, 0:64],
                                      W[2][:, dc, which * 256 + oc * 128:
                                           which * 256 + (oc + 1) * 128],
                                      Snf[:, dc, :],
                                      start=(dc == 0), stop=(dc == NDC - 1))
                    ctx.inc(mm, "PE")
                ctx.on("tensor", f_qm)
                pe_q = ctx.c["PE"]

                def f_qe(v, dst=dst, btab=btab, oc=oc, ps=ps, pe_q=pe_q):
                    ctx.wait(v, "PE", pe_q)
                    ctx.inc(v.tensor_scalar(dst[:, oc, :], ps[:, 0:64],
                                            btab[:, oc:oc + 1], None, ALU.add),
                            "DV")
                ctx.on("vector", f_qe)
                pingQ.release(oc, "DV", ctx.c["DV"])

        dump_point("fq", lambda: qf2[:].rearrange("p a b -> p (a b)"), 128)
        dump_point("fk", lambda: kf2[:].rearrange("p a b -> p (a b)"), 128)

        def f_sc2(t):
            ctx.wait(t, "DV")
            for oc in range(2):
                mm = t.matmul(psD[0:64, 0:64], qf2[:, oc, :], kf2[:, oc, :],
                              start=(oc == 0), stop=(oc == 1))
            ctx.inc(mm, "PE")
        ctx.on("tensor", f_sc2)
        pe_s2 = ctx.c["PE"]

        def f_sm1(v, pe_s2=pe_s2):
            ctx.wait(v, "PE", pe_s2)
            ctx.inc(v.reduce_max(sst[:, 2:3].unsqueeze(2),
                                 psD[0:64, 0:64].unsqueeze(1), AX.X), "DV")
            ctx.wait(v, "DV")
            ctx.inc(v.tensor_scalar(sst[:, 3:4], sst[:, 2:3], -0.0625, None,
                                    ALU.mult), "DV")
        ctx.on("vector", f_sm1)

        def f_sex(sc):
            ctx.wait(sc, "DV")
            ctx.inc(sc.activation(tmp1[:, 0:64], psD[0:64, 0:64], AF.Exp,
                                  bias=sst[:, 3:4], scale=0.0625,
                                  accum_out=sst[:, 4:5]), "AC")
        ctx.on("scalar", f_sex)

        def f_sm2(v):
            ctx.wait(v, "AC")
            ctx.inc(v.reciprocal(sst[:, 5:6], sst[:, 4:5]), "DV")
            ctx.wait(v, "DV")
            ctx.inc(v.tensor_scalar(atb[:], tmp1[0:64, 0:64], sst[:, 5:6], None,
                                    ALU.mult), "DV")
        ctx.on("vector", f_sm2)
        dv_at = ctx.dvsync()

        def f_att(t, dv_at=dv_at):
            t.wait_ge(ctx.sem["DV"], dv_at)
            ctx.inc(t.transpose(psT[0:64, 0:64], atb[:], identb[0:64, 0:64]), "PE")
        ctx.on("tensor", f_att)
        pe_at = ctx.c["PE"]

        def f_atte(v, pe_at=pe_at):
            ctx.wait(v, "PE", pe_at)
            ctx.inc(v.tensor_copy(atT[:], psT[0:64, 0:64]), "DV")
        ctx.on("vector", f_atte)

        dump_point("fat", lambda: atb[:], 64)

        for half in range(2):
            ps = psA if half == 0 else psB

            def f_vm(t, half=half, ps=ps):
                if half == 0:
                    ctx.wait(t, "DV")
                for dc in range(NDC):
                    mm = t.matmul(ps[0:64, 0:512], Snf[:, dc, :],
                                  W[3][:, dc, half * 512:(half + 1) * 512],
                                  start=(dc == 0), stop=(dc == NDC - 1))
                ctx.inc(mm, "PE")
            ctx.on("tensor", f_vm)
            pe_v2 = ctx.c["PE"]

            def f_ve(v, half=half, ps=ps, pe_v2=pe_v2):
                ctx.wait(v, "PE", pe_v2)
                ctx.inc(v.tensor_copy(vb2[:, half * 512:(half + 1) * 512],
                                      ps[0:64, 0:512]), "DV")
            ctx.on("vector", f_ve)
        dv_vb = ctx.dvsync()

        def f_cm(t, dv_vb=dv_vb):
            t.wait_ge(ctx.sem["DV"], dv_vb)
            ctx.inc(t.matmul(psD[0:64, 0:512], atT[:], vb2[:, 0:512],
                             start=True, stop=True), "PE")
            ctx.inc(t.matmul(psE[0:64, 0:512], atT[:], vb2[:, 512:1024],
                             start=True, stop=True), "PE")
        ctx.on("tensor", f_cm)
        pe_cm = ctx.c["PE"]

        def f_bvc_ld(gp):
            ctx.inc(gp.dma_start(out=tmpA[:], in_=dram["fbvc"][:, :]), "LDG", 16)
        ctx.on("gpsimd", f_bvc_ld)

        def f_y(v, pe_cm=pe_cm):
            ctx.wait(v, "PE", pe_cm)
            ctx.wait(v, "LDG")
            ctx.inc(v.tensor_tensor(tmp2[:], Snew[:], tmpA[:], ALU.add), "DV")
            ctx.wait(v, "DV")
            ctx.inc(v.tensor_tensor(tmp1[:, 0:512], psD[0:64, 0:512],
                                    tmp2[:, 0:512], ALU.add), "DV")
            ctx.inc(v.tensor_tensor(tmp1[:, 512:1024], psE[0:64, 0:512],
                                    tmp2[:, 512:1024], ALU.add), "DV")
            ctx.wait(v, "DV")
            ctx.inc(v.reduce_sum(sst[:, 0:1].unsqueeze(2), tmp1[:].unsqueeze(1),
                                 AX.X), "DV")
            ctx.wait(v, "DV")
            ctx.inc(v.tensor_scalar(sst[:, 2:3], sst[:, 0:1], -1.0 / D, None,
                                    ALU.mult), "DV")
        ctx.on("vector", f_y)

        dump_point("fy", lambda: tmp1[:], 1024)

        def f_ysq(v):
            ctx.wait(v, "DV")
            ctx.inc(v.tensor_tensor(tmpB[:], tmp1[:], tmp1[:], ALU.mult), "DV")
            ctx.wait(v, "DV")
            ctx.inc(v.reduce_sum(sst[:, 1:2].unsqueeze(2),
                                 tmpB[:].unsqueeze(1), AX.X), "DV")
        ctx.on("vector", f_ysq)

        def f_yst(v):
            ctx.wait(v, "DV")
            ctx.inc(v.tensor_scalar(sst[:, 1:2], sst[:, 1:2], 1.0 / D, None,
                                    ALU.mult), "DV")
            ctx.inc(v.tensor_tensor(sst[:, 6:7], sst[:, 2:3], sst[:, 2:3],
                                    ALU.mult), "DV")
            ctx.wait(v, "DV")
            ctx.inc(v.tensor_tensor(sst[:, 1:2], sst[:, 1:2], sst[:, 6:7],
                                    ALU.subtract), "DV")
        ctx.on("vector", f_yst)

        def f_ysqrt(sc):
            ctx.wait(sc, "DV")
            ctx.inc(sc.activation(sst[:, 6:7], sst[:, 1:2], AF.Ln,
                                  bias=cvals[0:64, 0:1]), "AC")
            ctx.wait(sc, "AC")
            ctx.inc(sc.activation(sst[:, 7:8], sst[:, 6:7], AF.Exp,
                                  scale=-0.5), "AC")
        ctx.on("scalar", f_ysqrt)

        def f_lnld(gp):
            ctx.wait(gp, "DV")
            ctx.inc(gp.dma_start(out=tmpA[:], in_=dram["flngc"][:, :]), "LDG", 16)
            ctx.inc(gp.dma_start(out=tmpB[:], in_=dram["flnbc"][:, :]), "LDG", 16)
        ctx.on("gpsimd", f_lnld)

        def f_yn(v):
            ctx.wait(v, "AC")
            ctx.wait(v, "LDG")
            ctx.inc(v.tensor_scalar(tmp2[:], tmp1[:], sst[:, 2:3], sst[:, 7:8],
                                    ALU.add, ALU.mult), "DV")
            ctx.wait(v, "DV")
            ctx.inc(v.tensor_tensor(tmp2[:], tmp2[:], tmpA[:], ALU.mult), "DV")
            ctx.wait(v, "DV")
            ctx.inc(v.tensor_tensor(ynb[:], tmp2[:], tmpB[:], ALU.add), "DV")
        ctx.on("vector", f_yn)
        ctx.dvsync()

        dump_point("fyn", lambda: ynb[:], 1024)
        slot_transpose(ynb[:], ynf)

        ctx.barrier()

        def f_wvp_ld(gp):
            ctx.wait(gp, "DV")
            ctx.inc(gp.dma_start(out=W[2][:], in_=dram["fwvp"].rearrange(
                "(dc p) o -> p dc o", p=128)), "LDW", 16)
            ctx.inc(gp.dma_start(out=W[3][:], in_=dram["fwo"].rearrange(
                "(dc p) o -> p dc o", p=128)), "LDW", 16)
            ctx.inc(gp.dma_start(out=tmpA[:], in_=dram["fboc"][:, :]), "LDG", 16)
        ctx.on("gpsimd", f_wvp_ld)

        pingSV = Ping(2)
        for oc in range(NDC):
            ps = psA if oc % 2 == 0 else psB

            def f_svm(t, oc=oc, ps=ps):
                if oc == 0:
                    ctx.wait(t, "LDW")
                    ctx.wait(t, "DV")
                pingSV.acquire(lambda s, v: t.wait_ge(ctx.sem[s], v), oc)
                for dc in range(NDC):
                    mm = t.matmul(ps[:, 0:64], W[2][:, dc, oc * 128:(oc + 1) * 128],
                                  ynf[:, dc, :],
                                  start=(dc == 0), stop=(dc == NDC - 1))
                ctx.inc(mm, "PE")
            ctx.on("tensor", f_svm)
            pe_sv = ctx.c["PE"]

            def f_sve(v, oc=oc, ps=ps, pe_sv=pe_sv):
                ctx.wait(v, "PE", pe_sv)
                ctx.inc(v.tensor_scalar(SVf[:, oc, :], ps[:, 0:64],
                                        fbvpt[:, oc:oc + 1], None, ALU.add),
                        "DV")
            ctx.on("vector", f_sve)
            pingSV.release(oc, "DV", ctx.c["DV"])

        for half in range(2):
            pstgt = psD if half == 0 else psE

            def f_svwm(t, half=half, pstgt=pstgt):
                if half == 0:
                    ctx.wait(t, "DV")
                for dc in range(NDC):
                    mm = t.matmul(pstgt[0:64, 0:512], SVf[:, dc, :],
                                  W[3][:, dc, half * 512:(half + 1) * 512],
                                  start=(dc == 0), stop=(dc == NDC - 1))
                ctx.inc(mm, "PE")
            ctx.on("tensor", f_svwm)
            pe_svw = ctx.c["PE"]

            def f_svwe(v, half=half, pstgt=pstgt, pe_svw=pe_svw):
                ctx.wait(v, "PE", pe_svw)
                ctx.wait(v, "LDG")
                ctx.inc(v.tensor_tensor(svwb[:, half * 512:(half + 1) * 512],
                                        pstgt[0:64, 0:512],
                                        tmpA[:, half * 512:(half + 1) * 512],
                                        ALU.add), "DV")
            ctx.on("vector", f_svwe)
        dump_point("fsvw", lambda: svwb[:], 1024)

        def f_svw2(gp):
            ctx.wait(gp, "DV")
            ctx.inc(gp.dma_start(out=SVW2[0:64, :], in_=svwb[:]), "LDG", 16)
            ctx.inc(gp.dma_start(out=SVW2[64:128, :], in_=svwb[:]), "LDG", 16)
        ctx.on("gpsimd", f_svw2)

        if STOP == "mid":
            fscope.close()
            return
        p2prev = {}
        for pb in range(NBLK):
            def f_x2(sync, pb=pb, xdv=p2prev.get("re2_dv")):
                if xdv is not None:
                    sync.wait_ge(ctx.sem["DV"], xdv)
                else:
                    ctx.wait(sync, "DV")
                ctx.inc(sync.dma_start(
                    out=xin[:, 0:BT, :],
                    in_=x_in[pb * BLK:(pb + 1) * BLK, :].rearrange(
                        "(tt p) d -> p tt d", p=128)), "LDX", 16)
            ctx.on("sync", f_x2)

            pingT2 = Ping(2)
            for tt in range(BT):
                half = tt % 2

                def f_atp(t, pb=pb, tt=tt, half=half):
                    if tt == 0:
                        ctx.wait(t, "DV")
                    pingT2.acquire(lambda s, v: t.wait_ge(ctx.sem[s], v), half)
                    ctx.inc(t.transpose(psT[0:64, half * 1024:half * 1024 + 128],
                                        alpha[:, pb * BT + tt, :],
                                        identb[:]), "PE")
                ctx.on("tensor", f_atp)
                pe_t2 = ctx.c["PE"]

                def f_ate(v, tt=tt, half=half, pe_t2=pe_t2):
                    ctx.wait(v, "PE", pe_t2)
                    ctx.inc(v.tensor_copy(aTb[:, tt * 128:(tt + 1) * 128],
                                          psT[0:64, half * 1024:half * 1024 + 128]),
                            "DV")
                ctx.on("vector", f_ate)
                pingT2.release(half, "DV", ctx.c["DV"])

            pingC2 = Ping(1)
            for tt in range(BT):
                def f_om2(t, tt=tt):
                    if tt == 0:
                        ctx.wait(t, "DV")
                        ctx.wait(t, "LDG")
                    pingC2.acquire(lambda s, v: t.wait_ge(ctx.sem[s], v), 0)
                    for half in range(2):
                        ctx.inc(t.matmul(psC[:, half * 512:(half + 1) * 512],
                                         aTb[0:64, tt * 128:(tt + 1) * 128],
                                         SVW2[0:64, half * 512:(half + 1) * 512],
                                         start=True, stop=True), "PE")
                ctx.on("tensor", f_om2)
                pe_o2 = ctx.c["PE"]

                def f_re2(v, tt=tt, pe_o2=pe_o2):
                    ctx.wait(v, "PE", pe_o2)
                    ctx.wait(v, "LDX")
                    if tt == 0:
                        ctx.wait(v, "STX")
                    ctx.inc(v.tensor_tensor(xout[:, tt, :], psC[:, 0:1024],
                                            xin[:, tt, :], ALU.add), "DV")
                ctx.on("vector", f_re2)
                pingC2.release(0, "DV", ctx.c["DV"])
            p2prev["re2_dv"] = ctx.c["DV"]

            def f_st2(gp, pb=pb):
                ctx.wait(gp, "DV")
                ctx.inc(gp.dma_start(
                    out=x_out[pb * BLK:(pb + 1) * BLK, :].rearrange(
                        "(tt p) d -> p tt d", p=128),
                    in_=xout[:, 0:BT, :]), "STX", 16)
            ctx.on("gpsimd", f_st2)
        fscope.close()

    def emit_attn_b(x_in, x_out):
        bscope = contextlib.ExitStack()
        bsb = lambda n, shp, dt: ctx.sb("b_" + n, shp, dt, stack=bscope)
        mw1sb = bsb("mw1", (128, 8, 288), BF16)
        mw2A = bsb("mw2A", (97, 1024), BF16)
        mw2B = bsb("mw2B", (97, 1024), BF16)
        mw2C = bsb("mw2C", (33, 1024), BF16)
        gw1sb = bsb("gw1", (128, 8, 64), BF16)
        gw2sb = bsb("gw2", (64, 1024), BF16)
        dw1sb = bsb("dw1", (128, 8, 64), BF16)
        dw2sb = bsb("dw2", (64, 1024), BF16)
        u16b = bsb("u16", (128, 8, 16), BF16)
        matab = {}
        for nm in ("maa_x", "maa_w", "maa_k", "maa_v", "maa_r", "maa_g"):
            matab[nm] = bsb(nm, (128, 8), F32)
        tdtab = bsb("tdt", (128, 8), F32)
        lngb = bsb("lngb", (128, 1024), BF16)
        lnbb = bsb("lnbb", (128, 1024), BF16)
        prevc = bsb("prevc", (128, 8), BF16)
        xx = bsb("xx", (128, 8, 512), BF16)
        mixb = bsb("mixb", (128, 8, 512), BF16)
        ubuf = bsb("ubuf", (128, 1024), BF16)
        baseb = ubuf[:, 0:512]
        mscr = ubuf[:, 512:1024]
        lr1 = bsb("lr1", (97, 512), BF16)
        lr2 = bsb("lr2", (97, 512), BF16)
        lrC = bsb("lrC", (33, 512), BF16)
        rf = bsb("rf", (128, 8, 512), BF16)
        kf = bsb("kf", (128, 8, 512), BF16)
        vt = bsb("vt", (128, 4, 1024), BF16)
        gt = bsb("gt", (128, 4, 1024), BF16)
        din = bsb("din", (64, 512), BF16)
        gin = bsb("gin", (64, 512), BF16)
        cib = bsb("cib", (128, 8, 130), F32)

        rbase = xx[:, :, 0:128]
        kbase = xx[:, :, 128:256]
        khf = xx[:, :, 256:384]
        rkb_v = xx[:, :, 384:512]
        kht = bsb("kht", (128, 1024), BF16)
        ebuf = kht.rearrange_view if False else kht
        ebuf_ap = lambda: kht[:].rearrange("p (a b) -> p a b", b=128)
        rkb = rkb_v
        ALb = bsb("ALb", (128, 8, 2), F32)
        dsb = bsb("dsb", (128, 16), F32)
        Pb = bsb("Pb", (128, 16, 128), BF16)
        S_sb = bsb("S_sb", (128, 8, 64), F32)
        S_b = bsb("S_b", (128, 8, 64), BF16)
        yscr = bsb("yscr", (128, 1024), BF16)
        ysc2v = xout[:, 3, :]
        ybuf = xout[:, 2, :]
        gnst = bsb("gnst", (128, 64), F32)
        ygb = ubuf

        ctx.barrier()

        def f_ldb(gp):
            for dst, src in ((W[0], "bWr"), (W[1], "bWk"), (W[2], "bWv"),
                             (W[3], "bWo")):
                ctx.inc(gp.dma_start(
                    out=dst[:],
                    in_=dram[src].rearrange("(dc p) o -> p dc o", p=128)),
                    "LDW", 16)
            ctx.inc(gp.dma_start(out=mw1sb[:], in_=dram["bmw1"].rearrange(
                "(dc p) o -> p dc o", p=128)), "LDW", 16)
            ctx.inc(gp.dma_start(out=mw2A[:], in_=dram["bmw2a"][:, :]), "LDW", 16)
            ctx.inc(gp.dma_start(out=mw2B[:], in_=dram["bmw2b"][:, :]), "LDW", 16)
            ctx.inc(gp.dma_start(out=mw2C[:], in_=dram["bmw2c"][:, :]), "LDW", 16)
            ctx.inc(gp.dma_start(out=gw1sb[:], in_=dram["bgw1"].rearrange(
                "(dc p) o -> p dc o", p=128)), "LDW", 16)
            ctx.inc(gp.dma_start(out=gw2sb[:], in_=dram["bgw2"][:, :]), "LDW", 16)
            ctx.inc(gp.dma_start(out=dw1sb[:], in_=dram["bdw1"].rearrange(
                "(dc p) o -> p dc o", p=128)), "LDW", 16)
            ctx.inc(gp.dma_start(out=dw2sb[:], in_=dram["bdw2"][:, :]), "LDW", 16)
            ctx.inc(gp.dma_start(out=u16b[:], in_=dram["bU16"].rearrange(
                "a p h -> p a h")), "LDW", 16)
            for nm in matab:
                ctx.inc(gp.dma_start(out=matab[nm][:], in_=dram["b" + nm][:, :]),
                        "LDW", 16)
            ctx.inc(gp.dma_start(out=tdtab[:], in_=dram["btd"][:, :]), "LDW", 16)
            ctx.inc(gp.dma_start(out=lngb[:], in_=dram["blngc"][:, :]), "LDW", 16)
            ctx.inc(gp.dma_start(out=lnbb[:], in_=dram["blnbc"][:, :]), "LDW", 16)
        ctx.on("gpsimd", f_ldb)

        def f_init(v):
            v.memset(lr1[32:64, :], 1.0)
            v.memset(lr1[64:97, :], 1.0)
            v.memset(lr2[32:64, :], 1.0)
            v.memset(lr2[64:97, :], 1.0)
            v.memset(lrC[32:33, :], 1.0)
            v.memset(prevc[:], 0.0)
            v.memset(S_sb[:], 0.0)
            v.memset(S_b[:], 0.0)
            v.memset(cib[:, :, 0:1], 0.0)
            ctx.inc(v.engine_nop(), "DV")
        ctx.on("vector", f_init)

        pingY = Ping(1)
        pingD = Ping(1)
        pingW = Ping(1)
        pingE = Ping(1)

        bprev = {}
        for bb in range(NBLK):
            ln_block(x_in, bb * BLK, g={
                "xin_dv": bprev.get("res_dv"),
                "hfeat_dv": bprev.get("hfeat_dv"),
                "hfeat_gp": bprev.get("hfeat_gp"),
                "psT_ac": bprev.get("psT_ac"),
            })
            if STOP == "bln" and bb == 0:
                bscope.close()
                return

            def f_xx(gp, ha=ctx.hfeat_ac, bpe=bprev.get("blk_pe")):
                gp.wait_ge(ctx.sem["AC"], ha)
                if bpe is not None:
                    gp.wait_ge(ctx.sem["PE"], bpe)
                ctx.wait(gp, "DV")
                ctx.inc(gp.tensor_tensor(xx[:, :, 0:1], prevc[:].unsqueeze(2),
                                         hfeat[:, :, 0:1], ALU.subtract), "GP")
                ctx.inc(gp.tensor_tensor(xx[:, :, 1:512], hfeat[:, :, 0:511],
                                         hfeat[:, :, 1:512], ALU.subtract), "GP")
                ctx.wait(gp, "GP")
                ctx.inc(gp.tensor_copy(prevc[:].unsqueeze(2),
                                       hfeat[:, :, 511:512]), "GP")
            ctx.on("gpsimd", f_xx)

            gp_xx = ctx.gpsync()
            bprev["hfeat_gp"] = gp_xx

            def f_xxx(v, gp_xx=gp_xx):
                ctx.wait(v, "LDW")
                v.wait_ge(ctx.sem["GP"], gp_xx)
                for oc in range(NDC):
                    ctx.inc(v.scalar_tensor_tensor(
                        mixb[:, oc, :], xx[:, oc, :],
                        matab["maa_x"][:, oc:oc + 1], hfeat[:, oc, :],
                        ALU.mult, ALU.add), "DV")
            ctx.on("vector", f_xxx)
            gp_xxx = ctx.dvsync()

            if bb == 0:
                dump_point("bxxe", lambda: xx[:].rearrange("p a b -> p (a b)"), 4096)
                dump_point("bxxx", lambda: mixb[:].rearrange("p a b -> p (a b)"), 4096)

            def f_lora(t, gp_xxx=gp_xxx):
                t.wait_ge(ctx.sem["DV"], gp_xxx)
                pingY.acquire(lambda s, v: t.wait_ge(ctx.sem[s], v), 0)
                pingE.acquire(lambda s, v: t.wait_ge(ctx.sem[s], v), 0)
                for half, psx in ((0, psA), (1, psB)):
                    for dc in range(NDC):
                        mm = t.matmul(psx[:, 0:512],
                                      mw1sb[:, dc, half * 128:(half + 1) * 128],
                                      mixb[:, dc, :], start=(dc == 0),
                                      stop=(dc == NDC - 1))
                    ctx.inc(mm, "PE")
                for dc in range(NDC):
                    mm = t.matmul(psE[0:32, 0:512], mw1sb[:, dc, 256:288],
                                  mixb[:, dc, :], start=(dc == 0),
                                  stop=(dc == NDC - 1))
                ctx.inc(mm, "PE")
            ctx.on("tensor", f_lora)
            pe_lora = ctx.c["PE"]

            def f_lorae(sc, pe_lora=pe_lora):
                ctx.wait(sc, "PE", pe_lora)
                ctx.inc(sc.activation(lr1[0:32, :], psA[0:32, 0:512],
                                      AF.Tanh), "AC")
                ctx.inc(sc.activation(lr1[64:96, :], psA[64:96, 0:512],
                                      AF.Tanh), "AC")
                ctx.inc(sc.activation(lr2[0:32, :], psB[0:32, 0:512],
                                      AF.Tanh), "AC")
                ctx.inc(sc.activation(lr2[64:96, :], psB[64:96, 0:512],
                                      AF.Tanh), "AC")
                ctx.inc(sc.activation(lrC[0:32, :], psE[0:32, 0:512],
                                      AF.Tanh), "AC")
            ctx.on("scalar", f_lorae)
            if bb == 0:
                dump_point("blora", lambda: lr1[0:64, :], 512)
            ac_lora = ctx.c["AC"]
            pingE.release(0, "AC", ac_lora)
            pingY.release(0, "AC", ac_lora)

            specs = [("maa_w", 0), ("maa_k", 1), ("maa_v", 2), ("maa_r", 3),
                     ("maa_g", 4)]
            mixsrc = [(mw2A, lr1, 0), (mw2A, lr1, 64), (mw2B, lr2, 0),
                      (mw2B, lr2, 64), (mw2C, lrC, 0)]
            for nm, fi in specs:
                pingM = Ping(2)
                prev_mix2 = [None]
                W2x, lrx, pb = mixsrc[fi]
                for oc in range(NDC):
                    ps = psA if oc % 2 == 0 else psB

                    def f_mm(t, fi=fi, oc=oc, ps=ps, W2x=W2x, lrx=lrx, pb=pb):
                        if fi == 0 and oc == 0:
                            ctx.wait(t, "AC", ac_lora)
                        pingM.acquire(lambda s, v: t.wait_ge(ctx.sem[s], v), oc)
                        ctx.inc(t.matmul(
                            ps[:, 0:512],
                            W2x[pb:pb + 33, oc * 128:(oc + 1) * 128],
                            lrx[pb:pb + 33, 0:512],
                            start=True, stop=True), "PE")
                    ctx.on("tensor", f_mm)
                    pe_m = ctx.c["PE"]

                    def f_mix1(v, oc=oc, ps=ps, pe_m=pe_m):
                        ctx.wait(v, "PE", pe_m)
                        if prev_mix2[0] is not None:
                            ctx.wait(v, "DV", prev_mix2[0])
                        ctx.inc(v.tensor_tensor(mscr, ps[:, 0:512],
                                                xx[:, oc, :], ALU.mult), "DV")
                    ctx.on("vector", f_mix1)
                    dv_m = ctx.c["DV"]
                    pingM.release(oc, "DV", dv_m)

                    def f_mix2(v, oc=oc, dv_m=dv_m):
                        ctx.wait(v, "DV", dv_m)
                        ctx.inc(v.tensor_tensor(mixb[:, oc, :], mscr,
                                                hfeat[:, oc, :], ALU.add), "DV")
                    ctx.on("vector", f_mix2)
                    prev_mix2[0] = ctx.c["DV"]
                dv_mix = ctx.dvsync()

                if bb == 0:
                    dump_point("bmix_" + nm,
                               lambda: mixb[:].rearrange("p a b -> p (a b)"), 4096)
                if nm == "maa_w":
                    def f_dm(t, dvr=dv_mix):
                        t.wait_ge(ctx.sem["DV"], dvr)
                        pingD.acquire(lambda s, v: t.wait_ge(ctx.sem[s], v), 0)
                        for dc in range(NDC):
                            mm = t.matmul(psD[0:64, 0:512], dw1sb[:, dc, :],
                                          mixb[:, dc, :], start=(dc == 0),
                                          stop=(dc == NDC - 1))
                        ctx.inc(mm, "PE")
                    ctx.on("tensor", f_dm)
                    pe_d = ctx.c["PE"]

                    def f_de(sc, pe_d=pe_d):
                        ctx.wait(sc, "PE", pe_d)
                        ctx.inc(sc.activation(din[:], psD[0:64, 0:512], AF.Tanh),
                                "AC")
                    ctx.on("scalar", f_de)
                    pingD.release(0, "AC", ctx.c["AC"])
                elif nm in ("maa_k", "maa_r"):
                    wsl = W[1] if nm == "maa_k" else W[0]
                    dst = kf if nm == "maa_k" else rf
                    pingP = Ping(2)
                    for oc in range(NDC):
                        off = (oc % 2) * 512

                        def f_km(t, oc=oc, off=off, wsl=wsl, dvr=dv_mix):
                            if oc == 0:
                                t.wait_ge(ctx.sem["DV"],
                                          dvr + int(os.environ.get("MK_DVPAD", "0")))
                                pingW.acquire(
                                    lambda s, v: t.wait_ge(ctx.sem[s], v), 0)
                            pingP.acquire(lambda s, v: t.wait_ge(ctx.sem[s], v), oc)
                            for dc in range(NDC):
                                mm = t.matmul(psC[:, off:off + 512],
                                              wsl[:, dc, oc * 128:(oc + 1) * 128],
                                              mixb[:, dc, :], start=(dc == 0),
                                              stop=(dc == NDC - 1))
                            ctx.inc(mm, "PE")
                        ctx.on("tensor", f_km)
                        pe_k = ctx.c["PE"]

                        def f_ke(v, oc=oc, off=off, dst=dst, pe_k=pe_k):
                            ctx.wait(v, "PE", pe_k)
                            ctx.inc(v.tensor_copy(dst[:, oc, :],
                                                  psC[:, off:off + 512]), "DV")
                        ctx.on("vector", f_ke)
                        pingP.release(oc, "DV", ctx.c["DV"])
                    pingW.release(0, "DV", ctx.c["DV"])
                elif nm == "maa_v":
                    pingP = Ping(1)
                    for tt in range(BT):
                        def f_vm(t, tt=tt, dvr=dv_mix):
                            if tt == 0:
                                t.wait_ge(ctx.sem["DV"], dvr)
                                pingW.acquire(
                                    lambda s, v: t.wait_ge(ctx.sem[s], v), 0)
                            pingP.acquire(lambda s, v: t.wait_ge(ctx.sem[s], v), 0)
                            for half in range(2):
                                for dc in range(NDC):
                                    mm = t.matmul(
                                        psC[:, half * 512:(half + 1) * 512],
                                        mixb[:, dc, tt * 128:(tt + 1) * 128],
                                        W[2][:, dc, half * 512:(half + 1) * 512],
                                        start=(dc == 0), stop=(dc == NDC - 1))
                            ctx.inc(mm, "PE")
                        ctx.on("tensor", f_vm)
                        pe_v = ctx.c["PE"]

                        def f_ve(v, tt=tt, pe_v=pe_v):
                            ctx.wait(v, "PE", pe_v)
                            ctx.inc(v.tensor_copy(vt[:, tt, :], psC[:, 0:1024]),
                                    "DV")
                        ctx.on("vector", f_ve)
                        pingP.release(0, "DV", ctx.c["DV"])
                    pingW.release(0, "DV", ctx.c["DV"])
                elif nm == "maa_g":
                    def f_gm(t, dvr=dv_mix):
                        t.wait_ge(ctx.sem["DV"], dvr)
                        pingD.acquire(lambda s, v: t.wait_ge(ctx.sem[s], v), 0)
                        for dc in range(NDC):
                            mm = t.matmul(psD[0:64, 0:512], gw1sb[:, dc, :],
                                          mixb[:, dc, :], start=(dc == 0),
                                          stop=(dc == NDC - 1))
                        ctx.inc(mm, "PE")
                    ctx.on("tensor", f_gm)
                    pe_g = ctx.c["PE"]

                    def f_gne(sc, pe_g=pe_g):
                        ctx.wait(sc, "PE", pe_g)
                        ctx.inc(sc.activation(gin[:], psD[0:64, 0:512], AF.Tanh),
                                "AC")
                    ctx.on("scalar", f_gne)
                    pingD.release(0, "AC", ctx.c["AC"])

                    ac_gin = ctx.c["AC"]
                    pingP = Ping(1)
                    for tt in range(BT):
                        def f_g2(t, tt=tt, ac_gin=ac_gin):
                            if tt == 0:
                                ctx.wait(t, "AC", ac_gin)
                                pingW.acquire(
                                    lambda s, v: t.wait_ge(ctx.sem[s], v), 0)
                            pingP.acquire(lambda s, v: t.wait_ge(ctx.sem[s], v), 0)
                            for half in range(2):
                                ctx.inc(t.matmul(
                                    psC[:, half * 512:(half + 1) * 512],
                                    gin[0:64, tt * 128:(tt + 1) * 128],
                                    gw2sb[0:64, half * 512:(half + 1) * 512],
                                    start=True, stop=True), "PE")
                        ctx.on("tensor", f_g2)
                        pe_g2 = ctx.c["PE"]

                        def f_g2e(v, tt=tt, pe_g2=pe_g2):
                            ctx.wait(v, "PE", pe_g2)
                            ctx.inc(v.tensor_copy(gt[:, tt, :], psC[:, 0:1024]),
                                    "DV")
                        ctx.on("vector", f_g2e)
                        pingP.release(0, "DV", ctx.c["DV"])
                    pingW.release(0, "DV", ctx.c["DV"])

            if bb == 0:
                dump_point("brfE", lambda: rf[:].rearrange("p a b -> p (a b)"), 4096)
                dump_point("bvtE", lambda: vt[:].rearrange("p a b -> p (a b)"), 4096)
            if STOP == "bmix" and bb == 0:
                bscope.close()
                return
            M = {"kht_ac": ctx.hfeat_ac, "yg_ac": ctx.hfeat_ac}

            def emit_posw(i):
                def f_pw(t, i=i):
                    if i == 0:
                        ctx.wait(t, "AC")
                    pingW.acquire(lambda s, v: t.wait_ge(ctx.sem[s], v), 0)
                    for oc in range(NDC):
                        ctx.inc(t.matmul(
                            psC[:, oc * 128:(oc + 1) * 128],
                            dw2sb[0:64, oc * 128:(oc + 1) * 128],
                            din[0:64, i * 128:(i + 1) * 128],
                            start=True, stop=True), "PE")
                ctx.on("tensor", f_pw)
                pe_pw = ctx.c["PE"]

                def f_pwe(sc, i=i, pe_pw=pe_pw):
                    ctx.wait(sc, "PE", pe_pw)
                    if M.get((i - 1, "devac")) is not None:
                        sc.wait_ge(ctx.sem["DV"], M[(i - 1, "devac")])
                    for oc in range(NDC):
                        dst = (psD[:, oc * 128:(oc + 1) * 128] if oc < 4
                               else psC[:, oc * 128:(oc + 1) * 128])
                        ctx.inc(sc.activation(dst,
                                              psC[:, oc * 128:(oc + 1) * 128],
                                              AF.Exp,
                                              bias=tdtab[:, oc:oc + 1]), "AC")
                ctx.on("scalar", f_pwe)
                M[(i, "poswac")] = ctx.c["AC"]

            def emit_scans(i):
                def f_scan(v, i=i):
                    ctx.wait(v, "AC")
                    for oc in range(NDC):
                        src = (psD[:, oc * 128:(oc + 1) * 128] if oc < 4
                               else psC[:, oc * 128:(oc + 1) * 128])
                        ctx.inc(v.tensor_tensor_scan(
                            cib[:, oc, 1:129], onesf[:, 0:128], src,
                            0.0, ALU.mult, ALU.add), "DV")
                    ctx.wait(v, "DV")
                    ctx.inc(v.tensor_scalar(
                        ALb[:, :, i % 2], cib[:, :, 128], -1.0,
                        None, ALU.mult), "DV")
                ctx.on("vector", f_scan)
                M[(i, "scandv")] = ctx.dvsync()

            def emit_mexp(i):
                resR = Pb[:, 0:8, :]
                resK = Pb[:, 8:16, :]

                def f_mexp(sc2, i=i):
                    ctx.wait(sc2, "DV", M[(i, "scandv")])
                    pe_prev = M.get((i - 1, "pestate"))
                    if pe_prev is not None:
                        sc2.wait_ge(ctx.sem["PE"], pe_prev)
                    ctx.inc(sc2.activation(resR, cib[:, :, 0:128],
                                           AF.Exp, scale=-1.0), "AC")
                    ctx.inc(sc2.activation(resK, cib[:, :, 1:129],
                                           AF.Exp), "AC")
                    for oc in range(NDC):
                        ctx.inc(sc2.activation(
                            kht[:, oc * 128:(oc + 1) * 128],
                            cib[:, oc, 1:129], AF.Exp,
                            bias=ALb[:, oc, i % 2:i % 2 + 1]), "AC")
                    ctx.wait(sc2, "AC")
                    ctx.inc(sc2.activation(ALb[:, :, i % 2], ALb[:, :, i % 2],
                                           AF.Exp), "AC")
                ctx.on("scalar", f_mexp)
                M[(i, "mexpac")] = ctx.c["AC"]

            def emit_mmul(i):
                rsl = rf[:, :, i * 128:(i + 1) * 128]
                ksl = kf[:, :, i * 128:(i + 1) * 128]

                def f_mmul(v, i=i, rsl=rsl, ksl=ksl):
                    ctx.wait(v, "AC", M[(i, "mexpac")])
                    ctx.inc(v.tensor_tensor(rbase, rsl, Pb[:, 0:8, :],
                                            ALU.mult), "DV")
                    ctx.inc(v.scalar_tensor_tensor(
                        kbase, Pb[:, 8:16, :], 3.73324200e32, ksl,
                        ALU.min, ALU.mult), "DV")
                    ctx.inc(v.tensor_tensor(
                        khf, ksl,
                        kht[:].rearrange("p (a b) -> p a b", b=128),
                        ALU.mult), "DV")
                    ctx.inc(v.tensor_tensor(rkb, rsl, ksl, ALU.mult), "DV")
                ctx.on("vector", f_mmul)
                M[(i, "modsdv")] = ctx.dvsync()

            def emit_kht(i):
                prev_ac = M.get("kht_ac")

                def f_kt(t, i=i, prev_ac=prev_ac):
                    t.wait_ge(ctx.sem["DV"], M[(i, "modsdv")])
                    if prev_ac is not None:
                        t.wait_ge(ctx.sem["AC"], prev_ac)
                    for oc in range(NDC):
                        ctx.inc(t.transpose(psT[:, oc * 128:(oc + 1) * 128],
                                            khf[:, oc, :], identb[:]), "PE")
                ctx.on("tensor", f_kt)
                pe_kt = ctx.c["PE"]

                def f_kte(sc, pe_kt=pe_kt):
                    ctx.wait(sc, "PE", pe_kt)
                    ctx.inc(sc.activation(kht[:, 0:1024], psT[:, 0:1024],
                                          AF.Identity), "AC")
                ctx.on("scalar", f_kte)
                M["kht_ac"] = M[(i, "khtac")] = ctx.c["AC"]

            def emit_d(i):
                def f_dmm(t, i=i):
                    t.wait_ge(ctx.sem["DV"], M[(i, "modsdv")])
                    for oc in range(NDC):
                        mm = t.matmul(psD[:, 0:16], rkb[:, oc, :],
                                      u16b[:, oc, :],
                                      start=(oc == 0), stop=(oc == NDC - 1),
                                      skip_group_check=True)
                    ctx.inc(mm, "PE")
                ctx.on("tensor", f_dmm)
                pe_d = ctx.c["PE"]

                def f_dev(v, pe_d=pe_d):
                    ctx.wait(v, "PE", pe_d)
                    ctx.inc(v.tensor_copy(dsb[:], psD[:, 0:16]), "DV")
                ctx.on("vector", f_dev)
                M[(i, "devac")] = ctx.c["DV"]
                pingD.release(0, "DV", ctx.c["DV"])

            def emit_w(i):
                pingWp = Ping(2)
                for j in range(8):
                    if j % 2 == 0:
                        pse, pso = psC[:, 0:128], psC[:, 512:640]
                    else:
                        pse, pso = psA[:, 0:128], psB[:, 0:128]

                    def f_wm(t, i=i, j=j, pse=pse, pso=pso):
                        if j == 0:
                            t.wait_ge(ctx.sem["DV"], M[(i, "modsdv")])
                        if j == 1:
                            pingY.acquire(
                                lambda s, v: t.wait_ge(ctx.sem[s], v), 0)
                        pingWp.acquire(lambda s, v: t.wait_ge(ctx.sem[s], v), j)
                        for par, ps in ((0, pse), (1, pso)):
                            hs = 64 * par
                            ctx.inc(t.matmul(
                                ps, kbase[hs:hs + 64, j, :],
                                rbase[hs:hs + 64, j, :],
                                start=True, stop=True, skip_group_check=True),
                                "PE")
                    ctx.on("tensor", f_wm)
                    pe_wm = ctx.c["PE"]

                    def f_pev(v, j=j, pse=pse, pso=pso, pe_wm=pe_wm):
                        ctx.wait(v, "PE", pe_wm)
                        ctx.inc(v.tensor_tensor(
                            Pb[:, 2 * j, :], pse,
                            maskB[:, :], ALU.mult), "DV")
                        ctx.inc(v.tensor_tensor(
                            Pb[:, 2 * j + 1, :], pso,
                            maskB[:, :], ALU.mult), "DV")
                    ctx.on("vector", f_pev)
                    pingWp.release(j, "DV", ctx.c["DV"])
                pingW.release(0, "DV", ctx.c["DV"])
                M[(i, "pevdv")] = ctx.dvsync()

            def emit_state_mm(i):
                def f_ym(t, i=i):
                    t.wait_ge(ctx.sem["DV"], M[(i, "pevdv")])
                    for h in range(NH):
                        hs = 64 * (h % 2)
                        psY = psA if h % 2 == 0 else psB
                        col = (h // 2) * 64
                        t.matmul(psY[:, col:col + 64],
                                 rbase[hs:hs + 64, h // 2, :],
                                 S_b[hs:hs + 64, h // 2, :],
                                 start=True, stop=False, skip_group_check=True)
                        mm = t.matmul(
                            psY[:, col:col + 64],
                            Pb[:, h, :],
                            vt[:, i, h * 64:(h + 1) * 64],
                            start=False, stop=True, skip_group_check=True)
                    ctx.inc(mm, "PE")

                    ctx.wait(t, "AC", M[(i, "khtac")])
                    pingE.acquire(lambda s, v: t.wait_ge(ctx.sem[s], v), 0)
                    for h in range(NH):
                        hs = 64 * (h % 2)
                        ctx.inc(t.matmul(
                            psE[hs:hs + 64, (h // 2) * 64:(h // 2) * 64 + 64],
                            kht[:, h * 64:(h + 1) * 64],
                            vt[:, i, h * 64:(h + 1) * 64],
                            start=True, stop=True, skip_group_check=True), "PE")
                ctx.on("tensor", f_ym)
                M[(i, "pestate")] = ctx.c["PE"]

            def emit_su_ye(i):
                pe_st = M[(i, "pestate")]

                def f_su(v, i=i, pe_st=pe_st):
                    ctx.wait(v, "PE", pe_st)
                    ctx.wait(v, "AC", M[(i, "mexpac")])
                    ctx.inc(v.tensor_tensor(
                        S_sb[:], S_sb[:],
                        ALb[:, :, i % 2:i % 2 + 1].broadcast_to([128, 8, 64]),
                        ALU.mult), "DV")
                    ctx.wait(v, "DV")
                    ctx.inc(v.tensor_tensor(
                        S_sb[:], S_sb[:],
                        psE[:, 0:512].rearrange("p (a z) -> p a z", z=64),
                        ALU.add), "DV")
                    ctx.wait(v, "DV")
                    ctx.inc(v.tensor_copy(S_b[:], S_sb[:]), "DV")
                ctx.on("vector", f_su)
                ctx.dvsync()
                pingE.release(0, "DV", ctx.c["DV"])

                def f_ye(v, i=i, pe_st=pe_st):
                    ctx.wait(v, "STX")
                    ctx.inc(v.tensor_tensor(
                        yscr[:].rearrange("p (h z) -> p h z", h=NH),
                        dsb[:].unsqueeze(2).broadcast_to([128, NH, 64]),
                        vt[:, i, :].rearrange("p (h z) -> p h z", h=NH),
                        ALU.mult), "DV")
                    ctx.wait(v, "DV")
                    yeven = ybuf.rearrange("p (a z) -> p a z", z=64)[:, 0:16:2, :]
                    yodd = ybuf.rearrange("p (a z) -> p a z", z=64)[:, 1:16:2, :]
                    seven = yscr[:].rearrange("p (a z) -> p a z", z=64)[:, 0:16:2, :]
                    sodd = yscr[:].rearrange("p (a z) -> p a z", z=64)[:, 1:16:2, :]
                    ctx.inc(v.tensor_tensor(
                        yeven, psA[:, 0:512].rearrange("p (a z) -> p a z", z=64),
                        seven, ALU.add), "DV")
                    ctx.inc(v.tensor_tensor(
                        yodd, psB[:, 0:512].rearrange("p (a z) -> p a z", z=64),
                        sodd, ALU.add), "DV")
                    ctx.wait(v, "DV")
                ctx.on("vector", f_ye)
                pingY.release(0, "DV", ctx.c["DV"])

            def emit_gn(i):
                def f_gn1(v):
                    ctx.wait(v, "STX")
                    yv = ybuf.rearrange("p (h z) -> p h z", h=NH)
                    ctx.inc(v.reduce_sum(gnst[:, 0:16].unsqueeze(2), yv, AX.X),
                            "DV")
                    ctx.inc(v.tensor_tensor(ysc2v, ybuf, ybuf, ALU.mult),
                            "DV")
                    ctx.wait(v, "DV")
                    ctx.inc(v.reduce_sum(
                        gnst[:, 16:32].unsqueeze(2),
                        ysc2v.rearrange("p (h z) -> p h z", h=NH), AX.X), "DV")
                    ctx.wait(v, "DV")
                    ctx.inc(v.tensor_scalar(gnst[:, 32:48], gnst[:, 0:16],
                                            1.0 / 64, None, ALU.mult), "DV")
                    ctx.inc(v.tensor_scalar(gnst[:, 16:32], gnst[:, 16:32],
                                            1.0 / 64, None, ALU.mult), "DV")
                    ctx.wait(v, "DV")
                    ctx.inc(v.tensor_tensor(gnst[:, 48:64], gnst[:, 32:48],
                                            gnst[:, 32:48], ALU.mult), "DV")
                    ctx.wait(v, "DV")
                    ctx.inc(v.tensor_tensor(gnst[:, 16:32], gnst[:, 16:32],
                                            gnst[:, 48:64], ALU.subtract), "DV")
                ctx.on("vector", f_gn1)
                dv_gn1 = ctx.c["DV"]

                def f_gn2(sc2, dv_gn1=dv_gn1):
                    ctx.wait(sc2, "DV", dv_gn1)
                    ctx.inc(sc2.activation(gnst[:, 48:64], gnst[:, 16:32],
                                           AF.Ln, bias=cvals[:, 2:3]), "AC")
                    ctx.wait(sc2, "AC")
                    ctx.inc(sc2.activation(gnst[:, 0:16], gnst[:, 48:64],
                                           AF.Exp, scale=-0.5), "AC")
                ctx.on("scalar", f_gn2)
                ac_gn2 = ctx.c["AC"]

                def f_gg1(v, i=i):
                    ctx.inc(v.tensor_tensor(yscr[:], lngb[:], gt[:, i, :],
                                            ALU.mult), "DV")
                ctx.on("vector", f_gg1)

                def f_gn3(v, ac_gn2=ac_gn2):
                    ctx.wait(v, "AC", ac_gn2)
                    muv = gnst[:, 32:48].unsqueeze(2).broadcast_to([128, NH, 64])
                    rsv = gnst[:, 0:16].unsqueeze(2).broadcast_to([128, NH, 64])
                    y3 = ysc2v.rearrange("p (h z) -> p h z", h=NH)
                    ctx.inc(v.tensor_tensor(
                        y3, ybuf.rearrange("p (h z) -> p h z", h=NH), muv,
                        ALU.subtract), "DV")
                    ctx.wait(v, "DV")
                    ctx.inc(v.tensor_tensor(y3, y3, rsv, ALU.mult), "DV")
                    ctx.wait(v, "DV")
                    ctx.inc(v.tensor_tensor(ysc2v, ysc2v, yscr[:], ALU.mult),
                            "DV")
                ctx.on("vector", f_gn3)
                dv_g3 = ctx.c["DV"]

                def f_gg2(v, i=i, dv_g3=dv_g3):
                    ctx.wait(v, "DV", dv_g3)
                    ctx.inc(v.tensor_tensor(yscr[:], lnbb[:], gt[:, i, :],
                                            ALU.mult), "DV")
                ctx.on("vector", f_gg2)

                def f_gn4(v):
                    ctx.wait(v, "DV")
                    ctx.inc(v.tensor_tensor(ygb[:], ysc2v, yscr[:], ALU.add),
                            "DV")
                ctx.on("vector", f_gn4)
                M[(i, "ygbdv")] = ctx.dvsync()

            def emit_yg(i):
                prev_yg_ac = M.get("yg_ac")

                def f_yt(t, i=i, prev_yg_ac=prev_yg_ac):
                    t.wait_ge(ctx.sem["DV"], M[(i, "ygbdv")])
                    if prev_yg_ac is not None:
                        t.wait_ge(ctx.sem["AC"], prev_yg_ac)
                    for oc in range(NDC):
                        ctx.inc(t.transpose(
                            psT[:, 1024 + oc * 128:1024 + (oc + 1) * 128],
                            ygb[:][:, oc * 128:(oc + 1) * 128],
                            identb[:]), "PE")
                ctx.on("tensor", f_yt)
                pe_yt = ctx.c["PE"]

                def f_yte(sc, i=i, pe_yt=pe_yt):
                    ctx.wait(sc, "PE", pe_yt)
                    ctx.inc(sc.activation(
                        mixb[:, :, i * 128:(i + 1) * 128],
                        psT[:, 1024:2048].rearrange("p (a b) -> p a b", b=128),
                        AF.Identity), "AC")
                ctx.on("scalar", f_yte)
                M["yg_ac"] = ctx.c["AC"]

            prev = [None]
            for sc_i in range(BT):
                emit_posw(sc_i)
                if prev[0] is not None:
                    emit_state_mm(prev[0])
                    emit_su_ye(prev[0])
                emit_scans(sc_i)
                emit_mexp(sc_i)
                if prev[0] is not None:
                    emit_gn(prev[0])
                emit_mmul(sc_i)
                emit_kht(sc_i)
                emit_d(sc_i)
                emit_w(sc_i)
                if prev[0] is not None:
                    emit_yg(prev[0])
                prev[0] = sc_i
            emit_state_mm(prev[0])
            emit_su_ye(prev[0])
            emit_gn(prev[0])
            emit_yg(prev[0])
            bprev["hfeat_dv"] = M[(prev[0], "ygbdv")]
            if STOP == "bwkv" and bb == 0:
                bscope.close()
                return

            dv_yg = ctx.c["DV"]
            ac_yg = ctx.c["AC"]
            pingO = Ping(1)
            for tt in range(BT):
                def f_om(t, tt=tt, dv_yg=dv_yg, ac_yg=ac_yg):
                    if tt == 0:
                        ctx.wait(t, "DV", dv_yg)
                        ctx.wait(t, "AC", ac_yg)
                        pingW.acquire(lambda s, v: t.wait_ge(ctx.sem[s], v), 0)
                    pingO.acquire(lambda s, v: t.wait_ge(ctx.sem[s], v), 0)
                    for half in range(2):
                        for dc in range(NDC):
                            mm = t.matmul(psC[:, half * 512:(half + 1) * 512],
                                          mixb[:, dc, tt * 128:(tt + 1) * 128],
                                          W[3][:, dc, half * 512:(half + 1) * 512],
                                          start=(dc == 0), stop=(dc == NDC - 1))
                        ctx.inc(mm, "PE")
                ctx.on("tensor", f_om)
                pe_o = ctx.c["PE"]

                def f_res(v, tt=tt, pe_o=pe_o):
                    ctx.wait(v, "PE", pe_o)
                    if tt == 0:
                        ctx.wait(v, "STX")
                    ctx.inc(v.tensor_tensor(xout[:, tt, :], psC[:, 0:1024],
                                            xin[:, tt, :], ALU.add), "DV")
                ctx.on("vector", f_res)
                pingO.release(0, "DV", ctx.c["DV"])
            pingW.release(0, "DV", ctx.c["DV"])

            bprev["res_dv"] = ctx.c["DV"]
            bprev["psT_ac"] = M["yg_ac"]
            bprev["blk_pe"] = ctx.c["PE"]

            def f_st(gp, bb=bb):
                ctx.wait(gp, "DV")
                ctx.inc(gp.dma_start(
                    out=x_out[bb * BLK:(bb + 1) * BLK, :].rearrange(
                        "(tt p) d -> p tt d", p=128),
                    in_=xout[:, 0:BT, :]), "STX", 16)
            ctx.on("gpsimd", f_st)
            if STOP == "bblk0" and bb == 0:
                bscope.close()
                return

            dump_point("brf", lambda: rf[:].rearrange("p a b -> p (a b)"), 4096)
            dump_point("bkf", lambda: kf[:].rearrange("p a b -> p (a b)"), 4096)
            dump_point("bvt", lambda: vt[:].rearrange("p a b -> p (a b)"), 4096)
            dump_point("bgt", lambda: gt[:].rearrange("p a b -> p (a b)"), 4096)
            dump_point("bposw", lambda: posw[:].rearrange("p a b -> p (a b)"), 1024)
            dump_point("bcib", lambda: cib[:].rearrange("p a b -> p (a b)"), 1056)
            dump_point("bS", lambda: S_sb[:].rearrange("p a b -> p (a b)"), 512)
            dump_point("byg", lambda: mixb[:].rearrange("p a b -> p (a b)"), 4096)
            dump_point("bxx", lambda: xx[:].rearrange("p a b -> p (a b)"), 4096)
            dump_point("bmw", lambda: ygb[:], 1024)
        bscope.close()

    targets = [x1d, x2d, x1d, out]
    targets[N_SUB - 1] = out
    if DUMP:
        targets = [x1d, x2d, x1d, x2d]
    emit_attn_a(targets[0])
    if N_SUB >= 2:
        emit_ffn(x1d, targets[1], "f1_")
    if N_SUB >= 3:
        emit_attn_b(x2d, targets[2])
    if N_SUB >= 4:
        emit_ffn(x1d, targets[3], "f2_")

    ctx.barrier()

    def f_fin(gp):
        ctx.wait(gp, "STX")
    ctx.on("gpsimd", f_fin)

    es.close()
    return nc, ctx


def kernel(**inputs):
    sh, xs = prep_host(inputs)
    nc = bass.Bass("TRN2", target_bir_lowering=False)
    nc, ctx = build(nc, sh)
    in_maps = [dict(sh, x0=xs[i]) for i in range(8)]
    res = run_bass_kernel_spmd(nc, in_maps, core_ids=list(range(8)))
    o = np.stack([np.asarray(res.results[i]["out"], np.float32) for i in range(8)], 0)
    return o


# version 22 (final)
# speedup vs baseline: 1.1834x; 1.1834x over previous
import contextlib
import os
import numpy as np
import ml_dtypes
import concourse.bass as bass
import concourse.mybir as mybir
from concourse.bass_utils import run_bass_kernel_spmd

F32 = mybir.dt.float32
BF16 = mybir.dt.bfloat16
AF = mybir.ActivationFunctionType
ALU = mybir.AluOpType
AX = mybir.AxisListType
BF = ml_dtypes.bfloat16

T, D, NH, HD = 4096, 1024, 16, 64
NDC = D // 128
BLK = 512
NBLK = T // BLK
BT = BLK // 128

N_SUB = int(os.environ.get("MK_NSUB", "4"))



def _chtab(v):
    return np.ascontiguousarray(np.asarray(v, np.float32).reshape(NDC, 128).T)


def _wT(w):
    return np.ascontiguousarray(np.asarray(w).T)


def prep_host(inputs):
    g = {k: np.asarray(v) for k, v in inputs.items()}
    sh = {}
    sh["aWq"] = _wT(g["L0_attn_a_Wq"])
    sh["aWk"] = _wT(g["L0_attn_a_Wk"])
    sh["aWv"] = _wT(g["L0_attn_a_Wv"])
    sh["aWo"] = _wT(g["L0_attn_a_Wo"])
    w1 = g["L0_ffn_w1"]
    sh["fw1a"] = np.ascontiguousarray(w1[:D])
    sh["fw1b"] = np.ascontiguousarray(w1[D:])
    sh["fb1"] = np.ascontiguousarray(g["L0_ffn_b1"].reshape(4, 128).T)
    sh["fw2"] = g["L0_ffn_w2"]
    sh["fb2c"] = np.ascontiguousarray(np.tile(g["L0_ffn_b2"][None, :], (128, 1)))
    sh["ftau"] = g["L0_ffn_tau"].reshape(1, 1)
    sh["fWihT"] = _wT(g["L0_ffn_Wih"])
    sh["fWhhT"] = _wT(g["L0_ffn_Whh"])
    sh["fbihc"] = np.ascontiguousarray(np.tile(g["L0_ffn_bih"][None, :], (64, 1)))
    sh["fbhh"] = np.ascontiguousarray(g["L0_ffn_bhh"][None, :])
    sh["fwq"] = g["L0_ffn_wq"]
    sh["fbq"] = np.ascontiguousarray(g["L0_ffn_bq"].reshape(2, 128).T)
    sh["fwk"] = g["L0_ffn_wk"]
    sh["fbk"] = np.ascontiguousarray(g["L0_ffn_bk"].reshape(2, 128).T)
    sh["fwv"] = g["L0_ffn_wv"]
    sh["fbvc"] = np.ascontiguousarray(np.tile(g["L0_ffn_bv"][None, :], (64, 1)))
    sh["flngc"] = np.ascontiguousarray(np.tile(g["L0_ffn_ln_g"][None, :], (64, 1)))
    sh["flnbc"] = np.ascontiguousarray(np.tile(g["L0_ffn_ln_b"][None, :], (64, 1)))
    sh["fwvp"] = g["L0_ffn_wvp"]
    sh["fbvp"] = _chtab(g["L0_ffn_bvp"])
    sh["fwo"] = g["L0_ffn_wo"]
    sh["fboc"] = np.ascontiguousarray(np.tile(g["L0_ffn_bo"][None, :], (64, 1)))
    sh["fslot"] = g["L0_ffn_slot_init"]
    for nm in ("maa_x", "maa_w", "maa_k", "maa_v", "maa_r", "maa_g"):
        sh["b" + nm] = _chtab(g["L0_attn_b_" + nm])
    _mw1 = np.asarray(g["L0_attn_b_maa_w1"], np.float32)
    _w1p = np.zeros((1024, 288), np.float32)
    for f in range(4):
        _w1p[:, (f % 2) * 64 + (f // 2) * 128:(f % 2) * 64 + (f // 2) * 128 + 32] \
            = _mw1[:, f * 32:(f + 1) * 32]
    _w1p[:, 256:288] = _mw1[:, 128:160]
    sh["bmw1"] = _w1p
    _mw2 = np.ascontiguousarray(g["L0_attn_b_maa_w2"].reshape(160, 1024))
    _nm5 = ("maa_w", "maa_k", "maa_v", "maa_r", "maa_g")

    def _pack_pair(i, j):
        a = np.zeros((97, 1024), np.float32)
        a[0:32] = _mw2[i * 32:(i + 1) * 32]
        a[32] = np.asarray(g["L0_attn_b_" + _nm5[i]], np.float32)
        a[64:96] = _mw2[j * 32:(j + 1) * 32]
        a[96] = np.asarray(g["L0_attn_b_" + _nm5[j]], np.float32)
        return np.ascontiguousarray(a)
    sh["bmw2a"] = _pack_pair(0, 1)
    sh["bmw2b"] = _pack_pair(2, 3)
    _c = np.zeros((33, 1024), np.float32)
    _c[0:32] = _mw2[128:160]
    _c[32] = np.asarray(g["L0_attn_b_maa_g"], np.float32)
    sh["bmw2c"] = _c
    sh["bWr"] = g["L0_attn_b_Wr"]
    sh["bWk"] = g["L0_attn_b_Wk"]
    sh["bWv"] = g["L0_attn_b_Wv"]
    sh["bWo"] = g["L0_attn_b_Wo"]
    sh["bgw1"] = g["L0_attn_b_gate_w1"]
    sh["bgw2"] = g["L0_attn_b_gate_w2"]
    sh["btd"] = _chtab(g["L0_attn_b_time_decay"])
    sh["bdw1"] = g["L0_attn_b_decay_w1"]
    sh["bdw2"] = g["L0_attn_b_decay_w2"]
    u = np.asarray(g["L0_attn_b_u"], np.float32)
    u16 = np.zeros((D, NH), np.float32)
    for h in range(NH):
        u16[h * HD:(h + 1) * HD, h] = u[h]
    sh["bU16"] = np.ascontiguousarray(u16.reshape(NDC, 128, NH))
    sh["blngc"] = np.ascontiguousarray(
        np.tile(g["L0_attn_b_ln_g"][None, :], (128, 1))).astype(BF)
    sh["blnbc"] = np.ascontiguousarray(
        np.tile(g["L0_attn_b_ln_b"][None, :], (128, 1))).astype(BF)
    sh["identb"] = np.eye(128, dtype=BF)
    sh["identf"] = np.eye(128, dtype=np.float32)
    ka = np.arange(128)
    sh["maskA"] = (ka[:, None] <= ka[None, :]).astype(BF)
    ks = np.arange(64)
    sh["maskW"] = np.ascontiguousarray(
        np.tile((ks[:, None] < ks[None, :]).astype(np.float32), (1, NH)).reshape(
            64, NH, 64).transpose(1, 0, 2).reshape(NH * 64, 64).T)
    sh["maskW"] = (ks[:, None] < ks[None, :]).astype(np.float32)
    kb128 = np.arange(128)
    sh["maskB"] = (kb128[:, None] < kb128[None, :]).astype(BF)
    sh["onesb"] = np.ones((128, 72), BF)
    sh["onesf"] = np.ones((128, 144), np.float32)
    cv = np.array([1e-5, -8.0, 6.4e-4, 1e-8, 0.1, 75.0, 1.0, 0.0], np.float32)
    sh["cvals"] = np.ascontiguousarray(np.tile(cv[None, :], (128, 1)))
    xs = [np.ascontiguousarray(g["L0_attn_a_hidden_states"][i], np.float32)
          for i in range(8)]
    return sh, xs



class Ctx:
    def __init__(self, nc, blk, es):
        self.nc = nc
        self.blk = blk
        self.es = es
        self.sem = {}
        self.c = {}
        self.sbuf_bytes = 0

    def mk_sems(self, *names):
        for n in names:
            self.sem[n] = self.es.enter_context(self.nc.semaphore(n))
            self.c[n] = 0

    def inc(self, instr, name, k=1):
        instr.then_inc(self.sem[name], k)
        self.c[name] += k
        return self.c[name]

    def wait(self, eng, name, v=None):
        eng.wait_ge(self.sem[name], self.c[name] if v is None else v)

    def sb(self, name, shape, dt, stack=None):
        t = (stack or self.es).enter_context(
            self.nc.sbuf_tensor(name, list(shape), dt))
        n = 1
        for s in shape[1:]:
            n *= s
        self.sbuf_bytes += n * (2 if dt == BF16 else 4)
        return t

    def on(self, engine_name, fn):
        getattr(self.blk, engine_name)(fn)

    def dvsync(self):
        def f(v):
            self.inc(v.engine_nop(), "DV")
        self.on("vector", f)
        return self.c["DV"]

    def gpsync(self):
        def f(gp):
            self.inc(gp.engine_nop(), "GP")
        self.on("gpsimd", f)
        return self.c["GP"]

    def barrier(self):
        sem = self.sem["BAR"]
        dma_sems = ["LDW", "LDX", "LDH", "LDM", "LDG", "STH", "STX"]
        base = self.c["BAR"]
        tgt = base + 5
        self.c["BAR"] = tgt
        snap = {n: self.c[n] for n in dma_sems}

        dummy = self.dummy

        def body(eng, en):
            for n in dma_sems:
                if snap[n]:
                    eng.wait_ge(self.sem[n], snap[n])
            if en == "sync":
                eng.sem_inc(sem, 1)
            else:
                eng.drain().then_inc(sem, 1)
            eng.wait_ge(sem, tgt)
        for en in ("tensor", "vector", "scalar", "gpsimd", "sync"):
            self.on(en, lambda eng, en=en: body(eng, en))


class Ping:

    def __init__(self, n):
        self.free = [None] * n
        self.n = n

    def acquire(self, eng_wait, i):
        f = self.free[i % self.n]
        if f is not None:
            eng_wait(*f)

    def release(self, i, semname, val):
        self.free[i % self.n] = (semname, val)



def build(nc, sh):
    es = contextlib.ExitStack()
    blk = None

    dram = {}
    x0 = nc.dram_tensor("x0", [T, D], F32, kind="ExternalInput").ap()
    for nm, arr in sh.items():
        dt = BF16 if arr.dtype == BF else F32
        dram[nm] = nc.dram_tensor(nm, list(arr.shape), dt, kind="ExternalInput").ap()
    out = nc.dram_tensor("out", [T, D], F32, kind="ExternalOutput").ap()
    x1d = nc.dram_tensor("x1d", [T, D], F32, kind="Internal").ap()
    x2d = nc.dram_tensor("x2d", [T, D], F32, kind="Internal").ap()
    hd = nc.dram_tensor("hd", [T, D], BF16, kind="Internal").ap()

    blk = es.enter_context(nc.Block())
    ctx = Ctx(nc, blk, es)
    ctx.dummy = None
    ctx.mk_sems("BAR", "LDW", "LDX", "LDH", "LDM", "LDG", "STH", "STX",
                "PE", "DV", "AC", "GP")
    sb = ctx.sb

    W = [sb(f"Wslot{i}", (128, NDC, 1024), BF16) for i in range(4)]
    xin = sb("xin", (128, BT, 1024), F32)
    xout = sb("xout", (128, BT, 1024), F32)
    htok = sb("htok", (128, BT, 1024), BF16)
    hfeat = sb("hfeat", (128, NDC, BLK), BF16)
    stats = sb("stats", (128, 32), F32)
    ctx.dummy = stats

    identb = sb("identb_s", (128, 128), BF16)

    maskA = sb("maskA_s", (128, 128), BF16)
    maskB = sb("maskB_s", (128, 128), BF16)
    onesb = sb("onesb_s", (128, 72), BF16)
    onesf = sb("onesf_s", (128, 144), F32)
    cvals = sb("cvals_s", (128, 8), F32)


    psA = es.enter_context(nc.psum_tensor("psA", [128, 512], F32))
    psB = es.enter_context(nc.psum_tensor("psB", [128, 512], F32))
    psC = es.enter_context(nc.psum_tensor("psC", [128, 1024], F32))
    psD = es.enter_context(nc.psum_tensor("psD", [128, 512], F32))
    psE = es.enter_context(nc.psum_tensor("psE", [128, 512], F32))
    psT = es.enter_context(nc.psum_tensor("psT", [128, 2048], BF16))

    DUMP = os.environ.get("MK_DUMP", "")
    STOP = os.environ.get("MK_STOP", "")
    dump_done = [False]

    def dump_point(tag, mkap, nfree):
        if DUMP != tag or dump_done[0]:
            return
        dump_done[0] = True
        ctx.barrier()

        def f(gp):
            outv = out.rearrange("(a p) d -> p a d", p=128)
            src = mkap()
            np_ = src.shape[0]
            nfull, r = nfree // 1024, nfree % 1024
            if nfull:
                ctx.inc(gp.dma_start(
                    out=outv[0:np_, 0:nfull, :],
                    in_=src[:, 0:nfull * 1024].rearrange(
                        "p (a d) -> p a d", d=1024)), "STX", 16)
            if r:
                ctx.inc(gp.dma_start(
                    out=outv[0:np_, nfull, 0:r],
                    in_=src[:, nfull * 1024:nfree]), "STX", 16)
        ctx.on("gpsimd", f)

    def load_consts():
        def f(sync):
            for dst, src in ((identb, "identb"),
                             (maskA, "maskA"),
                             (maskB, "maskB"),
                             (onesb, "onesb"), (onesf, "onesf"),
                             (cvals, "cvals")):
                ctx.inc(sync.dma_start(out=dst[:], in_=dram[src][:, :]), "LDM", 16)
        ctx.on("sync", f)

        def g(gp):
            for i, src in enumerate(("aWq", "aWk", "aWv", "aWo")):
                ctx.inc(gp.dma_start(
                    out=W[i][:],
                    in_=dram[src].rearrange("(dc p) o -> p dc o", p=128)), "LDW", 16)
        ctx.on("gpsimd", g)



    def ln_block(xsrc, t0, nt=BT, g=None, htokb=None):
        g = g or {}
        hb = htok if htokb is None else htokb
        nw = nt * 128

        def f_load(sync):
            if g.get("xin_dv") is not None:
                sync.wait_ge(ctx.sem["DV"], g["xin_dv"])
            ctx.inc(sync.dma_start(
                out=xin[:, 0:nt, :],
                in_=xsrc[t0:t0 + nw, :].rearrange("(tt p) d -> p tt d", p=128)),
                "LDX", 16)
        ctx.on("sync", f_load)

        def f_stat_sc(sc):
            ctx.wait(sc, "LDX")
            if g.get("hfeat_pe") is not None:
                sc.wait_ge(ctx.sem["PE"], g["hfeat_pe"])
            if g.get("hfeat_dv") is not None:
                sc.wait_ge(ctx.sem["DV"], g["hfeat_dv"])
            if g.get("hfeat_gp") is not None:
                sc.wait_ge(ctx.sem["GP"], g["hfeat_gp"])
            scr = hfeat[:].rearrange("p a b -> p (a b)")
            for tt in range(nt):
                ctx.inc(sc.activation(scr[:, tt * 1024:(tt + 1) * 1024],
                                      xin[:, tt, :], AF.Square,
                                      accum_out=stats[:, 8 + tt:9 + tt]), "AC")
            for tt in range(nt):
                ctx.wait(sc, "AC")
                ctx.inc(sc.activation(scr[:, tt * 1024:(tt + 1) * 1024],
                                      xin[:, tt, :], AF.Identity,
                                      accum_out=stats[:, tt:tt + 1]), "AC")
        ctx.on("scalar", f_stat_sc)

        def f_stat(v):
            ctx.wait(v, "AC")
            ctx.inc(v.tensor_scalar(stats[:, 16:16 + nt], stats[:, 0:nt],
                                    -1.0 / D, None, ALU.mult), "DV")
            ctx.wait(v, "DV")
            ctx.inc(v.tensor_scalar(stats[:, 8:8 + nt], stats[:, 8:8 + nt],
                                    1.0 / D, None, ALU.mult), "DV")
            ctx.inc(v.tensor_tensor(stats[:, 24:24 + nt], stats[:, 16:16 + nt],
                                    stats[:, 16:16 + nt], ALU.mult), "DV")
            ctx.wait(v, "DV")
            ctx.inc(v.tensor_tensor(stats[:, 8:8 + nt], stats[:, 8:8 + nt],
                                    stats[:, 24:24 + nt], ALU.subtract), "DV")
        ctx.on("vector", f_stat)

        def f_sqrt(sc):
            ctx.wait(sc, "DV")
            ctx.inc(sc.activation(stats[:, 24:24 + nt], stats[:, 8:8 + nt],
                                  AF.Ln, bias=cvals[:, 0:1]), "AC")
            ctx.wait(sc, "AC")
            ctx.inc(sc.activation(stats[:, 8:8 + nt], stats[:, 24:24 + nt],
                                  AF.Exp, scale=-0.5), "AC")
        ctx.on("scalar", f_sqrt)

        def f_norm(v):
            ctx.wait(v, "AC")
            if g.get("htok_pe") is not None:
                v.wait_ge(ctx.sem["PE"], g["htok_pe"])
            for tt in range(nt):
                v.tensor_scalar(hb[:, tt, :], xin[:, tt, :],
                                stats[:, 16 + tt:17 + tt],
                                stats[:, 8 + tt:9 + tt], ALU.add, ALU.mult)
            ctx.inc(v.engine_nop(), "DV")
        ctx.on("vector", f_norm)
        dv_norm = ctx.c["DV"]
        ctx.norm_dv = dv_norm

        ac_prev = [None]
        for tp2 in range(0, nt, 2):
            def f_t(t, tp2=tp2, ap=ac_prev[0], dv_norm=dv_norm):
                if tp2 == 0:
                    t.wait_ge(ctx.sem["DV"], dv_norm)
                    if g.get("psT_ac") is not None:
                        t.wait_ge(ctx.sem["AC"], g["psT_ac"])
                    if g.get("psT_dv") is not None:
                        t.wait_ge(ctx.sem["DV"], g["psT_dv"])
                else:
                    t.wait_ge(ctx.sem["AC"], ap)
                for tt in (tp2, tp2 + 1):
                    bank = (tt % 2) * 1024
                    for dc in range(NDC):
                        ctx.inc(t.transpose(
                            psT[:, bank + dc * 128:bank + (dc + 1) * 128],
                            hb[:, tt, dc * 128:(dc + 1) * 128],
                            identb[:]), "PE")
            ctx.on("tensor", f_t)
            pe_t = ctx.c["PE"]

            def f_e(sc, tp2=tp2, pe_t=pe_t):
                ctx.wait(sc, "PE", pe_t)
                for tt in (tp2, tp2 + 1):
                    bank = (tt % 2) * 1024
                    ctx.inc(sc.activation(
                        hfeat[:, :, tt * 128:(tt + 1) * 128],
                        psT[:, bank:bank + 1024].rearrange(
                            "p (a b) -> p a b", b=128),
                        AF.Identity), "AC")
            ctx.on("scalar", f_e)
            ac_prev[0] = ctx.c["AC"]
        ctx.hfeat_ac = ctx.c["AC"]

    def proj_feat(wslot, dst, dst_off, nw, bias_tab=None):
        ping = Ping(2)
        ha = ctx.hfeat_ac
        for ot in range(NDC):
            ps = psA if ot % 2 == 0 else psB

            def f_mm(t, ot=ot, ps=ps, ha=ha):
                if ot < 2:
                    ctx.wait(t, "LDW")
                    t.wait_ge(ctx.sem["AC"], ha)
                ping.acquire(lambda s, v: t.wait_ge(ctx.sem[s], v), ot)
                for dc in range(NDC):
                    mm = t.matmul(ps[:, 0:nw], wslot[:, dc, ot * 128:(ot + 1) * 128],
                                  hfeat[:, dc, 0:nw],
                                  start=(dc == 0), stop=(dc == NDC - 1))
                ctx.inc(mm, "PE")
            ctx.on("tensor", f_mm)
            pe_v = ctx.c["PE"]

            def f_ev(v, ot=ot, ps=ps, pe_v=pe_v):
                ctx.wait(v, "PE", pe_v)
                if bias_tab is None:
                    ins = v.tensor_copy(dst[:, ot, dst_off:dst_off + nw],
                                        ps[:, 0:nw])
                else:
                    ins = v.tensor_scalar(dst[:, ot, dst_off:dst_off + nw],
                                          ps[:, 0:nw],
                                          bias_tab[:, ot:ot + 1], None, ALU.add)
                ctx.inc(ins, "DV")
            ctx.on("vector", f_ev)
            ping.release(ot, "DV", ctx.c["DV"])

    def emit_attn_a(xtgt):
        ascope = contextlib.ExitStack()
        Kf = sb("Kf", (128, NDC, 1024), BF16, stack=ascope)
        Vaug = sb("Vaug", (128, 8, NH, 65), BF16, stack=ascope)
        Qf = sb("Qf", (128, NDC, BLK), BF16, stack=ascope)
        Otok = sb("Otok", (128, BT, 1024), BF16, stack=ascope)
        Ofeat = sb("Ofeat", (128, NDC, BLK), BF16, stack=ascope)
        nrm4 = sb("nrm4", (128, 8), F32, stack=ascope)
        load_consts()

        def f_vone(v):
            v.memset(Vaug[:, :, :, 64:65], 1.0)
            ctx.inc(v.engine_nop(), "DV")
        ctx.on("vector", f_vone)
        ctx.barrier()

        kprev = {}
        pingC = Ping(1)
        for kb in range(2):
            ln_block(x0, kb * BLK, g={
                "xin_dv": kprev.get("norm_dv"),
                "hfeat_pe": kprev.get("pe"),
                "psT_ac": kprev.get("psT_ac"),
            })
            kprev["norm_dv"] = ctx.norm_dv
            kprev["psT_ac"] = ctx.hfeat_ac
            if kb == 0:
                if DUMP == "hd" and not dump_done[0]:
                    dump_done[0] = True
                    ctx.barrier()
                    def f_hd(gp):
                        ctx.inc(gp.dma_start(out=out[0:512, :], in_=hd[0:512, :]),
                                "STX", 16)
                    ctx.on("gpsimd", f_hd)
                dump_point("stats", lambda: stats[:], 32)
                dump_point("xin", lambda: xin[:].rearrange("p a b -> p (a b)"), 4096)
                dump_point("htok", lambda: htok[:].rearrange("p a b -> p (a b)"), 4096)
                dump_point("hfeat", lambda: hfeat[:].rearrange("p a b -> p (a b)"), 4096)
            proj_feat(W[1], Kf, kb * BLK, BLK)

            for tt in range(BT):
                def f_vm(t, tt=tt):
                    pingC.acquire(lambda s, v: t.wait_ge(ctx.sem[s], v), 0)
                    for half in range(2):
                        for dc in range(NDC):
                            mm = t.matmul(psC[:, half * 512:(half + 1) * 512],
                                          hfeat[:, dc, tt * 128:(tt + 1) * 128],
                                          W[2][:, dc, half * 512:(half + 1) * 512],
                                          start=(dc == 0), stop=(dc == NDC - 1))
                        ctx.inc(mm, "PE")
                ctx.on("tensor", f_vm)
                pe_v = ctx.c["PE"]

                def f_vev(v, tt=tt, kb=kb, pe_v=pe_v):
                    ctx.wait(v, "PE", pe_v)
                    ctx.inc(v.tensor_copy(
                        Vaug[:, kb * BT + tt, :, 0:64],
                        psC[:, 0:1024].rearrange("p (h j) -> p h j", h=NH)), "DV")
                ctx.on("vector", f_vev)
                pingC.release(0, "DV", ctx.c["DV"])
            kprev["pe"] = ctx.c["PE"]

        dump_point("Kf", lambda: Kf[:].rearrange("p a b -> p (a b)"), 8192)
        dump_point("Vaug", lambda: Vaug[:].rearrange("p a b c -> p (a b c)"), 8320)
        if STOP == "akey":
            ascope.close()
            return

        Ph2 = [W[1][:].rearrange("p a b -> p (a b)"),
               W[2][:].rearrange("p a b -> p (a b)")]
        aprev = {}
        ctx.barrier()
        for qb in range(NBLK):
            ln_block(x0, qb * BLK, g={
                "xin_dv": aprev.get("res_dv"),
                "hfeat_pe": aprev.get("qf_pe"),
                "psT_dv": aprev.get("te_dv"),
            })
            proj_feat(W[0], Qf, 0, BLK)
            aprev["qf_pe"] = ctx.c["PE"]
            r = qb % 2
            KTm = r * 4 + 4

            pingS = Ping(4)
            pingPV = Ping(2)
            pv_args = [None]

            def emit_pv(h, dv_mask):
                psPV = psD if h % 2 == 0 else psE
                Ph = Ph2[h % 2]

                def f_pv(t, h=h, psPV=psPV, Ph=Ph, dv_mask=dv_mask):
                    t.wait_ge(ctx.sem["DV"], dv_mask)
                    pingPV.acquire(lambda s, v: t.wait_ge(ctx.sem[s], v), h)
                    for qt in range(BT):
                        lq = r * 4 + qt
                        for kt in range(lq + 1):
                            mm = t.matmul(
                                psPV[:, qt * 65:qt * 65 + 65],
                                Ph[:, kt * 512 + qt * 128:kt * 512 + (qt + 1) * 128],
                                Vaug[:, kt, h, :],
                                start=(kt == 0), stop=(kt == lq))
                    ctx.inc(mm, "PE")
                ctx.on("tensor", f_pv)
                pe_pv = ctx.c["PE"]

                def f_oev(v, h=h, psPV=psPV, pe_pv=pe_pv):
                    ctx.wait(v, "PE", pe_pv)
                    pv = psPV[:, 0:260].rearrange("p (q z) -> p q z", z=65)
                    ctx.inc(v.tensor_copy(nrm4[:, 0:4].unsqueeze(2),
                                          pv[:, :, 64:65]), "DV")
                    ctx.wait(v, "DV")
                    ctx.inc(v.reciprocal(nrm4[:, 4:8], nrm4[:, 0:4]), "DV")
                    ctx.wait(v, "DV")
                    ctx.inc(v.tensor_tensor(
                        Otok[:, :, 64 * h:64 * h + 64], pv[:, :, 0:64],
                        nrm4[:, 4:8].unsqueeze(2).broadcast_to([128, 4, 64]),
                        ALU.mult), "DV")
                ctx.on("vector", f_oev)
                pingPV.release(h, "DV", ctx.c["DV"])

            sbank = [psA[:, 0:512], psB[:, 0:512], psC[:, 0:512],
                     psC[:, 512:1024]]
            for h in range(NH):
                hs, hf = 64 * (h % 2), h // 2
                Ph = Ph2[h % 2]
                for kp in range(0, KTm, 4):
                    grp = [kp, kp + 1, kp + 2, kp + 3]
                    for kt in grp:
                        qo = max(0, kt - r * 4) * 128
                        ps = sbank[kt % 4]

                        first = (h == 0 and kt == 0)

                        def f_sc(t, h=h, kt=kt, qo=qo, ps=ps, hs=hs, hf=hf,
                                 first=first):
                            if first:
                                ctx.wait(t, "DV")
                            pingS.acquire(
                                lambda s, v: t.wait_ge(ctx.sem[s], v), kt)
                            mm = t.matmul(ps[:, 0:512 - qo],
                                          Kf[hs:hs + 64, hf,
                                             kt * 128:(kt + 1) * 128],
                                          Qf[hs:hs + 64, hf, qo:512],
                                          start=True, stop=True)
                            ctx.inc(mm, "PE")
                        ctx.on("tensor", f_sc)
                    pe_grp = ctx.c["PE"]

                    def f_ex(sc, grp=grp, Ph=Ph, pe_grp=pe_grp):
                        ctx.wait(sc, "PE", pe_grp)
                        for kt in grp:
                            qo = max(0, kt - r * 4) * 128
                            ps = sbank[kt % 4]
                            ctx.inc(sc.activation(
                                Ph[:, kt * 512 + qo:(kt + 1) * 512],
                                ps[:, 0:512 - qo], AF.Exp, bias=cvals[:, 1:2],
                                scale=0.125), "AC")
                    ctx.on("scalar", f_ex)
                    for j, kt in enumerate(grp):
                        pingS.release(kt, "AC", ctx.c["AC"] - 3 + j)

                def f_mask(v, r=r, Ph=Ph):
                    ctx.wait(v, "AC")
                    base = r * 4 * 512
                    view = Ph[:, base:base + 2560].rearrange(
                        "p (j z) -> p j z", z=640)[:, :, 0:128]
                    mb = maskA[:, :].unsqueeze(1).broadcast_to([128, 4, 128])
                    ctx.inc(v.tensor_tensor(view, view, mb, ALU.mult), "DV")
                ctx.on("vector", f_mask)
                gp_mask = ctx.dvsync()

                if STOP != "asc" and pv_args[0] is not None:
                    emit_pv(*pv_args[0])
                pv_args[0] = (h, gp_mask)
            if STOP == "asc" and qb == 0:
                ascope.close()
                return
            emit_pv(*pv_args[0])
            if STOP == "aq1" and qb == 0:
                ascope.close()
                return

            if qb == 0:
                dump_point("Qf", lambda: Qf[:].rearrange("p a b -> p (a b)"), 4096)
                dump_point("Phead", lambda: Ph2[1][:, 0:4800], 4800)
                dump_point("Otok", lambda: Otok[:].rearrange("p a b -> p (a b)"), 4096)

            prev_te = [None, None]
            for tt in range(BT):
                bank = (tt % 2) * 1024

                def f_tp(t, tt=tt, bank=bank, pte=prev_te[tt % 2]):
                    if tt == 0:
                        ctx.wait(t, "DV")
                    if pte is not None:
                        t.wait_ge(ctx.sem["DV"], pte)
                    for dc in range(NDC):
                        ctx.inc(t.transpose(
                            psT[:, bank + dc * 128:bank + (dc + 1) * 128],
                            Otok[:, tt, dc * 128:(dc + 1) * 128],
                            identb[:]), "PE")
                ctx.on("tensor", f_tp)
                pe_v = ctx.c["PE"]

                def f_te(v, tt=tt, bank=bank, pe_v=pe_v):
                    ctx.wait(v, "PE", pe_v)
                    ctx.inc(v.tensor_copy(
                        Ofeat[:, :, tt * 128:(tt + 1) * 128],
                        psT[:, bank:bank + 1024].rearrange(
                            "p (a b) -> p a b", b=128)), "DV")
                ctx.on("vector", f_te)
                prev_te[tt % 2] = ctx.c["DV"]
            aprev["te_dv"] = ctx.c["DV"]
            if STOP == "atp" and qb == 0:
                ascope.close()
                return

            pingC = Ping(1)
            for tt in range(BT):
                def f_om(t, tt=tt):
                    if tt == 0:
                        ctx.wait(t, "DV")
                    pingC.acquire(lambda s, v: t.wait_ge(ctx.sem[s], v), 0)
                    for half in range(2):
                        for dc in range(NDC):
                            mm = t.matmul(psC[:, half * 512:(half + 1) * 512],
                                          Ofeat[:, dc, tt * 128:(tt + 1) * 128],
                                          W[3][:, dc, half * 512:(half + 1) * 512],
                                          start=(dc == 0), stop=(dc == NDC - 1))
                        ctx.inc(mm, "PE")
                ctx.on("tensor", f_om)
                pe_v = ctx.c["PE"]

                def f_res(v, tt=tt, pe_v=pe_v):
                    ctx.wait(v, "PE", pe_v)
                    if tt == 0:
                        ctx.wait(v, "STX")
                    ctx.inc(v.tensor_tensor(xout[:, tt, :], psC[:, 0:1024],
                                            xin[:, tt, :], ALU.add), "DV")
                ctx.on("vector", f_res)
                pingC.release(0, "DV", ctx.c["DV"])
            aprev["res_dv"] = ctx.c["DV"]

            def f_st(gp, qb=qb):
                ctx.wait(gp, "DV")
                ctx.inc(gp.dma_start(
                    out=xtgt[qb * BLK:(qb + 1) * BLK, :].rearrange(
                        "(tt p) d -> p tt d", p=128),
                    in_=xout[:, 0:BT, :]), "STX", 16)
            ctx.on("gpsimd", f_st)
            if STOP == f"aqb{qb}":
                ascope.close()
                return
        ascope.close()

    def emit_ffn(x_in, x_out, tag):
        fscope = contextlib.ExitStack()
        _sb0 = ctx.sb
        ctx_sb = lambda n, shp, dt: _sb0(n, shp, dt, stack=fscope)
        alpha = ctx_sb(tag + "alpha", (128, 32, 64), BF16)
        htok2 = ctx_sb(tag + "htok2", (128, BT, 1024), BF16)
        Gf = ctx_sb(tag + "Gf", (128, 4, 512), BF16)
        lgt = ctx_sb(tag + "lgt", (128, 4, 64), F32)
        LW = ctx_sb(tag + "LW", (128, 4, 64), F32)
        eqb = ctx_sb(tag + "eqb", (128, 4, 64), F32)
        mxs = ctx_sb(tag + "mxs", (128, 16), F32)
        S0f = ctx_sb(tag + "S0f", (64, 1024), F32)
        S0b = ctx_sb(tag + "S0b", (64, 1024), BF16)
        S0ft = ctx_sb(tag + "S0ft", (128, 8, 64), BF16)
        smf = ctx_sb(tag + "smf", (128, 8), F32)
        smfb = ctx_sb(tag + "smfb", (128, 8), BF16)
        b1p = ctx_sb(tag + "b1p", (128, 4), F32)
        w2sb = ctx_sb(tag + "w2sb", (128, 4, 64), BF16)
        fb1t = ctx_sb(tag + "fb1t", (128, 4), F32)
        staub = ctx_sb(tag + "staub", (128, 1), F32)
        b2s = ctx_sb(tag + "b2s", (128, 64), F32)
        tau1 = ctx_sb(tag + "tau1", (1, 4), F32)
        Cadd = ctx_sb(tag + "Cadd", (64, 3072), F32)
        Hn = ctx_sb(tag + "Hn", (64, 1024), F32)
        bhhb = ctx_sb(tag + "bhhb", (1, 3072), BF16)
        preN = ctx_sb(tag + "preN", (64, 1024), F32)
        rz = ctx_sb(tag + "rz", (64, 2048), F32)
        tmp1 = ctx_sb(tag + "tmp1", (64, 1024), F32)
        tmp2 = ctx_sb(tag + "tmp2", (64, 1024), F32)
        tmpA = ctx_sb(tag + "tmpA", (64, 1024), F32)
        tmpB = ctx_sb(tag + "tmpB", (64, 1024), F32)
        Snew = ctx_sb(tag + "Snew", (64, 1024), F32)
        Snf = ctx_sb(tag + "Snf", (128, 8, 64), BF16)
        qf2 = ctx_sb(tag + "qf2", (128, 2, 64), BF16)
        kf2 = ctx_sb(tag + "kf2", (128, 2, 64), BF16)
        fbqt = ctx_sb(tag + "fbqt", (128, 2), F32)
        fbkt = ctx_sb(tag + "fbkt", (128, 2), F32)
        sst = ctx_sb(tag + "sst", (64, 8), F32)
        atb = ctx_sb(tag + "atb", (64, 64), BF16)
        atT = ctx_sb(tag + "atT", (64, 64), BF16)
        vb2 = ctx_sb(tag + "vb2", (64, 1024), BF16)
        ynb = ctx_sb(tag + "ynb", (64, 1024), BF16)
        ynf = ctx_sb(tag + "ynf", (128, 8, 64), BF16)
        SVf = ctx_sb(tag + "SVf", (128, 8, 64), BF16)
        fbvpt = ctx_sb(tag + "fbvpt", (128, 8), F32)
        svwb = ctx_sb(tag + "svwb", (64, 1024), BF16)
        SVW2 = ctx_sb(tag + "SVW2", (128, 1024), BF16)
        aTb = ctx_sb(tag + "aTb", (64, 512), BF16)

        pingST = Ping(2)

        def slot_transpose(srcap, dst):
            def f_t(t):
                ctx.wait(t, "DV")
                for dc in range(NDC):
                    ctx.inc(t.transpose(psT[:, dc * 64:(dc + 1) * 64],
                                        srcap[0:64, dc * 128:(dc + 1) * 128],
                                        identb[0:64, 0:64]), "PE")
            ctx.on("tensor", f_t)
            pe_v = ctx.c["PE"]

            def f_e(v, pe_v=pe_v):
                ctx.wait(v, "PE", pe_v)
                ctx.inc(v.tensor_copy(
                    dst[:],
                    psT[:, 0:512].rearrange("p (a b) -> p a b", b=64)), "DV")
            ctx.on("vector", f_e)

        ctx.barrier()

        def f_ld(gp):
            for dst, src in (
                (W[0][:, :, 0:512], dram["fw1a"].rearrange("(dc p) o -> p dc o", p=128)),
                (W[1][:, :, 0:512], dram["fw1b"].rearrange("(dc p) o -> p dc o", p=128)),
                (w2sb[:], dram["fw2"].rearrange("(oc p) s -> p oc s", p=128)),
                (S0f[:], dram["fslot"][:, :]),
                (bhhb[:], dram["fbhh"][:, :]),
                (Cadd[:], dram["fbihc"][:, :]),
                (tau1[0:1, 0:1], dram["ftau"][:, :]),
                (fb1t[:], dram["fb1"][:, :]),
                (b2s[:], dram["fb2c"][:, :]),
                (fbqt[:], dram["fbq"][:, :]),
                (fbkt[:], dram["fbk"][:, :]),
                (fbvpt[:], dram["fbvp"][:, :]),
            ):
                ctx.inc(gp.dma_start(out=dst, in_=src), "LDW", 16)
        ctx.on("gpsimd", f_ld)

        def f_tauabs(sc):
            ctx.wait(sc, "LDW")
            ctx.inc(sc.activation(tau1[0:1, 1:2], tau1[0:1, 0:1], AF.Abs), "AC")
        ctx.on("scalar", f_tauabs)

        def f_pv(v):
            ctx.wait(v, "LDW")
            ctx.inc(v.tensor_copy(S0b[:], S0f[:]), "DV")
            ctx.wait(v, "AC")
            ctx.inc(v.tensor_scalar(tau1[0:1, 1:2], tau1[0:1, 1:2], 0.1, None,
                                    ALU.add), "DV")
            ctx.wait(v, "DV")
            ctx.inc(v.reciprocal(tau1[0:1, 2:3], tau1[0:1, 1:2]), "DV")
        ctx.on("vector", f_pv)

        slot_transpose(S0b[:], S0ft)

        def f_sm(t):
            ctx.wait(t, "DV")
            for dc in range(NDC):
                ctx.inc(t.matmul(psA[:, dc:dc + 1], S0b[:, dc * 128:(dc + 1) * 128],
                                 onesb[0:64, 0:1], start=True, stop=True), "PE")
            ctx.inc(t.matmul(psD[:, 0:1], onesf[0:1, 0:128], tau1[0:1, 2:3],
                             start=True, stop=True), "PE")
        ctx.on("tensor", f_sm)
        pe_sm = ctx.c["PE"]

        def f_sme(v, pe_sm=pe_sm):
            ctx.wait(v, "PE", pe_sm)
            ctx.inc(v.tensor_scalar(smf[:, 0:8], psA[:, 0:8], 1.0 / 64, None,
                                    ALU.mult), "DV")
        ctx.on("vector", f_sme)

        def f_smb(v):
            ctx.wait(v, "AC")
            ctx.wait(v, "DV")
            ctx.inc(v.tensor_copy(smfb[:], smf[:]), "DV")
            ctx.inc(v.tensor_copy(staub[:, 0:1], psD[:, 0:1]), "DV")
            ctx.wait(v, "DV")
            ctx.inc(v.tensor_tensor(b2s[:], b2s[:],
                                    staub[:, 0:1].broadcast_to([128, 64]),
                                    ALU.mult), "DV")
        ctx.on("vector", f_smb)
        dv_smb = ctx.c["DV"]

        def f_b1(t, dv_smb=dv_smb):
            t.wait_ge(ctx.sem["DV"], dv_smb)
            for oc in range(4):
                for dc in range(NDC):
                    mm = t.matmul(psB[:, oc:oc + 1],
                                  W[1][:, dc, oc * 128:(oc + 1) * 128],
                                  smfb[:, dc:dc + 1],
                                  start=(dc == 0), stop=(dc == NDC - 1))
                ctx.inc(mm, "PE")
        ctx.on("tensor", f_b1)
        pe_b1 = ctx.c["PE"]

        def f_b1e(v, pe_b1=pe_b1):
            ctx.wait(v, "PE", pe_b1)
            ctx.inc(v.tensor_tensor(b1p[:], psB[:, 0:4], fb1t[:], ALU.add), "DV")
        ctx.on("vector", f_b1e)

        ctx.barrier()

        def f_whh(gp):
            for sl, c0 in ((W[2], 0), (W[3], 1024)):
                ctx.inc(gp.dma_start(
                    out=sl[:],
                    in_=dram["fWhhT"][:, c0:c0 + 1024].rearrange(
                        "(dc p) o -> p dc o", p=128)), "LDW", 16)
        ctx.on("gpsimd", f_whh)

        def f_whh2(gp):
            ctx.wait(gp, "PE")
            ctx.inc(gp.dma_start(
                out=W[2][:],
                in_=dram["fWhhT"][:, 2048:3072].rearrange(
                    "(dc p) o -> p dc o", p=128)), "LDW", 16)

        for piece in range(6):
            wslot, coff = [(W[2], 0), (W[2], 512), (W[3], 0), (W[3], 512),
                           (W[2], 0), (W[2], 512)][piece]
            ps = psA if piece % 2 == 0 else psB
            if piece == 4:
                ctx.on("gpsimd", f_whh2)

            def f_hm(t, wslot=wslot, coff=coff, ps=ps, piece=piece):
                if piece in (0, 4):
                    ctx.wait(t, "LDW")
                if piece == 0:
                    ctx.wait(t, "DV")
                if piece >= 2:
                    ctx.wait(t, "DV")
                for dc in range(NDC):
                    t.matmul(ps[0:64, 0:512], S0ft[:, dc, :],
                             wslot[:, dc, coff:coff + 512],
                             start=(dc == 0), stop=False)
                mm = t.matmul(ps[0:64, 0:512], onesb[0:1, 0:64],
                              bhhb[0:1, piece * 512:(piece + 1) * 512],
                              start=False, stop=True)
                ctx.inc(mm, "PE")
            ctx.on("tensor", f_hm)
            pe_h = ctx.c["PE"]

            def f_he(v, piece=piece, ps=ps, pe_h=pe_h):
                ctx.wait(v, "PE", pe_h)
                if piece < 4:
                    ctx.inc(v.tensor_tensor(Cadd[:, piece * 512:(piece + 1) * 512],
                                            Cadd[:, piece * 512:(piece + 1) * 512],
                                            ps[0:64, 0:512], ALU.add), "DV")
                else:
                    off = (piece - 4) * 512
                    ctx.inc(v.tensor_copy(Hn[:, off:off + 512], ps[0:64, 0:512]),
                            "DV")
            ctx.on("vector", f_he)

        if STOP == "pre":
            fscope.close()
            return
        slotin_first = [True]
        fprev = {}
        si_args = [None]

        def emit_si(fb, hb, dv_alpha):
            def f_si(t, fb=fb, hb=hb, dv_alpha=dv_alpha):
                t.wait_ge(ctx.sem["DV"], dv_alpha)
                for tt in range(BT):
                    gtt = fb * BT + tt
                    st = slotin_first[0]
                    for half in range(2):
                        t.matmul(psC[0:64, half * 512:(half + 1) * 512],
                                 alpha[:, gtt, :],
                                 hb[:, tt, half * 512:(half + 1) * 512],
                                 start=st, stop=False, skip_group_check=True)
                    t.matmul(psE[0:64, 0:1], alpha[:, gtt, :], onesb[:, 0:1],
                             start=st, stop=False, skip_group_check=True)
                    slotin_first[0] = False
                ctx.inc(t.drain(), "PE")
            ctx.on("tensor", f_si)
            fprev["si_pe%d" % (fb % 2)] = ctx.c["PE"]

        ctx.barrier()
        for fb in range(NBLK):
            hb = htok if fb % 2 == 0 else htok2
            ln_block(x_in, fb * BLK, htokb=hb, g={
                "xin_dv": fprev.get("norm_dv"),
                "hfeat_pe": fprev.get("gm_pe"),
                "htok_pe": fprev.get("si_pe%d" % (fb % 2)),
                "psT_ac": fprev.get("psT_ac"),
            })
            fprev["norm_dv"] = ctx.norm_dv
            fprev["psT_ac"] = ctx.hfeat_ac

            pingG = Ping(2)
            for op in range(0, 4, 2):
                for oc in (op, op + 1):
                    ps = psA if oc % 2 == 0 else psB

                    def f_gm(t, oc=oc, ps=ps, ha=ctx.hfeat_ac):
                        if oc == 0:
                            t.wait_ge(ctx.sem["AC"], ha)
                        pingG.acquire(lambda s, v: t.wait_ge(ctx.sem[s], v), oc)
                        for dc in range(NDC):
                            mm = t.matmul(ps[:, 0:512],
                                          W[0][:, dc, oc * 128:(oc + 1) * 128],
                                          hfeat[:, dc, :],
                                          start=(dc == 0), stop=(dc == NDC - 1))
                        ctx.inc(mm, "PE")
                    ctx.on("tensor", f_gm)
                pe_g = ctx.c["PE"]

                def f_ge(sc, op=op, pe_g=pe_g):
                    ctx.wait(sc, "PE", pe_g)
                    for oc in (op, op + 1):
                        ps = psA if oc % 2 == 0 else psB
                        ctx.inc(sc.activation(Gf[:, oc, :], ps[:, 0:512], AF.Gelu,
                                              bias=b1p[:, oc:oc + 1]), "AC")
                ctx.on("scalar", f_ge)
                pingG.release(op, "AC", ctx.c["AC"] - 1)
                pingG.release(op + 1, "AC", ctx.c["AC"])
            fprev["gm_pe"] = ctx.c["PE"]

            def f_lm(t):
                ctx.wait(t, "AC")
                for tt in range(BT):
                    for oc in range(4):
                        mm = t.matmul(psD[:, tt * 64:(tt + 1) * 64],
                                      Gf[:, oc, tt * 128:(tt + 1) * 128],
                                      w2sb[:, oc, :],
                                      start=(oc == 0), stop=(oc == 3))
                    ctx.inc(mm, "PE")
            ctx.on("tensor", f_lm)
            pe_l = ctx.c["PE"]
            if si_args[0] is not None:
                emit_si(*si_args[0])

            def f_le(v, pe_l=pe_l):
                ctx.wait(v, "PE", pe_l)
                ctx.inc(v.scalar_tensor_tensor(
                    lgt[:], psD[:, 0:256].rearrange("p (a b) -> p a b", b=64),
                    staub[:, 0:1], b2s[:].unsqueeze(1).broadcast_to([128, 4, 64]),
                    ALU.mult, ALU.add), "DV")
                ctx.wait(v, "DV")
                ctx.inc(v.tensor_copy(LW[:], lgt[:]), "DV")
            ctx.on("vector", f_le)

            def f_topk(v):
                ctx.wait(v, "DV")
                for rnd in range(8):
                    ctx.inc(v.reduce_max(mxs[:, 0:4].unsqueeze(2), LW[:], AX.X),
                            "DV")
                    ctx.wait(v, "DV")
                    if rnd == 0:
                        ctx.inc(v.tensor_scalar(mxs[:, 4:8], mxs[:, 0:4], -1.0,
                                                None, ALU.mult), "DV")
                    if rnd == 7:
                        ctx.inc(v.tensor_copy(mxs[:, 8:12], mxs[:, 0:4]), "DV")
                        ctx.wait(v, "DV")
                        break
                    ctx.inc(v.tensor_tensor(
                        eqb[:], LW[:],
                        mxs[:, 0:4].unsqueeze(2).broadcast_to([128, 4, 64]),
                        ALU.is_equal), "DV")
                    ctx.wait(v, "DV")
                    ctx.inc(v.scalar_tensor_tensor(LW[:], eqb[:], -1e30, LW[:],
                                                   ALU.mult, ALU.add), "DV")
                    ctx.wait(v, "DV")
                ctx.inc(v.tensor_tensor(
                    eqb[:], lgt[:],
                    mxs[:, 8:12].unsqueeze(2).broadcast_to([128, 4, 64]),
                    ALU.is_ge), "DV")
            ctx.on("vector", f_topk)

            def f_aexp(sc):
                ctx.wait(sc, "DV")
                for tt in range(BT):
                    ctx.inc(sc.activation(LW[:, tt, :], lgt[:, tt, :], AF.Exp,
                                          bias=mxs[:, 4 + tt:5 + tt]), "AC")
            ctx.on("scalar", f_aexp)

            def f_anorm(v, fb=fb):
                ctx.wait(v, "AC")
                ctx.inc(v.tensor_tensor(LW[:], LW[:], eqb[:], ALU.mult), "DV")
                ctx.wait(v, "DV")
                ctx.inc(v.reduce_sum(mxs[:, 0:4].unsqueeze(2), LW[:], AX.X), "DV")
                ctx.wait(v, "DV")
                ctx.inc(v.reciprocal(mxs[:, 4:8], mxs[:, 0:4]), "DV")
                ctx.wait(v, "DV")
                ctx.inc(v.tensor_tensor(
                    alpha[:, fb * BT:(fb + 1) * BT, :], LW[:],
                    mxs[:, 4:8].unsqueeze(2).broadcast_to([128, 4, 64]),
                    ALU.mult), "DV")
            ctx.on("vector", f_anorm)
            dv_alpha = ctx.dvsync()

            si_args[0] = (fb, hb, dv_alpha)

        emit_si(*si_args[0])
        if STOP == "p1":
            fscope.close()
            return
        ctx.barrier()
        dump_point("falpha", lambda: alpha[:].rearrange("p a b -> p (a b)"), 2048)

        def f_mid_si(v):
            ctx.inc(v.tensor_scalar(sst[:, 0:1], psE[0:64, 0:1],
                                    cvals[0:64, 3:4], None, ALU.add), "DV")
            ctx.wait(v, "DV")
            ctx.inc(v.reciprocal(sst[:, 1:2], sst[:, 0:1]), "DV")
            ctx.wait(v, "DV")
            ctx.inc(v.tensor_scalar(S0b[:], psC[0:64, 0:1024], sst[:, 1:2], None,
                                    ALU.mult), "DV")
        ctx.on("vector", f_mid_si)
        ctx.dvsync()
        dump_point("fsi", lambda: S0b[:], 1024)
        if STOP == "mid0a":
            fscope.close()
            return

        slot_transpose(S0b[:], ynf)
        if STOP == "mid0b":
            fscope.close()
            return

        if STOP == "mid1":
            fscope.close()
            return
        ctx.barrier()

        def f_wih(gp):
            for sl, c0 in ((W[2], 0), (W[3], 1024)):
                ctx.inc(gp.dma_start(
                    out=sl[:],
                    in_=dram["fWihT"][:, c0:c0 + 1024].rearrange(
                        "(dc p) o -> p dc o", p=128)), "LDW", 16)
        ctx.on("gpsimd", f_wih)

        def f_wih2(gp):
            ctx.wait(gp, "PE")
            ctx.inc(gp.dma_start(
                out=W[2][:],
                in_=dram["fWihT"][:, 2048:3072].rearrange(
                    "(dc p) o -> p dc o", p=128)), "LDW", 16)

        for piece in range(6):
            wslot, coff = [(W[2], 0), (W[2], 512), (W[3], 0), (W[3], 512),
                           (W[2], 0), (W[2], 512)][piece]
            ps = psA if piece % 2 == 0 else psB
            if piece == 4:
                ctx.on("gpsimd", f_wih2)

            def f_gim(t, wslot=wslot, coff=coff, ps=ps, piece=piece):
                if piece in (0, 4):
                    ctx.wait(t, "LDW")
                if piece == 0:
                    ctx.wait(t, "DV")
                if piece >= 2:
                    ctx.wait(t, "DV")
                for dc in range(NDC):
                    mm = t.matmul(ps[0:64, 0:512], ynf[:, dc, :],
                                  wslot[:, dc, coff:coff + 512],
                                  start=(dc == 0), stop=(dc == NDC - 1))
                ctx.inc(mm, "PE")
            ctx.on("tensor", f_gim)
            pe_gi = ctx.c["PE"]

            def f_gie(v, piece=piece, ps=ps, pe_gi=pe_gi):
                ctx.wait(v, "PE", pe_gi)
                if piece < 4:
                    ctx.inc(v.tensor_tensor(rz[:, piece * 512:(piece + 1) * 512],
                                            ps[0:64, 0:512],
                                            Cadd[:, piece * 512:(piece + 1) * 512],
                                            ALU.add), "DV")
                else:
                    off = (piece - 4) * 512
                    ctx.inc(v.tensor_tensor(preN[:, off:off + 512],
                                            ps[0:64, 0:512],
                                            Cadd[:, 2048 + off:2048 + off + 512],
                                            ALU.add), "DV")
            ctx.on("vector", f_gie)

        def f_sig(sc):
            ctx.wait(sc, "DV")
            ctx.inc(sc.activation(rz[:, 0:1024], rz[:, 0:1024], AF.Sigmoid), "AC")
            ctx.inc(sc.activation(rz[:, 1024:2048], rz[:, 1024:2048], AF.Sigmoid),
                    "AC")
        ctx.on("scalar", f_sig)

        def f_gru1(v):
            ctx.wait(v, "AC")
            ctx.inc(v.tensor_tensor(tmp1[:], rz[:, 0:1024], Hn[:], ALU.mult), "DV")
            ctx.wait(v, "DV")
            ctx.inc(v.tensor_tensor(tmp2[:], preN[:], tmp1[:], ALU.add), "DV")
        ctx.on("vector", f_gru1)

        def f_tanh(sc):
            ctx.wait(sc, "DV")
            ctx.inc(sc.activation(tmp1[:], tmp2[:], AF.Tanh), "AC")
        ctx.on("scalar", f_tanh)

        def f_gru2(v):
            ctx.wait(v, "AC")
            ctx.inc(v.tensor_tensor(tmp2[:], S0f[:], tmp1[:], ALU.subtract), "DV")
            ctx.wait(v, "DV")
            ctx.inc(v.tensor_tensor(tmp2[:], rz[:, 1024:2048], tmp2[:], ALU.mult),
                    "DV")
            ctx.wait(v, "DV")
            ctx.inc(v.tensor_tensor(Snew[:], tmp1[:], tmp2[:], ALU.add), "DV")
            ctx.wait(v, "DV")
            ctx.inc(v.tensor_copy(S0b[:], Snew[:]), "DV")
        ctx.on("vector", f_gru2)
        ctx.dvsync()
        dump_point("fSnew", lambda: Snew[:], 1024)

        slot_transpose(S0b[:], Snf)

        if STOP == "gru":
            fscope.close()
            return
        ctx.barrier()

        def f_qkv_ld(gp):
            ctx.inc(gp.dma_start(out=W[2][:, :, 0:256], in_=dram["fwq"].rearrange(
                "(dc p) o -> p dc o", p=128)), "LDW", 16)
            ctx.inc(gp.dma_start(out=W[2][:, :, 256:512], in_=dram["fwk"].rearrange(
                "(dc p) o -> p dc o", p=128)), "LDW", 16)
            ctx.inc(gp.dma_start(out=W[3][:], in_=dram["fwv"].rearrange(
                "(dc p) o -> p dc o", p=128)), "LDW", 16)
        ctx.on("gpsimd", f_qkv_ld)

        pingQ = Ping(2)
        for which, (dst, bt) in enumerate(((qf2, "fbqt"), (kf2, "fbkt"))):
            btab = fbqt if which == 0 else fbkt
            for oc in range(2):
                ps = psA if oc == 0 else psB

                def f_qm(t, which=which, oc=oc, ps=ps):
                    if which == 0 and oc == 0:
                        ctx.wait(t, "LDW")
                        ctx.wait(t, "DV")
                    pingQ.acquire(lambda s, v: t.wait_ge(ctx.sem[s], v), oc)
                    for dc in range(NDC):
                        mm = t.matmul(ps[:, 0:64],
                                      W[2][:, dc, which * 256 + oc * 128:
                                           which * 256 + (oc + 1) * 128],
                                      Snf[:, dc, :],
                                      start=(dc == 0), stop=(dc == NDC - 1))
                    ctx.inc(mm, "PE")
                ctx.on("tensor", f_qm)
                pe_q = ctx.c["PE"]

                def f_qe(v, dst=dst, btab=btab, oc=oc, ps=ps, pe_q=pe_q):
                    ctx.wait(v, "PE", pe_q)
                    ctx.inc(v.tensor_scalar(dst[:, oc, :], ps[:, 0:64],
                                            btab[:, oc:oc + 1], None, ALU.add),
                            "DV")
                ctx.on("vector", f_qe)
                pingQ.release(oc, "DV", ctx.c["DV"])

        dump_point("fq", lambda: qf2[:].rearrange("p a b -> p (a b)"), 128)
        dump_point("fk", lambda: kf2[:].rearrange("p a b -> p (a b)"), 128)

        def f_sc2(t):
            ctx.wait(t, "DV")
            for oc in range(2):
                mm = t.matmul(psD[0:64, 0:64], qf2[:, oc, :], kf2[:, oc, :],
                              start=(oc == 0), stop=(oc == 1))
            ctx.inc(mm, "PE")
        ctx.on("tensor", f_sc2)
        pe_s2 = ctx.c["PE"]

        def f_sm1(v, pe_s2=pe_s2):
            ctx.wait(v, "PE", pe_s2)
            ctx.inc(v.reduce_max(sst[:, 2:3].unsqueeze(2),
                                 psD[0:64, 0:64].unsqueeze(1), AX.X), "DV")
            ctx.wait(v, "DV")
            ctx.inc(v.tensor_scalar(sst[:, 3:4], sst[:, 2:3], -0.0625, None,
                                    ALU.mult), "DV")
        ctx.on("vector", f_sm1)

        def f_sex(sc):
            ctx.wait(sc, "DV")
            ctx.inc(sc.activation(tmp1[:, 0:64], psD[0:64, 0:64], AF.Exp,
                                  bias=sst[:, 3:4], scale=0.0625,
                                  accum_out=sst[:, 4:5]), "AC")
        ctx.on("scalar", f_sex)

        def f_sm2(v):
            ctx.wait(v, "AC")
            ctx.inc(v.reciprocal(sst[:, 5:6], sst[:, 4:5]), "DV")
            ctx.wait(v, "DV")
            ctx.inc(v.tensor_scalar(atb[:], tmp1[0:64, 0:64], sst[:, 5:6], None,
                                    ALU.mult), "DV")
        ctx.on("vector", f_sm2)
        dv_at = ctx.dvsync()

        def f_att(t, dv_at=dv_at):
            t.wait_ge(ctx.sem["DV"], dv_at)
            ctx.inc(t.transpose(psT[0:64, 0:64], atb[:], identb[0:64, 0:64]), "PE")
        ctx.on("tensor", f_att)
        pe_at = ctx.c["PE"]

        def f_atte(v, pe_at=pe_at):
            ctx.wait(v, "PE", pe_at)
            ctx.inc(v.tensor_copy(atT[:], psT[0:64, 0:64]), "DV")
        ctx.on("vector", f_atte)

        dump_point("fat", lambda: atb[:], 64)

        for half in range(2):
            ps = psA if half == 0 else psB

            def f_vm(t, half=half, ps=ps):
                if half == 0:
                    ctx.wait(t, "DV")
                for dc in range(NDC):
                    mm = t.matmul(ps[0:64, 0:512], Snf[:, dc, :],
                                  W[3][:, dc, half * 512:(half + 1) * 512],
                                  start=(dc == 0), stop=(dc == NDC - 1))
                ctx.inc(mm, "PE")
            ctx.on("tensor", f_vm)
            pe_v2 = ctx.c["PE"]

            def f_ve(v, half=half, ps=ps, pe_v2=pe_v2):
                ctx.wait(v, "PE", pe_v2)
                ctx.inc(v.tensor_copy(vb2[:, half * 512:(half + 1) * 512],
                                      ps[0:64, 0:512]), "DV")
            ctx.on("vector", f_ve)
        dv_vb = ctx.dvsync()

        def f_cm(t, dv_vb=dv_vb):
            t.wait_ge(ctx.sem["DV"], dv_vb)
            ctx.inc(t.matmul(psD[0:64, 0:512], atT[:], vb2[:, 0:512],
                             start=True, stop=True), "PE")
            ctx.inc(t.matmul(psE[0:64, 0:512], atT[:], vb2[:, 512:1024],
                             start=True, stop=True), "PE")
        ctx.on("tensor", f_cm)
        pe_cm = ctx.c["PE"]

        def f_bvc_ld(gp):
            ctx.inc(gp.dma_start(out=tmpA[:], in_=dram["fbvc"][:, :]), "LDG", 16)
        ctx.on("gpsimd", f_bvc_ld)

        def f_y(v, pe_cm=pe_cm):
            ctx.wait(v, "PE", pe_cm)
            ctx.wait(v, "LDG")
            ctx.inc(v.tensor_tensor(tmp2[:], Snew[:], tmpA[:], ALU.add), "DV")
            ctx.wait(v, "DV")
            ctx.inc(v.tensor_tensor(tmp1[:, 0:512], psD[0:64, 0:512],
                                    tmp2[:, 0:512], ALU.add), "DV")
            ctx.inc(v.tensor_tensor(tmp1[:, 512:1024], psE[0:64, 0:512],
                                    tmp2[:, 512:1024], ALU.add), "DV")
            ctx.wait(v, "DV")
            ctx.inc(v.reduce_sum(sst[:, 0:1].unsqueeze(2), tmp1[:].unsqueeze(1),
                                 AX.X), "DV")
            ctx.wait(v, "DV")
            ctx.inc(v.tensor_scalar(sst[:, 2:3], sst[:, 0:1], -1.0 / D, None,
                                    ALU.mult), "DV")
        ctx.on("vector", f_y)

        dump_point("fy", lambda: tmp1[:], 1024)

        def f_ysq(v):
            ctx.wait(v, "DV")
            ctx.inc(v.tensor_tensor(tmpB[:], tmp1[:], tmp1[:], ALU.mult), "DV")
            ctx.wait(v, "DV")
            ctx.inc(v.reduce_sum(sst[:, 1:2].unsqueeze(2),
                                 tmpB[:].unsqueeze(1), AX.X), "DV")
        ctx.on("vector", f_ysq)

        def f_yst(v):
            ctx.wait(v, "DV")
            ctx.inc(v.tensor_scalar(sst[:, 1:2], sst[:, 1:2], 1.0 / D, None,
                                    ALU.mult), "DV")
            ctx.inc(v.tensor_tensor(sst[:, 6:7], sst[:, 2:3], sst[:, 2:3],
                                    ALU.mult), "DV")
            ctx.wait(v, "DV")
            ctx.inc(v.tensor_tensor(sst[:, 1:2], sst[:, 1:2], sst[:, 6:7],
                                    ALU.subtract), "DV")
        ctx.on("vector", f_yst)

        def f_ysqrt(sc):
            ctx.wait(sc, "DV")
            ctx.inc(sc.activation(sst[:, 6:7], sst[:, 1:2], AF.Ln,
                                  bias=cvals[0:64, 0:1]), "AC")
            ctx.wait(sc, "AC")
            ctx.inc(sc.activation(sst[:, 7:8], sst[:, 6:7], AF.Exp,
                                  scale=-0.5), "AC")
        ctx.on("scalar", f_ysqrt)

        def f_lnld(gp):
            ctx.wait(gp, "DV")
            ctx.inc(gp.dma_start(out=tmpA[:], in_=dram["flngc"][:, :]), "LDG", 16)
            ctx.inc(gp.dma_start(out=tmpB[:], in_=dram["flnbc"][:, :]), "LDG", 16)
        ctx.on("gpsimd", f_lnld)

        def f_yn(v):
            ctx.wait(v, "AC")
            ctx.wait(v, "LDG")
            ctx.inc(v.tensor_scalar(tmp2[:], tmp1[:], sst[:, 2:3], sst[:, 7:8],
                                    ALU.add, ALU.mult), "DV")
            ctx.wait(v, "DV")
            ctx.inc(v.tensor_tensor(tmp2[:], tmp2[:], tmpA[:], ALU.mult), "DV")
            ctx.wait(v, "DV")
            ctx.inc(v.tensor_tensor(ynb[:], tmp2[:], tmpB[:], ALU.add), "DV")
        ctx.on("vector", f_yn)
        ctx.dvsync()

        dump_point("fyn", lambda: ynb[:], 1024)
        slot_transpose(ynb[:], ynf)

        ctx.barrier()

        def f_wvp_ld(gp):
            ctx.wait(gp, "DV")
            ctx.inc(gp.dma_start(out=W[2][:], in_=dram["fwvp"].rearrange(
                "(dc p) o -> p dc o", p=128)), "LDW", 16)
            ctx.inc(gp.dma_start(out=W[3][:], in_=dram["fwo"].rearrange(
                "(dc p) o -> p dc o", p=128)), "LDW", 16)
            ctx.inc(gp.dma_start(out=tmpA[:], in_=dram["fboc"][:, :]), "LDG", 16)
        ctx.on("gpsimd", f_wvp_ld)

        pingSV = Ping(2)
        for oc in range(NDC):
            ps = psA if oc % 2 == 0 else psB

            def f_svm(t, oc=oc, ps=ps):
                if oc == 0:
                    ctx.wait(t, "LDW")
                    ctx.wait(t, "DV")
                pingSV.acquire(lambda s, v: t.wait_ge(ctx.sem[s], v), oc)
                for dc in range(NDC):
                    mm = t.matmul(ps[:, 0:64], W[2][:, dc, oc * 128:(oc + 1) * 128],
                                  ynf[:, dc, :],
                                  start=(dc == 0), stop=(dc == NDC - 1))
                ctx.inc(mm, "PE")
            ctx.on("tensor", f_svm)
            pe_sv = ctx.c["PE"]

            def f_sve(v, oc=oc, ps=ps, pe_sv=pe_sv):
                ctx.wait(v, "PE", pe_sv)
                ctx.inc(v.tensor_scalar(SVf[:, oc, :], ps[:, 0:64],
                                        fbvpt[:, oc:oc + 1], None, ALU.add),
                        "DV")
            ctx.on("vector", f_sve)
            pingSV.release(oc, "DV", ctx.c["DV"])

        for half in range(2):
            pstgt = psD if half == 0 else psE

            def f_svwm(t, half=half, pstgt=pstgt):
                if half == 0:
                    ctx.wait(t, "DV")
                for dc in range(NDC):
                    mm = t.matmul(pstgt[0:64, 0:512], SVf[:, dc, :],
                                  W[3][:, dc, half * 512:(half + 1) * 512],
                                  start=(dc == 0), stop=(dc == NDC - 1))
                ctx.inc(mm, "PE")
            ctx.on("tensor", f_svwm)
            pe_svw = ctx.c["PE"]

            def f_svwe(v, half=half, pstgt=pstgt, pe_svw=pe_svw):
                ctx.wait(v, "PE", pe_svw)
                ctx.wait(v, "LDG")
                ctx.inc(v.tensor_tensor(svwb[:, half * 512:(half + 1) * 512],
                                        pstgt[0:64, 0:512],
                                        tmpA[:, half * 512:(half + 1) * 512],
                                        ALU.add), "DV")
            ctx.on("vector", f_svwe)
        dump_point("fsvw", lambda: svwb[:], 1024)

        def f_svw2(gp):
            ctx.wait(gp, "DV")
            ctx.inc(gp.dma_start(out=SVW2[0:64, :], in_=svwb[:]), "LDG", 16)
            ctx.inc(gp.dma_start(out=SVW2[64:128, :], in_=svwb[:]), "LDG", 16)
        ctx.on("gpsimd", f_svw2)

        if STOP == "mid":
            fscope.close()
            return
        p2prev = {}
        for pb in range(NBLK):
            def f_x2(sync, pb=pb, xdv=p2prev.get("re2_dv")):
                if xdv is not None:
                    sync.wait_ge(ctx.sem["DV"], xdv)
                else:
                    ctx.wait(sync, "DV")
                ctx.inc(sync.dma_start(
                    out=xin[:, 0:BT, :],
                    in_=x_in[pb * BLK:(pb + 1) * BLK, :].rearrange(
                        "(tt p) d -> p tt d", p=128)), "LDX", 16)
            ctx.on("sync", f_x2)

            pingT2 = Ping(2)
            for tt in range(BT):
                half = tt % 2

                def f_atp(t, pb=pb, tt=tt, half=half):
                    if tt == 0:
                        ctx.wait(t, "DV")
                    pingT2.acquire(lambda s, v: t.wait_ge(ctx.sem[s], v), half)
                    ctx.inc(t.transpose(psT[0:64, half * 1024:half * 1024 + 128],
                                        alpha[:, pb * BT + tt, :],
                                        identb[:]), "PE")
                ctx.on("tensor", f_atp)
                pe_t2 = ctx.c["PE"]

                def f_ate(v, tt=tt, half=half, pe_t2=pe_t2):
                    ctx.wait(v, "PE", pe_t2)
                    ctx.inc(v.tensor_copy(aTb[:, tt * 128:(tt + 1) * 128],
                                          psT[0:64, half * 1024:half * 1024 + 128]),
                            "DV")
                ctx.on("vector", f_ate)
                pingT2.release(half, "DV", ctx.c["DV"])

            pingC2 = Ping(1)
            for tt in range(BT):
                def f_om2(t, tt=tt):
                    if tt == 0:
                        ctx.wait(t, "DV")
                        ctx.wait(t, "LDG")
                    pingC2.acquire(lambda s, v: t.wait_ge(ctx.sem[s], v), 0)
                    for half in range(2):
                        ctx.inc(t.matmul(psC[:, half * 512:(half + 1) * 512],
                                         aTb[0:64, tt * 128:(tt + 1) * 128],
                                         SVW2[0:64, half * 512:(half + 1) * 512],
                                         start=True, stop=True), "PE")
                ctx.on("tensor", f_om2)
                pe_o2 = ctx.c["PE"]

                def f_re2(v, tt=tt, pe_o2=pe_o2):
                    ctx.wait(v, "PE", pe_o2)
                    ctx.wait(v, "LDX")
                    if tt == 0:
                        ctx.wait(v, "STX")
                    ctx.inc(v.tensor_tensor(xout[:, tt, :], psC[:, 0:1024],
                                            xin[:, tt, :], ALU.add), "DV")
                ctx.on("vector", f_re2)
                pingC2.release(0, "DV", ctx.c["DV"])
            p2prev["re2_dv"] = ctx.c["DV"]

            def f_st2(gp, pb=pb):
                ctx.wait(gp, "DV")
                ctx.inc(gp.dma_start(
                    out=x_out[pb * BLK:(pb + 1) * BLK, :].rearrange(
                        "(tt p) d -> p tt d", p=128),
                    in_=xout[:, 0:BT, :]), "STX", 16)
            ctx.on("gpsimd", f_st2)
        fscope.close()

    def emit_attn_b(x_in, x_out):
        bscope = contextlib.ExitStack()
        bsb = lambda n, shp, dt: ctx.sb("b_" + n, shp, dt, stack=bscope)
        mw1sb = bsb("mw1", (128, 8, 288), BF16)
        mw2A = bsb("mw2A", (97, 1024), BF16)
        mw2B = bsb("mw2B", (97, 1024), BF16)
        mw2C = bsb("mw2C", (33, 1024), BF16)
        gw1sb = bsb("gw1", (128, 8, 64), BF16)
        gw2sb = bsb("gw2", (64, 1024), BF16)
        dw1sb = bsb("dw1", (128, 8, 64), BF16)
        dw2sb = bsb("dw2", (64, 1024), BF16)
        u16b = bsb("u16", (128, 8, 16), BF16)
        matab = {}
        for nm in ("maa_x", "maa_w", "maa_k", "maa_v", "maa_r", "maa_g"):
            matab[nm] = bsb(nm, (128, 8), F32)
        tdtab = bsb("tdt", (128, 8), F32)
        lngb = bsb("lngb", (128, 1024), BF16)
        lnbb = bsb("lnbb", (128, 1024), BF16)
        prevc = bsb("prevc", (128, 8), BF16)
        xx = bsb("xx", (128, 8, 512), BF16)
        mixb = bsb("mixb", (128, 8, 512), BF16)
        ubuf = bsb("ubuf", (128, 1024), BF16)
        baseb = ubuf[:, 0:512]
        mscr = ubuf[:, 512:1024]
        lr1 = bsb("lr1", (97, 512), BF16)
        lr2 = bsb("lr2", (97, 512), BF16)
        lrC = bsb("lrC", (33, 512), BF16)
        rf = bsb("rf", (128, 8, 512), BF16)
        kf = bsb("kf", (128, 8, 512), BF16)
        vt = bsb("vt", (128, 4, 1024), BF16)
        gt = bsb("gt", (128, 4, 1024), BF16)
        din = bsb("din", (64, 512), BF16)
        gin = bsb("gin", (64, 512), BF16)
        cib = bsb("cib", (128, 8, 130), F32)

        rbase = xx[:, :, 0:128]
        kbase = xx[:, :, 128:256]
        khf = xx[:, :, 256:384]
        rkb_v = xx[:, :, 384:512]
        kht = bsb("kht", (128, 1024), BF16)
        ebuf = kht.rearrange_view if False else kht
        ebuf_ap = lambda: kht[:].rearrange("p (a b) -> p a b", b=128)
        rkb = rkb_v
        ALb = bsb("ALb", (128, 8, 2), F32)
        dsb = bsb("dsb", (128, 16), F32)
        Pb = bsb("Pb", (128, 16, 128), BF16)
        S_sb = bsb("S_sb", (128, 8, 64), F32)
        S_b = bsb("S_b", (128, 8, 64), BF16)
        yscr = bsb("yscr", (128, 1024), BF16)
        ysc2v = xout[:, 3, :]
        ybuf = xout[:, 2, :]
        gnst = bsb("gnst", (128, 64), F32)
        ygb = ubuf

        ctx.barrier()

        def f_ldb(gp):
            for dst, src in ((W[0], "bWr"), (W[1], "bWk"), (W[2], "bWv"),
                             (W[3], "bWo")):
                ctx.inc(gp.dma_start(
                    out=dst[:],
                    in_=dram[src].rearrange("(dc p) o -> p dc o", p=128)),
                    "LDW", 16)
            ctx.inc(gp.dma_start(out=mw1sb[:], in_=dram["bmw1"].rearrange(
                "(dc p) o -> p dc o", p=128)), "LDW", 16)
            ctx.inc(gp.dma_start(out=mw2A[:], in_=dram["bmw2a"][:, :]), "LDW", 16)
            ctx.inc(gp.dma_start(out=mw2B[:], in_=dram["bmw2b"][:, :]), "LDW", 16)
            ctx.inc(gp.dma_start(out=mw2C[:], in_=dram["bmw2c"][:, :]), "LDW", 16)
            ctx.inc(gp.dma_start(out=gw1sb[:], in_=dram["bgw1"].rearrange(
                "(dc p) o -> p dc o", p=128)), "LDW", 16)
            ctx.inc(gp.dma_start(out=gw2sb[:], in_=dram["bgw2"][:, :]), "LDW", 16)
            ctx.inc(gp.dma_start(out=dw1sb[:], in_=dram["bdw1"].rearrange(
                "(dc p) o -> p dc o", p=128)), "LDW", 16)
            ctx.inc(gp.dma_start(out=dw2sb[:], in_=dram["bdw2"][:, :]), "LDW", 16)
            ctx.inc(gp.dma_start(out=u16b[:], in_=dram["bU16"].rearrange(
                "a p h -> p a h")), "LDW", 16)
            for nm in matab:
                ctx.inc(gp.dma_start(out=matab[nm][:], in_=dram["b" + nm][:, :]),
                        "LDW", 16)
            ctx.inc(gp.dma_start(out=tdtab[:], in_=dram["btd"][:, :]), "LDW", 16)
            ctx.inc(gp.dma_start(out=lngb[:], in_=dram["blngc"][:, :]), "LDW", 16)
            ctx.inc(gp.dma_start(out=lnbb[:], in_=dram["blnbc"][:, :]), "LDW", 16)
        ctx.on("gpsimd", f_ldb)

        def f_init(v):
            v.memset(lr1[32:64, :], 1.0)
            v.memset(lr1[64:97, :], 1.0)
            v.memset(lr2[32:64, :], 1.0)
            v.memset(lr2[64:97, :], 1.0)
            v.memset(lrC[32:33, :], 1.0)
            v.memset(prevc[:], 0.0)
            v.memset(S_sb[:], 0.0)
            v.memset(S_b[:], 0.0)
            v.memset(cib[:, :, 0:1], 0.0)
            ctx.inc(v.engine_nop(), "DV")
        ctx.on("vector", f_init)

        pingY = Ping(1)
        pingD = Ping(1)
        pingW = Ping(1)
        pingE = Ping(1)

        bprev = {}
        for bb in range(NBLK):
            ln_block(x_in, bb * BLK, g={
                "xin_dv": bprev.get("res_dv"),
                "hfeat_dv": bprev.get("hfeat_dv"),
                "hfeat_gp": bprev.get("hfeat_gp"),
                "psT_ac": bprev.get("psT_ac"),
            })
            if STOP == "bln" and bb == 0:
                bscope.close()
                return

            def f_xx(gp, ha=ctx.hfeat_ac, bpe=bprev.get("blk_pe")):
                gp.wait_ge(ctx.sem["AC"], ha)
                if bpe is not None:
                    gp.wait_ge(ctx.sem["PE"], bpe)
                ctx.wait(gp, "DV")
                ctx.inc(gp.tensor_tensor(xx[:, :, 0:1], prevc[:].unsqueeze(2),
                                         hfeat[:, :, 0:1], ALU.subtract), "GP")
                ctx.inc(gp.tensor_tensor(xx[:, :, 1:512], hfeat[:, :, 0:511],
                                         hfeat[:, :, 1:512], ALU.subtract), "GP")
                ctx.wait(gp, "GP")
                ctx.inc(gp.tensor_copy(prevc[:].unsqueeze(2),
                                       hfeat[:, :, 511:512]), "GP")
            ctx.on("gpsimd", f_xx)

            gp_xx = ctx.gpsync()
            bprev["hfeat_gp"] = gp_xx

            def f_xxx(v, gp_xx=gp_xx):
                ctx.wait(v, "LDW")
                v.wait_ge(ctx.sem["GP"], gp_xx)
                for oc in range(NDC):
                    ctx.inc(v.scalar_tensor_tensor(
                        mixb[:, oc, :], xx[:, oc, :],
                        matab["maa_x"][:, oc:oc + 1], hfeat[:, oc, :],
                        ALU.mult, ALU.add), "DV")
            ctx.on("vector", f_xxx)
            gp_xxx = ctx.dvsync()

            if bb == 0:
                dump_point("bxxe", lambda: xx[:].rearrange("p a b -> p (a b)"), 4096)
                dump_point("bxxx", lambda: mixb[:].rearrange("p a b -> p (a b)"), 4096)

            def f_lora(t, gp_xxx=gp_xxx):
                t.wait_ge(ctx.sem["DV"], gp_xxx)
                pingY.acquire(lambda s, v: t.wait_ge(ctx.sem[s], v), 0)
                pingE.acquire(lambda s, v: t.wait_ge(ctx.sem[s], v), 0)
                for half, psx in ((0, psA), (1, psB)):
                    for dc in range(NDC):
                        mm = t.matmul(psx[:, 0:512],
                                      mw1sb[:, dc, half * 128:(half + 1) * 128],
                                      mixb[:, dc, :], start=(dc == 0),
                                      stop=(dc == NDC - 1))
                    ctx.inc(mm, "PE")
                for dc in range(NDC):
                    mm = t.matmul(psE[0:32, 0:512], mw1sb[:, dc, 256:288],
                                  mixb[:, dc, :], start=(dc == 0),
                                  stop=(dc == NDC - 1))
                ctx.inc(mm, "PE")
            ctx.on("tensor", f_lora)
            pe_lora = ctx.c["PE"]

            def f_lorae(sc, pe_lora=pe_lora):
                ctx.wait(sc, "PE", pe_lora)
                ctx.inc(sc.activation(lr1[0:32, :], psA[0:32, 0:512],
                                      AF.Tanh), "AC")
                ctx.inc(sc.activation(lr1[64:96, :], psA[64:96, 0:512],
                                      AF.Tanh), "AC")
                ctx.inc(sc.activation(lr2[0:32, :], psB[0:32, 0:512],
                                      AF.Tanh), "AC")
                ctx.inc(sc.activation(lr2[64:96, :], psB[64:96, 0:512],
                                      AF.Tanh), "AC")
                ctx.inc(sc.activation(lrC[0:32, :], psE[0:32, 0:512],
                                      AF.Tanh), "AC")
            ctx.on("scalar", f_lorae)
            if bb == 0:
                dump_point("blora", lambda: lr1[0:64, :], 512)
            ac_lora = ctx.c["AC"]
            pingE.release(0, "AC", ac_lora)
            pingY.release(0, "AC", ac_lora)

            specs = [("maa_w", 0), ("maa_k", 1), ("maa_v", 2), ("maa_r", 3),
                     ("maa_g", 4)]
            mixsrc = [(mw2A, lr1, 0), (mw2A, lr1, 64), (mw2B, lr2, 0),
                      (mw2B, lr2, 64), (mw2C, lrC, 0)]
            for nm, fi in specs:
                pingM = Ping(2)
                prev_mix2 = [None]
                W2x, lrx, pb = mixsrc[fi]
                for oc in range(NDC):
                    ps = psA if oc % 2 == 0 else psB

                    def f_mm(t, fi=fi, oc=oc, ps=ps, W2x=W2x, lrx=lrx, pb=pb):
                        if fi == 0 and oc == 0:
                            ctx.wait(t, "AC", ac_lora)
                        pingM.acquire(lambda s, v: t.wait_ge(ctx.sem[s], v), oc)
                        ctx.inc(t.matmul(
                            ps[:, 0:512],
                            W2x[pb:pb + 33, oc * 128:(oc + 1) * 128],
                            lrx[pb:pb + 33, 0:512],
                            start=True, stop=True), "PE")
                    ctx.on("tensor", f_mm)
                    pe_m = ctx.c["PE"]

                    def f_mix1(v, oc=oc, ps=ps, pe_m=pe_m):
                        ctx.wait(v, "PE", pe_m)
                        if prev_mix2[0] is not None:
                            ctx.wait(v, "DV", prev_mix2[0])
                        ctx.inc(v.tensor_tensor(mscr, ps[:, 0:512],
                                                xx[:, oc, :], ALU.mult), "DV")
                    ctx.on("vector", f_mix1)
                    dv_m = ctx.c["DV"]
                    pingM.release(oc, "DV", dv_m)

                    def f_mix2(v, oc=oc, dv_m=dv_m):
                        ctx.wait(v, "DV", dv_m)
                        ctx.inc(v.tensor_tensor(mixb[:, oc, :], mscr,
                                                hfeat[:, oc, :], ALU.add), "DV")
                    ctx.on("vector", f_mix2)
                    prev_mix2[0] = ctx.c["DV"]
                dv_mix = ctx.dvsync()

                if bb == 0:
                    dump_point("bmix_" + nm,
                               lambda: mixb[:].rearrange("p a b -> p (a b)"), 4096)
                if nm == "maa_w":
                    def f_dm(t, dvr=dv_mix):
                        t.wait_ge(ctx.sem["DV"], dvr)
                        pingD.acquire(lambda s, v: t.wait_ge(ctx.sem[s], v), 0)
                        for dc in range(NDC):
                            mm = t.matmul(psD[0:64, 0:512], dw1sb[:, dc, :],
                                          mixb[:, dc, :], start=(dc == 0),
                                          stop=(dc == NDC - 1))
                        ctx.inc(mm, "PE")
                    ctx.on("tensor", f_dm)
                    pe_d = ctx.c["PE"]

                    def f_de(sc, pe_d=pe_d):
                        ctx.wait(sc, "PE", pe_d)
                        ctx.inc(sc.activation(din[:], psD[0:64, 0:512], AF.Tanh),
                                "AC")
                    ctx.on("scalar", f_de)
                    pingD.release(0, "AC", ctx.c["AC"])
                elif nm in ("maa_k", "maa_r"):
                    wsl = W[1] if nm == "maa_k" else W[0]
                    dst = kf if nm == "maa_k" else rf
                    pingP = Ping(2)
                    for oc in range(NDC):
                        off = (oc % 2) * 512

                        def f_km(t, oc=oc, off=off, wsl=wsl, dvr=dv_mix):
                            if oc == 0:
                                t.wait_ge(ctx.sem["DV"],
                                          dvr + int(os.environ.get("MK_DVPAD", "0")))
                                pingW.acquire(
                                    lambda s, v: t.wait_ge(ctx.sem[s], v), 0)
                            pingP.acquire(lambda s, v: t.wait_ge(ctx.sem[s], v), oc)
                            for dc in range(NDC):
                                mm = t.matmul(psC[:, off:off + 512],
                                              wsl[:, dc, oc * 128:(oc + 1) * 128],
                                              mixb[:, dc, :], start=(dc == 0),
                                              stop=(dc == NDC - 1))
                            ctx.inc(mm, "PE")
                        ctx.on("tensor", f_km)
                        pe_k = ctx.c["PE"]

                        def f_ke(v, oc=oc, off=off, dst=dst, pe_k=pe_k):
                            ctx.wait(v, "PE", pe_k)
                            ctx.inc(v.tensor_copy(dst[:, oc, :],
                                                  psC[:, off:off + 512]), "DV")
                        ctx.on("vector", f_ke)
                        pingP.release(oc, "DV", ctx.c["DV"])
                    pingW.release(0, "DV", ctx.c["DV"])
                elif nm == "maa_v":
                    pingP = Ping(1)
                    for tt in range(BT):
                        def f_vm(t, tt=tt, dvr=dv_mix):
                            if tt == 0:
                                t.wait_ge(ctx.sem["DV"], dvr)
                                pingW.acquire(
                                    lambda s, v: t.wait_ge(ctx.sem[s], v), 0)
                            pingP.acquire(lambda s, v: t.wait_ge(ctx.sem[s], v), 0)
                            for half in range(2):
                                for dc in range(NDC):
                                    mm = t.matmul(
                                        psC[:, half * 512:(half + 1) * 512],
                                        mixb[:, dc, tt * 128:(tt + 1) * 128],
                                        W[2][:, dc, half * 512:(half + 1) * 512],
                                        start=(dc == 0), stop=(dc == NDC - 1))
                            ctx.inc(mm, "PE")
                        ctx.on("tensor", f_vm)
                        pe_v = ctx.c["PE"]

                        def f_ve(v, tt=tt, pe_v=pe_v):
                            ctx.wait(v, "PE", pe_v)
                            ctx.inc(v.tensor_copy(vt[:, tt, :], psC[:, 0:1024]),
                                    "DV")
                        ctx.on("vector", f_ve)
                        pingP.release(0, "DV", ctx.c["DV"])
                    pingW.release(0, "DV", ctx.c["DV"])
                elif nm == "maa_g":
                    def f_gm(t, dvr=dv_mix):
                        t.wait_ge(ctx.sem["DV"], dvr)
                        pingD.acquire(lambda s, v: t.wait_ge(ctx.sem[s], v), 0)
                        for dc in range(NDC):
                            mm = t.matmul(psD[0:64, 0:512], gw1sb[:, dc, :],
                                          mixb[:, dc, :], start=(dc == 0),
                                          stop=(dc == NDC - 1))
                        ctx.inc(mm, "PE")
                    ctx.on("tensor", f_gm)
                    pe_g = ctx.c["PE"]

                    def f_gne(sc, pe_g=pe_g):
                        ctx.wait(sc, "PE", pe_g)
                        ctx.inc(sc.activation(gin[:], psD[0:64, 0:512], AF.Tanh),
                                "AC")
                    ctx.on("scalar", f_gne)
                    pingD.release(0, "AC", ctx.c["AC"])

                    ac_gin = ctx.c["AC"]
                    pingP = Ping(1)
                    for tt in range(BT):
                        def f_g2(t, tt=tt, ac_gin=ac_gin):
                            if tt == 0:
                                ctx.wait(t, "AC", ac_gin)
                                pingW.acquire(
                                    lambda s, v: t.wait_ge(ctx.sem[s], v), 0)
                            pingP.acquire(lambda s, v: t.wait_ge(ctx.sem[s], v), 0)
                            for half in range(2):
                                ctx.inc(t.matmul(
                                    psC[:, half * 512:(half + 1) * 512],
                                    gin[0:64, tt * 128:(tt + 1) * 128],
                                    gw2sb[0:64, half * 512:(half + 1) * 512],
                                    start=True, stop=True), "PE")
                        ctx.on("tensor", f_g2)
                        pe_g2 = ctx.c["PE"]

                        def f_g2e(v, tt=tt, pe_g2=pe_g2):
                            ctx.wait(v, "PE", pe_g2)
                            ctx.inc(v.tensor_copy(gt[:, tt, :], psC[:, 0:1024]),
                                    "DV")
                        ctx.on("vector", f_g2e)
                        pingP.release(0, "DV", ctx.c["DV"])
                    pingW.release(0, "DV", ctx.c["DV"])

            if bb == 0:
                dump_point("brfE", lambda: rf[:].rearrange("p a b -> p (a b)"), 4096)
                dump_point("bvtE", lambda: vt[:].rearrange("p a b -> p (a b)"), 4096)
            if STOP == "bmix" and bb == 0:
                bscope.close()
                return
            M = {"kht_ac": ctx.hfeat_ac, "yg_ac": ctx.hfeat_ac}

            def emit_posw(i):
                def f_pw(t, i=i):
                    if i == 0:
                        ctx.wait(t, "AC")
                    pingW.acquire(lambda s, v: t.wait_ge(ctx.sem[s], v), 0)
                    for oc in range(NDC):
                        ctx.inc(t.matmul(
                            psC[:, oc * 128:(oc + 1) * 128],
                            dw2sb[0:64, oc * 128:(oc + 1) * 128],
                            din[0:64, i * 128:(i + 1) * 128],
                            start=True, stop=True), "PE")
                ctx.on("tensor", f_pw)
                pe_pw = ctx.c["PE"]

                def f_pwe(sc, i=i, pe_pw=pe_pw):
                    ctx.wait(sc, "PE", pe_pw)
                    if M.get((i - 1, "devac")) is not None:
                        sc.wait_ge(ctx.sem["DV"], M[(i - 1, "devac")])
                    for oc in range(NDC):
                        dst = (psD[:, oc * 128:(oc + 1) * 128] if oc < 4
                               else psC[:, oc * 128:(oc + 1) * 128])
                        ctx.inc(sc.activation(dst,
                                              psC[:, oc * 128:(oc + 1) * 128],
                                              AF.Exp,
                                              bias=tdtab[:, oc:oc + 1]), "AC")
                ctx.on("scalar", f_pwe)
                M[(i, "poswac")] = ctx.c["AC"]

            def emit_scans(i):
                def f_scan(v, i=i):
                    ctx.wait(v, "AC")
                    for oc in range(NDC):
                        src = (psD[:, oc * 128:(oc + 1) * 128] if oc < 4
                               else psC[:, oc * 128:(oc + 1) * 128])
                        ctx.inc(v.tensor_tensor_scan(
                            cib[:, oc, 1:129], onesf[:, 0:128], src,
                            0.0, ALU.mult, ALU.add), "DV")
                    ctx.wait(v, "DV")
                    ctx.inc(v.tensor_scalar(
                        ALb[:, :, i % 2], cib[:, :, 128], -1.0,
                        None, ALU.mult), "DV")
                ctx.on("vector", f_scan)
                M[(i, "scandv")] = ctx.dvsync()

            def emit_mexp(i):
                resR = Pb[:, 0:8, :]
                resK = Pb[:, 8:16, :]

                def f_mexp(sc2, i=i):
                    ctx.wait(sc2, "DV", M[(i, "scandv")])
                    pe_prev = M.get((i - 1, "pestate"))
                    if pe_prev is not None:
                        sc2.wait_ge(ctx.sem["PE"], pe_prev)
                    ctx.inc(sc2.activation(resR, cib[:, :, 0:128],
                                           AF.Exp, scale=-1.0), "AC")
                    ctx.inc(sc2.activation(resK, cib[:, :, 1:129],
                                           AF.Exp), "AC")
                    for oc in range(NDC):
                        ctx.inc(sc2.activation(
                            kht[:, oc * 128:(oc + 1) * 128],
                            cib[:, oc, 1:129], AF.Exp,
                            bias=ALb[:, oc, i % 2:i % 2 + 1]), "AC")
                    ctx.wait(sc2, "AC")
                    ctx.inc(sc2.activation(ALb[:, :, i % 2], ALb[:, :, i % 2],
                                           AF.Exp), "AC")
                ctx.on("scalar", f_mexp)
                M[(i, "mexpac")] = ctx.c["AC"]

            def emit_mmul(i):
                rsl = rf[:, :, i * 128:(i + 1) * 128]
                ksl = kf[:, :, i * 128:(i + 1) * 128]

                def f_mmul(v, i=i, rsl=rsl, ksl=ksl):
                    ctx.wait(v, "AC", M[(i, "mexpac")])
                    ctx.inc(v.tensor_tensor(rbase, rsl, Pb[:, 0:8, :],
                                            ALU.mult), "DV")
                    ctx.inc(v.scalar_tensor_tensor(
                        kbase, Pb[:, 8:16, :], 3.73324200e32, ksl,
                        ALU.min, ALU.mult), "DV")
                    ctx.inc(v.tensor_tensor(
                        khf, ksl,
                        kht[:].rearrange("p (a b) -> p a b", b=128),
                        ALU.mult), "DV")
                    ctx.inc(v.tensor_tensor(rkb, rsl, ksl, ALU.mult), "DV")
                ctx.on("vector", f_mmul)
                M[(i, "modsdv")] = ctx.dvsync()

            def emit_kht(i):
                prev_ac = M.get("kht_ac")

                def f_kt(t, i=i, prev_ac=prev_ac):
                    t.wait_ge(ctx.sem["DV"], M[(i, "modsdv")])
                    if prev_ac is not None:
                        t.wait_ge(ctx.sem["AC"], prev_ac)
                    for oc in range(NDC):
                        ctx.inc(t.transpose(psT[:, oc * 128:(oc + 1) * 128],
                                            khf[:, oc, :], identb[:]), "PE")
                ctx.on("tensor", f_kt)
                pe_kt = ctx.c["PE"]

                def f_kte(sc, pe_kt=pe_kt):
                    ctx.wait(sc, "PE", pe_kt)
                    ctx.inc(sc.activation(kht[:, 0:1024], psT[:, 0:1024],
                                          AF.Identity), "AC")
                ctx.on("scalar", f_kte)
                M["kht_ac"] = M[(i, "khtac")] = ctx.c["AC"]

            def emit_d(i):
                def f_dmm(t, i=i):
                    t.wait_ge(ctx.sem["DV"], M[(i, "modsdv")])
                    for oc in range(NDC):
                        mm = t.matmul(psD[:, 0:16], rkb[:, oc, :],
                                      u16b[:, oc, :],
                                      start=(oc == 0), stop=(oc == NDC - 1),
                                      skip_group_check=True)
                    ctx.inc(mm, "PE")
                ctx.on("tensor", f_dmm)
                pe_d = ctx.c["PE"]

                def f_dev(v, pe_d=pe_d):
                    ctx.wait(v, "PE", pe_d)
                    ctx.inc(v.tensor_copy(dsb[:], psD[:, 0:16]), "DV")
                ctx.on("vector", f_dev)
                M[(i, "devac")] = ctx.c["DV"]
                pingD.release(0, "DV", ctx.c["DV"])

            def emit_w(i):
                pingWp = Ping(2)
                for j in range(8):
                    if j % 2 == 0:
                        pse, pso = psC[:, 0:128], psC[:, 512:640]
                    else:
                        pse, pso = psA[:, 0:128], psB[:, 0:128]

                    def f_wm(t, i=i, j=j, pse=pse, pso=pso):
                        if j == 0:
                            t.wait_ge(ctx.sem["DV"], M[(i, "modsdv")])
                        if j == 1:
                            pingY.acquire(
                                lambda s, v: t.wait_ge(ctx.sem[s], v), 0)
                        pingWp.acquire(lambda s, v: t.wait_ge(ctx.sem[s], v), j)
                        for par, ps in ((0, pse), (1, pso)):
                            hs = 64 * par
                            ctx.inc(t.matmul(
                                ps, kbase[hs:hs + 64, j, :],
                                rbase[hs:hs + 64, j, :],
                                start=True, stop=True, skip_group_check=True),
                                "PE")
                    ctx.on("tensor", f_wm)
                    pe_wm = ctx.c["PE"]

                    def f_pev(v, j=j, pse=pse, pso=pso, pe_wm=pe_wm):
                        ctx.wait(v, "PE", pe_wm)
                        ctx.inc(v.tensor_tensor(
                            Pb[:, 2 * j, :], pse,
                            maskB[:, :], ALU.mult), "DV")
                        ctx.inc(v.tensor_tensor(
                            Pb[:, 2 * j + 1, :], pso,
                            maskB[:, :], ALU.mult), "DV")
                    ctx.on("vector", f_pev)
                    pingWp.release(j, "DV", ctx.c["DV"])
                pingW.release(0, "DV", ctx.c["DV"])
                M[(i, "pevdv")] = ctx.dvsync()

            def emit_state_mm(i):
                def f_ym(t, i=i):
                    t.wait_ge(ctx.sem["DV"], M[(i, "pevdv")])
                    for h in range(NH):
                        hs = 64 * (h % 2)
                        psY = psA if h % 2 == 0 else psB
                        col = (h // 2) * 64
                        t.matmul(psY[:, col:col + 64],
                                 rbase[hs:hs + 64, h // 2, :],
                                 S_b[hs:hs + 64, h // 2, :],
                                 start=True, stop=False, skip_group_check=True)
                        mm = t.matmul(
                            psY[:, col:col + 64],
                            Pb[:, h, :],
                            vt[:, i, h * 64:(h + 1) * 64],
                            start=False, stop=True, skip_group_check=True)
                    ctx.inc(mm, "PE")

                    ctx.wait(t, "AC", M[(i, "khtac")])
                    pingE.acquire(lambda s, v: t.wait_ge(ctx.sem[s], v), 0)
                    for h in range(NH):
                        hs = 64 * (h % 2)
                        ctx.inc(t.matmul(
                            psE[hs:hs + 64, (h // 2) * 64:(h // 2) * 64 + 64],
                            kht[:, h * 64:(h + 1) * 64],
                            vt[:, i, h * 64:(h + 1) * 64],
                            start=True, stop=True, skip_group_check=True), "PE")
                ctx.on("tensor", f_ym)
                M[(i, "pestate")] = ctx.c["PE"]

            def emit_su_ye(i):
                pe_st = M[(i, "pestate")]

                def f_su(v, i=i, pe_st=pe_st):
                    ctx.wait(v, "PE", pe_st)
                    ctx.wait(v, "AC", M[(i, "mexpac")])
                    ctx.inc(v.tensor_tensor(
                        S_sb[:], S_sb[:],
                        ALb[:, :, i % 2:i % 2 + 1].broadcast_to([128, 8, 64]),
                        ALU.mult), "DV")
                    ctx.wait(v, "DV")
                    ctx.inc(v.tensor_tensor(
                        S_sb[:], S_sb[:],
                        psE[:, 0:512].rearrange("p (a z) -> p a z", z=64),
                        ALU.add), "DV")
                    ctx.wait(v, "DV")
                    ctx.inc(v.tensor_copy(S_b[:], S_sb[:]), "DV")
                ctx.on("vector", f_su)
                ctx.dvsync()
                pingE.release(0, "DV", ctx.c["DV"])

                def f_ye(v, i=i, pe_st=pe_st):
                    ctx.wait(v, "STX")
                    ctx.inc(v.tensor_tensor(
                        yscr[:].rearrange("p (h z) -> p h z", h=NH),
                        dsb[:].unsqueeze(2).broadcast_to([128, NH, 64]),
                        vt[:, i, :].rearrange("p (h z) -> p h z", h=NH),
                        ALU.mult), "DV")
                    ctx.wait(v, "DV")
                    yeven = ybuf.rearrange("p (a z) -> p a z", z=64)[:, 0:16:2, :]
                    yodd = ybuf.rearrange("p (a z) -> p a z", z=64)[:, 1:16:2, :]
                    seven = yscr[:].rearrange("p (a z) -> p a z", z=64)[:, 0:16:2, :]
                    sodd = yscr[:].rearrange("p (a z) -> p a z", z=64)[:, 1:16:2, :]
                    ctx.inc(v.tensor_tensor(
                        yeven, psA[:, 0:512].rearrange("p (a z) -> p a z", z=64),
                        seven, ALU.add), "DV")
                    ctx.inc(v.tensor_tensor(
                        yodd, psB[:, 0:512].rearrange("p (a z) -> p a z", z=64),
                        sodd, ALU.add), "DV")
                    ctx.wait(v, "DV")
                ctx.on("vector", f_ye)
                pingY.release(0, "DV", ctx.c["DV"])

            def emit_gn(i):
                def f_gn1(v):
                    ctx.wait(v, "STX")
                    yv = ybuf.rearrange("p (h z) -> p h z", h=NH)
                    ctx.inc(v.reduce_sum(gnst[:, 0:16].unsqueeze(2), yv, AX.X),
                            "DV")
                    ctx.inc(v.tensor_tensor(ysc2v, ybuf, ybuf, ALU.mult),
                            "DV")
                    ctx.wait(v, "DV")
                    ctx.inc(v.reduce_sum(
                        gnst[:, 16:32].unsqueeze(2),
                        ysc2v.rearrange("p (h z) -> p h z", h=NH), AX.X), "DV")
                    ctx.wait(v, "DV")
                    ctx.inc(v.tensor_scalar(gnst[:, 32:48], gnst[:, 0:16],
                                            1.0 / 64, None, ALU.mult), "DV")
                    ctx.inc(v.tensor_scalar(gnst[:, 16:32], gnst[:, 16:32],
                                            1.0 / 64, None, ALU.mult), "DV")
                    ctx.wait(v, "DV")
                    ctx.inc(v.tensor_tensor(gnst[:, 48:64], gnst[:, 32:48],
                                            gnst[:, 32:48], ALU.mult), "DV")
                    ctx.wait(v, "DV")
                    ctx.inc(v.tensor_tensor(gnst[:, 16:32], gnst[:, 16:32],
                                            gnst[:, 48:64], ALU.subtract), "DV")
                ctx.on("vector", f_gn1)
                dv_gn1 = ctx.c["DV"]

                def f_gn2(sc2, dv_gn1=dv_gn1):
                    ctx.wait(sc2, "DV", dv_gn1)
                    ctx.inc(sc2.activation(gnst[:, 48:64], gnst[:, 16:32],
                                           AF.Ln, bias=cvals[:, 2:3]), "AC")
                    ctx.wait(sc2, "AC")
                    ctx.inc(sc2.activation(gnst[:, 0:16], gnst[:, 48:64],
                                           AF.Exp, scale=-0.5), "AC")
                ctx.on("scalar", f_gn2)
                ac_gn2 = ctx.c["AC"]

                def f_gg1(v, i=i):
                    ctx.inc(v.tensor_tensor(yscr[:], lngb[:], gt[:, i, :],
                                            ALU.mult), "DV")
                ctx.on("vector", f_gg1)

                def f_gn3(v, ac_gn2=ac_gn2):
                    ctx.wait(v, "AC", ac_gn2)
                    muv = gnst[:, 32:48].unsqueeze(2).broadcast_to([128, NH, 64])
                    rsv = gnst[:, 0:16].unsqueeze(2).broadcast_to([128, NH, 64])
                    y3 = ysc2v.rearrange("p (h z) -> p h z", h=NH)
                    ctx.inc(v.tensor_tensor(
                        y3, ybuf.rearrange("p (h z) -> p h z", h=NH), muv,
                        ALU.subtract), "DV")
                    ctx.wait(v, "DV")
                    ctx.inc(v.tensor_tensor(y3, y3, rsv, ALU.mult), "DV")
                    ctx.wait(v, "DV")
                    ctx.inc(v.tensor_tensor(ysc2v, ysc2v, yscr[:], ALU.mult),
                            "DV")
                ctx.on("vector", f_gn3)
                dv_g3 = ctx.c["DV"]

                def f_gg2(v, i=i, dv_g3=dv_g3):
                    ctx.wait(v, "DV", dv_g3)
                    ctx.inc(v.tensor_tensor(yscr[:], lnbb[:], gt[:, i, :],
                                            ALU.mult), "DV")
                ctx.on("vector", f_gg2)

                def f_gn4(v):
                    ctx.wait(v, "DV")
                    ctx.inc(v.tensor_tensor(ygb[:], ysc2v, yscr[:], ALU.add),
                            "DV")
                ctx.on("vector", f_gn4)
                M[(i, "ygbdv")] = ctx.dvsync()

            def emit_yg(i):
                prev_yg_ac = M.get("yg_ac")

                def f_yt(t, i=i, prev_yg_ac=prev_yg_ac):
                    t.wait_ge(ctx.sem["DV"], M[(i, "ygbdv")])
                    if prev_yg_ac is not None:
                        t.wait_ge(ctx.sem["AC"], prev_yg_ac)
                    for oc in range(NDC):
                        ctx.inc(t.transpose(
                            psT[:, 1024 + oc * 128:1024 + (oc + 1) * 128],
                            ygb[:][:, oc * 128:(oc + 1) * 128],
                            identb[:]), "PE")
                ctx.on("tensor", f_yt)
                pe_yt = ctx.c["PE"]

                def f_yte(sc, i=i, pe_yt=pe_yt):
                    ctx.wait(sc, "PE", pe_yt)
                    ctx.inc(sc.activation(
                        mixb[:, :, i * 128:(i + 1) * 128],
                        psT[:, 1024:2048].rearrange("p (a b) -> p a b", b=128),
                        AF.Identity), "AC")
                ctx.on("scalar", f_yte)
                M["yg_ac"] = ctx.c["AC"]

            prev = [None]
            for sc_i in range(BT):
                emit_posw(sc_i)
                if prev[0] is not None:
                    emit_state_mm(prev[0])
                    emit_su_ye(prev[0])
                emit_scans(sc_i)
                emit_mexp(sc_i)
                if prev[0] is not None:
                    emit_gn(prev[0])
                emit_mmul(sc_i)
                emit_kht(sc_i)
                emit_d(sc_i)
                emit_w(sc_i)
                if prev[0] is not None:
                    emit_yg(prev[0])
                prev[0] = sc_i
            emit_state_mm(prev[0])
            emit_su_ye(prev[0])
            emit_gn(prev[0])
            emit_yg(prev[0])
            bprev["hfeat_dv"] = M[(prev[0], "ygbdv")]
            if STOP == "bwkv" and bb == 0:
                bscope.close()
                return

            dv_yg = ctx.c["DV"]
            ac_yg = ctx.c["AC"]
            pingO = Ping(1)
            for tt in range(BT):
                def f_om(t, tt=tt, dv_yg=dv_yg, ac_yg=ac_yg):
                    if tt == 0:
                        ctx.wait(t, "DV", dv_yg)
                        ctx.wait(t, "AC", ac_yg)
                        pingW.acquire(lambda s, v: t.wait_ge(ctx.sem[s], v), 0)
                    pingO.acquire(lambda s, v: t.wait_ge(ctx.sem[s], v), 0)
                    for half in range(2):
                        for dc in range(NDC):
                            mm = t.matmul(psC[:, half * 512:(half + 1) * 512],
                                          mixb[:, dc, tt * 128:(tt + 1) * 128],
                                          W[3][:, dc, half * 512:(half + 1) * 512],
                                          start=(dc == 0), stop=(dc == NDC - 1))
                        ctx.inc(mm, "PE")
                ctx.on("tensor", f_om)
                pe_o = ctx.c["PE"]

                def f_res(v, tt=tt, pe_o=pe_o):
                    ctx.wait(v, "PE", pe_o)
                    if tt == 0:
                        ctx.wait(v, "STX")
                    ctx.inc(v.tensor_tensor(xout[:, tt, :], psC[:, 0:1024],
                                            xin[:, tt, :], ALU.add), "DV")
                ctx.on("vector", f_res)
                pingO.release(0, "DV", ctx.c["DV"])
            pingW.release(0, "DV", ctx.c["DV"])

            bprev["res_dv"] = ctx.c["DV"]
            bprev["psT_ac"] = M["yg_ac"]
            bprev["blk_pe"] = ctx.c["PE"]

            def f_st(gp, bb=bb):
                ctx.wait(gp, "DV")
                ctx.inc(gp.dma_start(
                    out=x_out[bb * BLK:(bb + 1) * BLK, :].rearrange(
                        "(tt p) d -> p tt d", p=128),
                    in_=xout[:, 0:BT, :]), "STX", 16)
            ctx.on("gpsimd", f_st)
            if STOP == "bblk0" and bb == 0:
                bscope.close()
                return

            dump_point("brf", lambda: rf[:].rearrange("p a b -> p (a b)"), 4096)
            dump_point("bkf", lambda: kf[:].rearrange("p a b -> p (a b)"), 4096)
            dump_point("bvt", lambda: vt[:].rearrange("p a b -> p (a b)"), 4096)
            dump_point("bgt", lambda: gt[:].rearrange("p a b -> p (a b)"), 4096)
            dump_point("bposw", lambda: posw[:].rearrange("p a b -> p (a b)"), 1024)
            dump_point("bcib", lambda: cib[:].rearrange("p a b -> p (a b)"), 1056)
            dump_point("bS", lambda: S_sb[:].rearrange("p a b -> p (a b)"), 512)
            dump_point("byg", lambda: mixb[:].rearrange("p a b -> p (a b)"), 4096)
            dump_point("bxx", lambda: xx[:].rearrange("p a b -> p (a b)"), 4096)
            dump_point("bmw", lambda: ygb[:], 1024)
        bscope.close()

    targets = [x1d, x2d, x1d, out]
    targets[N_SUB - 1] = out
    if DUMP:
        targets = [x1d, x2d, x1d, x2d]
    emit_attn_a(targets[0])
    if N_SUB >= 2:
        emit_ffn(x1d, targets[1], "f1_")
    if N_SUB >= 3:
        emit_attn_b(x2d, targets[2])
    if N_SUB >= 4:
        emit_ffn(x1d, targets[3], "f2_")

    ctx.barrier()

    def f_fin(gp):
        ctx.wait(gp, "STX")
    ctx.on("gpsimd", f_fin)

    es.close()
    return nc, ctx


def kernel(**inputs):
    sh, xs = prep_host(inputs)
    nc = bass.Bass("TRN2", target_bir_lowering=False)
    nc, ctx = build(nc, sh)
    in_maps = [dict(sh, x0=xs[i]) for i in range(8)]
    res = run_bass_kernel_spmd(nc, in_maps, core_ids=list(range(8)))
    o = np.stack([np.asarray(res.results[i]["out"], np.float32) for i in range(8)], 0)
    return o
